# version 03
# baseline (speedup 1.0000x reference)
import contextlib
import numpy as np
import ml_dtypes
import concourse.bass as bass
import concourse.mybir as mybir
from concourse.bass_utils import run_bass_kernel_spmd

F32 = mybir.dt.float32
BF16 = mybir.dt.bfloat16
I8 = mybir.dt.int8
AF = mybir.ActivationFunctionType
ALU = mybir.AluOpType
AX = mybir.AxisListType
BF = ml_dtypes.bfloat16

NCORES = 8
NBLK = 16
TB = 512
L = 8192
DM = 2048
DI = 4096
DS = 16
DTR = 128

ENGINES = ("sync", "scalar", "vector", "tensor", "gpsimd")


def blocks_of(core):
    return (core, NBLK - 1 - core)


class SemCtr:
    def __init__(self, h):
        self.h = h
        self.v = 0

    def inc(self, n=1):
        self.v += n
        return self.v


class Prog:
    def __init__(self, nc, stack):
        self.nc = nc
        self.stack = stack
        self.ops = {e: [] for e in ENGINES}

    def sem(self, name=None):
        return SemCtr(self.stack.enter_context(self.nc.semaphore(name)))

    def sbuf(self, name, shape, dtype):
        return self.stack.enter_context(self.nc.sbuf_tensor(name, shape, dtype))

    def psum(self, name, shape, dtype=F32):
        return self.stack.enter_context(self.nc.psum_tensor(name, shape, dtype))

    def dma(self, eng, out, in_, sem):
        tgt = sem.inc(16)
        def f(e, out=out, in_=in_, h=sem.h):
            e.dma_start(out=out, in_=in_).then_inc(h, 16)
        self.ops[eng].append(f)
        return tgt

    def wait(self, eng, sem, v):
        if v is None or v <= 0:
            return
        self.ops[eng].append(lambda e, h=sem.h, v=v: e.wait_ge(h, v))

    def op(self, eng, fn, sem=None, n=1):
        if sem is None:
            self.ops[eng].append(lambda e, fn=fn: fn(e))
            return None
        tgt = sem.inc(n)
        def f(e, fn=fn, h=sem.h, n=n):
            fn(e).then_inc(h, n)
        self.ops[eng].append(f)
        return tgt

    def finalize(self):
        def runner(ops):
            def go(eng):
                for f in ops:
                    f(eng)
            return go
        with self.nc.Block() as block:
            for e in ENGINES:
                getattr(block, e)(runner(self.ops[e]))


def planar(sb, j, T):
    return sb[:, j * T:(j + 1) * T]


def sb3(sb, T):
    return sb[:].rearrange("p (j t) -> p j t", t=T)


def host_onehot_bc():
    oh = np.zeros((32, 32 * 128), dtype=BF)
    for n in range(32):
        oh[n, n * 128:(n + 1) * 128] = 1
    return oh


def build_l1_mamba_main(stop_after=None):
    nc = bass.Bass("TRN2", target_bir_lowering=False, debug=False)
    stack = contextlib.ExitStack()

    TX = 515
    NT = 2 * TX
    TH = 256

    x_in = nc.dram_tensor("x", [DM, NT], F32, kind="ExternalInput").ap()
    w_inT = nc.dram_tensor("w_inT", [DM, 2 * DI], F32, kind="ExternalInput").ap()
    conv_w = nc.dram_tensor("conv_w", [128, 128], F32, kind="ExternalInput").ap()
    conv_b = nc.dram_tensor("conv_b", [128, 32], F32, kind="ExternalInput").ap()
    w_xT = nc.dram_tensor("w_xT", [DI, 160], F32, kind="ExternalInput").ap()
    w_dtT = nc.dram_tensor("w_dtT", [DTR, DI], F32, kind="ExternalInput").ap()
    dt_bias = nc.dram_tensor("dt_bias", [128, 32], F32, kind="ExternalInput").ap()
    a_log = nc.dram_tensor("a_log", [128, 512], F32, kind="ExternalInput").ap()
    onehot = nc.dram_tensor("onehot", [32, 32 * 128], BF16, kind="ExternalInput").ap()

    u_out = nc.dram_tensor("u_out", [DI, 2 * TB], BF16, kind="ExternalOutput").ap()
    z_out = nc.dram_tensor("z_out", [DI, 2 * TB], BF16, kind="ExternalOutput").ap()
    d_out = nc.dram_tensor("d_out", [DI, 2 * TB], BF16, kind="ExternalOutput").ap()
    y0_out = nc.dram_tensor("y0_out", [DI, 2 * TB], F32, kind="ExternalOutput").ap()
    bc_out = nc.dram_tensor("bc_out", [32, 2 * TB], F32, kind="ExternalOutput").ap()
    sum_out = nc.dram_tensor("sum_out", [DI, 64], F32, kind="ExternalOutput").ap()

    xc_scr = nc.dram_tensor("xc_scr", [DI, NT], BF16, kind="ExternalOutput").ap()

    with stack:
        p = Prog(nc, stack)
        mm = p.sem("mm")
        ve = p.sem("ve")
        ac = p.sem("ac")
        s_c = p.sem("s_c")
        s_al = p.sem("s_al")
        s_oh = p.sem("s_oh")
        s_wx = p.sem("s_wx")
        s_wdt = p.sem("s_wdt")
        s_x = p.sem("s_x")
        s_w = [p.sem("s_w0"), p.sem("s_w1")]
        s_stg = [p.sem("s_stg0"), p.sem("s_stg1")]
        s_xc = p.sem("s_xc")
        s_y0 = p.sem("s_y0")
        s_big = p.sem("s_big")

        consts = p.sbuf("consts", [128, 704], F32)
        onehot_sb = p.sbuf("onehot_sb", [32, 32 * 128], BF16)
        wx_sb = p.sbuf("wx_sb", [128, 32 * 160], BF16)
        wdt_sb = p.sbuf("wdt_sb", [128, 32 * 128], BF16)
        summ = p.sbuf("summ", [128, 32 * 64], F32)

        ps = [p.psum(f"ps{i}", [128, 512], F32) for i in range(2)]
        ps_bc = p.psum("ps_bc", [32, TB], F32)
        ps_rep = [p.psum(f"ps_rep{i}", [128, TH], F32) for i in range(2)]

        def Acol(j, n):
            return consts[:, 192 + j * DS + n:193 + j * DS + n]

        p.dma("sync", consts[:, 0:128], conv_w, s_c)
        p.dma("sync", consts[:, 128:160], conv_b, s_c)
        v_c = p.dma("sync", consts[:, 160:192], dt_bias, s_c)
        v_al = p.dma("sync", consts[:, 192:704], a_log, s_al)
        v_oh = p.dma("sync", onehot_sb[:], onehot, s_oh)
        v_wx = p.dma("gpsimd", wx_sb[:].rearrange("p (j e) -> p j e", e=160),
                     w_xT.rearrange("(j p) e -> p j e", p=128), s_wx)
        v_wdt = p.dma("gpsimd", wdt_sb[:].rearrange("p (j e) -> p j e", e=128),
                      w_dtT.rearrange("p (j e) -> p j e", e=128), s_wdt)
        p.wait("scalar", s_al, v_al)
        v_expA = p.op("scalar", lambda e: e.activation(out=consts[:, 192:704], in_=consts[:, 192:704], func=AF.Exp), sem=ac)
        p.wait("vector", ac, v_expA)
        p.op("vector", lambda e: e.tensor_scalar_mul(consts[:, 192:704], consts[:, 192:704], -1.0), sem=ve)

        stackA = contextlib.ExitStack()
        x_bf = stackA.enter_context(nc.sbuf_tensor("x_bf", [128, 16 * TX], BF16))
        wbuf = [stackA.enter_context(nc.sbuf_tensor(f"wbuf{i}", [128, 16 * 128], BF16)) for i in range(2)]
        stg = [stackA.enter_context(nc.sbuf_tensor(f"stg{i}", [128, 258], BF16)) for i in range(2)]

        GRP = [(0, 258), (258, 257)]

        def stripe_ap(e):
            return w_inT[:, e * 128:(e + 1) * 128].rearrange("(k p) e -> p k e", p=128)

        v_evac = {}
        v_mm_stripe = {}
        git = 0
        for half in range(2):
            if half == 1:
                p.wait("gpsimd", mm, v_mm_stripe[(0, 63)])
            v_x = p.dma("gpsimd", x_bf[:].rearrange("p (j t) -> p j t", t=TX),
                        x_in[:, half * TX:(half + 1) * TX].rearrange("(j p) t -> p j t", p=128), s_x)
            v_wq = {0: p.dma("gpsimd", wbuf[0][:].rearrange("p (k e) -> p k e", e=128), stripe_ap(0), s_w[0])}
            p.wait("tensor", s_x, v_x)
            for e in range(64):
                buf = wbuf[e % 2]
                if e + 1 < 64:
                    if e >= 1:
                        p.wait("gpsimd", mm, v_mm_stripe[(half, e - 1)])
                    v_wq[e + 1] = p.dma("gpsimd", wbuf[(e + 1) % 2][:].rearrange("p (k e) -> p k e", e=128),
                                        stripe_ap(e + 1), s_w[(e + 1) % 2])
                p.wait("tensor", s_w[e % 2], v_wq[e])
                for gi, (c0, cn) in enumerate(GRP):
                    pb = ps[git % 2]
                    if git - 2 >= 0:
                        p.wait("tensor", ac, v_evac[git - 2])
                    for k in range(16):
                        def mmop(t, pb=pb, buf=buf, k=k, c0=c0, cn=cn, s=(k == 0), z=(k == 15)):
                            return t.matmul(pb[:, 0:cn], buf[:, k * 128:(k + 1) * 128],
                                            x_bf[:, k * TX + c0:k * TX + c0 + cn], start=s, stop=z)
                        if k == 15:
                            v_mm_stripe[(half, e)] = p.op("tensor", mmop, sem=mm)
                        else:
                            p.op("tensor", mmop)
                    sg = stg[git % 2]
                    sgsem = s_stg[git % 2]
                    p.wait("scalar", mm, v_mm_stripe[(half, e)])
                    p.wait("scalar", sgsem, sgsem.v)
                    v_evac[git] = p.op("scalar", lambda sc, pb=pb, sg=sg, cn=cn:
                                       sc.activation(out=sg[:, 0:cn], in_=pb[:, 0:cn], func=AF.Copy), sem=ac)
                    p.wait("sync", ac, v_evac[git])
                    if e < 32:
                        p.dma("sync", xc_scr[e * 128:(e + 1) * 128, half * TX + c0:half * TX + c0 + cn],
                              sg[:, 0:cn], sgsem)
                    else:
                        ecol = e - 32
                        a0, a1 = max(c0, 3), c0 + cn
                        if a0 < a1:
                            zcol = half * TB + (a0 - 3)
                            p.dma("sync", z_out[ecol * 128:(ecol + 1) * 128, zcol:zcol + (a1 - a0)],
                                  sg[:, a0 - c0:a1 - c0], sgsem)
                    git += 1
        stackA.close()

        if stop_after == "A":
            p.wait("sync", s_stg[0], s_stg[0].v)
            p.wait("sync", s_stg[1], s_stg[1].v)
            p.finalize()
            return nc

        stackB = contextlib.ExitStack()
        E = stackB.enter_context
        u_sb = E(nc.sbuf_tensor("u_sb", [128, 32 * TB], BF16))
        del_sb = E(nc.sbuf_tensor("del_sb", [128, 32 * TB], BF16))
        xcdu_sb = E(nc.sbuf_tensor("xcdu_sb", [128, 32 * TX], BF16))
        y0_sb = E(nc.sbuf_tensor("y0_sb", [128, 32 * TH], F32))
        cscr = y0_sb
        carry = E(nc.sbuf_tensor("carry", [128, 32 * DS], F32))
        dtr_sb = E(nc.sbuf_tensor("dtr_sb", [128, TB], BF16))
        bc_f = E(nc.sbuf_tensor("bc_f", [32, TB], F32))
        bc_b = E(nc.sbuf_tensor("bc_b", [32, TB], BF16))
        sumd = E(nc.sbuf_tensor("sumd", [128, 32], F32))
        brep = [E(nc.sbuf_tensor(f"brep{i}", [128, TH], BF16)) for i in range(2)]
        crep = [E(nc.sbuf_tensor(f"crep{i}", [128, TH], BF16)) for i in range(2)]
        at = [E(nc.sbuf_tensor(f"at{i}", [128, TH], F32)) for i in range(2)]
        bt = [E(nc.sbuf_tensor(f"bt{i}", [128, TH], BF16)) for i in range(2)]
        sst = [E(nc.sbuf_tensor(f"sst{i}", [128, TH], F32)) for i in range(2)]
        sct = [E(nc.sbuf_tensor(f"sct{i}", [128, TH], F32)) for i in range(2)]
        spa = [E(nc.sbuf_tensor(f"spa{i}", [128, TB], F32)) for i in range(2)]
        spb = [E(nc.sbuf_tensor(f"spb{i}", [128, TB], F32)) for i in range(2)]

        def xc(j, off):
            return xcdu_sb[:, j * TX + off:j * TX + off + TH]

        def du(j, h):
            return xcdu_sb[:, j * TB + h * TH:j * TB + (h + 1) * TH]

        def du_full(j):
            return xcdu_sb[:, j * TB:(j + 1) * TB]

        v_blk_prev_scan = None
        ps_dt = ps
        for bi in range(2):
            tb0 = bi * TB
            p.wait("sync", s_stg[0], s_stg[0].v)
            p.wait("sync", s_stg[1], s_stg[1].v)
            if bi == 1:
                p.wait("sync", ve, v_blk_prev_scan)
                p.wait("sync", s_big, s_big.v)
            v_xc = p.dma("sync", xcdu_sb[:].rearrange("p (j t) -> p j t", t=TX),
                         xc_scr[:, bi * TX:(bi + 1) * TX].rearrange("(j p) t -> p j t", p=128), s_xc)

            p.wait("vector", s_xc, v_xc)
            p.wait("vector", s_c, v_c)
            if bi == 1:
                p.wait("vector", s_y0, s_y0.v)
            v_silu = {}
            for h in range(2):
                v_conv = {}
                for j in range(32):
                    cs = planar(cscr, j, TH)
                    p.op("vector", lambda v, j=j, h=h, cs=cs: v.tensor_scalar(
                        out=cs, in0=xc(j, h * TH), scalar1=consts[:, j * 4:j * 4 + 1], scalar2=None, op0=ALU.mult))
                    for tau in range(1, 3):
                        p.op("vector", lambda v, j=j, h=h, tau=tau, cs=cs: v.scalar_tensor_tensor(
                            out=cs, in0=xc(j, h * TH + tau), scalar=consts[:, j * 4 + tau:j * 4 + tau + 1],
                            in1=cs, op0=ALU.mult, op1=ALU.add))
                    v_conv[j] = p.op("vector", lambda v, j=j, h=h, cs=cs: v.scalar_tensor_tensor(
                        out=cs, in0=xc(j, h * TH + 3), scalar=consts[:, j * 4 + 3:j * 4 + 4],
                        in1=cs, op0=ALU.mult, op1=ALU.add), sem=ve)
                for j in range(32):
                    p.wait("scalar", ve, v_conv[j])
                    if bi == 1:
                        p.wait("scalar", s_big, s_big.v)
                    v_silu[(h, j)] = p.op("scalar", lambda sc, j=j, h=h: sc.activation(
                        out=u_sb[:, j * TB + h * TH:j * TB + (h + 1) * TH], in_=planar(cscr, j, TH),
                        func=AF.Silu, bias=consts[:, 128 + j:129 + j]), sem=ac)

            if stop_after == "conv":
                for jj in range(32):
                    p.wait("sync", ac, v_silu[(1, jj)])
                p.wait("sync", ac, v_silu[(0, 31)])
                vu_dma = p.dma("sync", u_out[:, tb0:tb0 + TB].rearrange("(j p) t -> p j t", p=128), sb3(u_sb, TB), s_big)
                p.wait("sync", s_big, s_big.v)
                stackB.close()
                p.finalize()
                return nc

            p.wait("tensor", s_wx, v_wx)
            for j in range(32):
                p.wait("tensor", ac, v_silu[(1, j)])
                p.op("tensor", lambda t, j=j: t.matmul(ps_dt[0][:], wx_sb[:, j * 160:j * 160 + 128],
                                                       planar(u_sb, j, TB), start=(j == 0), stop=(j == 31)))
            for j in range(32):
                def mmbc(t, j=j):
                    return t.matmul(ps_bc[:], wx_sb[:, j * 160 + 128:(j + 1) * 160],
                                    planar(u_sb, j, TB), start=(j == 0), stop=(j == 31))
                if j == 31:
                    v_bcmm = p.op("tensor", mmbc, sem=mm)
                else:
                    p.op("tensor", mmbc)
            p.wait("scalar", mm, v_bcmm)
            p.op("scalar", lambda sc: sc.activation(out=dtr_sb[:], in_=ps_dt[0][:], func=AF.Copy))
            p.op("scalar", lambda sc: sc.activation(out=bc_f[:], in_=ps_bc[:], func=AF.Copy))
            v_bce = p.op("scalar", lambda sc: sc.activation(out=bc_b[:], in_=ps_bc[:], func=AF.Copy), sem=ac)
            p.wait("sync", ac, v_bce)
            p.dma("sync", bc_out[:, tb0:tb0 + TB], bc_f[:], s_big)

            if stop_after == "xdbl":
                p.wait("sync", ac, v_bce)
                p.dma("sync", d_out[0:128, 0:TB], dtr_sb[:], s_big)
                p.dma("sync", bc_out[:, tb0:tb0 + TB], bc_f[:], s_big)
                p.wait("sync", s_big, s_big.v)
                stackB.close()
                p.finalize()
                return nc

            p.wait("tensor", s_wdt, v_wdt)
            p.wait("tensor", ac, v_bce)
            v_dmm = {}
            v_dsp = {}
            v_sp1 = {}
            for j in range(32):
                pb = ps_dt[j % 2]
                if j - 2 >= 0:
                    p.wait("tensor", ve, v_dsp[j - 2])
                    p.wait("tensor", ac, v_sp1[j - 2])
                v_dmm[j] = p.op("tensor", lambda t, j=j, pb=pb: t.matmul(
                    pb[:], wdt_sb[:, j * 128:(j + 1) * 128], dtr_sb[:], start=True, stop=True), sem=mm)
                bcol = consts[:, 160 + j:161 + j]
                A_ = spa[j % 2]
                B_ = spb[j % 2]
                p.wait("scalar", mm, v_dmm[j])
                if j - 2 >= 0:
                    p.wait("scalar", ve, v_dsp[j - 2])
                v_abs = p.op("scalar", lambda sc, pb=pb, bcol=bcol, A_=A_: sc.activation(out=A_[:], in_=pb[:], func=AF.Abs, bias=bcol), sem=ac)
                p.op("scalar", lambda sc, A_=A_: sc.activation(out=A_[:], in_=A_[:], func=AF.Exp, scale=-1.0))
                v_sp1[j] = p.op("scalar", lambda sc, A_=A_: sc.activation(out=A_[:], in_=A_[:], func=AF.Ln, bias=1.0), sem=ac)
                p.wait("vector", mm, v_dmm[j])
                p.wait("vector", ac, v_abs)
                p.op("vector", lambda v, pb=pb, bcol=bcol, B_=B_: v.tensor_scalar(
                    out=B_[:], in0=pb[:], scalar1=bcol, scalar2=0.0, op0=ALU.add, op1=ALU.max))
                p.wait("vector", ac, v_sp1[j])
                p.op("vector", lambda v, j=j, A_=A_, B_=B_: v.tensor_tensor(planar(del_sb, j, TB), A_[:], B_[:], ALU.add))
                v_dsp[j] = p.op("vector", lambda v, j=j: v.reduce_sum(sumd[:, j:j + 1], planar(del_sb, j, TB), axis=AX.X), sem=ve)

            if stop_after == "delta":
                p.wait("sync", ve, v_dsp[31])
                p.dma("sync", d_out[:, tb0:tb0 + TB].rearrange("(j p) t -> p j t", p=128), sb3(del_sb, TB), s_big)
                p.dma("sync", bc_out[:, tb0:tb0 + TB], bc_f[:], s_big)
                p.wait("sync", s_big, s_big.v)
                stackB.close()
                p.finalize()
                return nc

            v_du = {}
            for j in range(32):
                v_du[j] = p.op("vector", lambda v, j=j: v.tensor_tensor(
                    du_full(j), planar(del_sb, j, TB), planar(u_sb, j, TB), ALU.mult), sem=ve)

            p.wait("scalar", ve, v_du[31])
            for j in range(32):
                p.op("scalar", lambda sc, j=j, bi=bi: sc.activation(
                    out=summ[:, j * 64 + bi * 32 + DS:j * 64 + bi * 32 + 2 * DS],
                    in_=consts[:, 192 + j * DS:192 + (j + 1) * DS], func=AF.Exp, scale=sumd[:, j:j + 1]))

            p.wait("tensor", s_oh, v_oh)
            v_scan = {}
            v_repe = {}
            isc = 0
            v_y0_dma = {}
            for h in range(2):
                for n in range(DS):
                    hn = h * DS + n
                    pr = ps_rep[hn % 2]
                    if hn - 2 >= 0:
                        p.wait("tensor", ac, v_repe[hn - 2])
                    vb = p.op("tensor", lambda t, n=n, h=h, pr=pr: t.matmul(
                        pr[:], onehot_sb[:, n * 128:(n + 1) * 128],
                        bc_b[:, h * TH:(h + 1) * TH], start=True, stop=True), sem=mm)
                    p.wait("scalar", mm, vb)
                    if hn - 2 >= 0:
                        p.wait("scalar", ve, v_scan[hn - 1, 31])
                    v_brepe = p.op("scalar", lambda sc, pr=pr, hn=hn: sc.activation(
                        out=brep[hn % 2][:], in_=pr[:], func=AF.Copy), sem=ac)
                    p.wait("tensor", ac, v_brepe)
                    vc = p.op("tensor", lambda t, n=n, h=h, pr=pr: t.matmul(
                        pr[:], onehot_sb[:, (16 + n) * 128:(17 + n) * 128],
                        bc_b[:, h * TH:(h + 1) * TH], start=True, stop=True), sem=mm)
                    p.wait("scalar", mm, vc)
                    v_repe[hn] = p.op("scalar", lambda sc, pr=pr, hn=hn: sc.activation(
                        out=crep[hn % 2][:], in_=pr[:], func=AF.Copy), sem=ac)
                    p.wait("vector", ac, v_repe[hn])
                    if h == 1 and n == 0:
                        p.wait("vector", s_y0, v_y0_dma[0])
                    for j in range(32):
                        ai = at[j % 2]; bj = bt[j % 2]; si = sst[j % 2]; ci = sct[j % 2]
                        if isc - 2 >= 0:
                            pn, pj = divmod(isc - 2, 32)
                            p.wait("scalar", ve, v_scan[pn, pj])
                        va = p.op("scalar", lambda sc, j=j, n=n, h=h, ai=ai: sc.activation(
                            out=ai[:], in_=del_sb[:, j * TB + h * TH:j * TB + (h + 1) * TH],
                            func=AF.Exp, scale=Acol(j, n)), sem=ac)
                        p.op("vector", lambda v, j=j, h=h, bj=bj, hn=hn: v.tensor_tensor(
                            bj[:], du(j, h), brep[hn % 2][:], ALU.mult))
                        p.wait("vector", ac, va)
                        init = 0.0 if h == 0 else carry[:, j * DS + n:j * DS + n + 1]
                        vscan = p.op("vector", lambda v, ai=ai, bj=bj, si=si, init=init: v.tensor_tensor_scan(
                            si[:], ai[:], bj[:], init, ALU.mult, ALU.add), sem=ve)
                        p.wait("vector", ve, vscan)
                        if h == 0:
                            p.op("vector", lambda v, si=si, j=j, n=n: v.tensor_copy(
                                carry[:, j * DS + n:j * DS + n + 1], si[:, TH - 1:TH]))
                        else:
                            p.op("vector", lambda v, si=si, j=j, n=n, bi=bi: v.tensor_copy(
                                summ[:, j * 64 + bi * 32 + n:j * 64 + bi * 32 + n + 1], si[:, TH - 1:TH]))
                        p.op("vector", lambda v, si=si, ci=ci, hn=hn: v.tensor_tensor(ci[:], si[:], crep[hn % 2][:], ALU.mult))
                        if n == 0:
                            v_scan[hn, j] = p.op("vector", lambda v, ci=ci, j=j: v.tensor_copy(planar(y0_sb, j, TH), ci[:]), sem=ve)
                        else:
                            v_scan[hn, j] = p.op("vector", lambda v, ci=ci, j=j: v.tensor_tensor(
                                planar(y0_sb, j, TH), planar(y0_sb, j, TH), ci[:], ALU.add), sem=ve)
                        isc += 1
                p.wait("sync", ve, v_scan[h * DS + DS - 1, 31])
                v_y0_dma[h] = p.dma("sync", y0_out[:, tb0 + h * TH:tb0 + (h + 1) * TH].rearrange(
                    "(j p) t -> p j t", p=128), sb3(y0_sb, TH), s_y0)

            v_scan_done = v_scan[2 * DS - 1, 31]
            p.dma("sync", u_out[:, tb0:tb0 + TB].rearrange("(j p) t -> p j t", p=128), sb3(u_sb, TB), s_big)
            p.dma("sync", d_out[:, tb0:tb0 + TB].rearrange("(j p) t -> p j t", p=128), sb3(del_sb, TB), s_big)
            v_blk_prev_scan = v_scan_done

        p.dma("sync", sum_out.rearrange("(j p) c -> p j c", p=128), sb3(summ, 64), s_big)
        p.wait("sync", s_big, s_big.v)
        p.wait("sync", s_y0, s_y0.v)
        stackB.close()
        p.finalize()
    return nc


def plan128(v):
    if v.ndim == 1:
        n = v.shape[0] // 128
        return np.ascontiguousarray(v.reshape(n, 128).T)
    n = v.shape[0] // 128
    k = v.shape[1]
    return np.ascontiguousarray(v.reshape(n, 128, k).transpose(1, 0, 2).reshape(128, n * k))


def host_inputs_l1(core, I):
    b0, b1 = blocks_of(core)
    x = I["L0_attn_hidden_states"].reshape(L, DM).T
    TX = 515
    xs = np.zeros((DM, 2 * TX), dtype=np.float32)
    for bi, blk in enumerate((b0, b1)):
        t0 = blk * TB
        lo = max(0, t0 - 3)
        xs[:, bi * TX + (3 - (t0 - lo)):(bi + 1) * TX] = x[:, lo:t0 + TB]
    return {
        "x": np.ascontiguousarray(xs),
        "w_inT": np.ascontiguousarray(I["L0_attn_W_in"].T),
        "conv_w": plan128(I["L0_attn_conv_w"]),
        "conv_b": plan128(I["L0_attn_conv_b"]),
        "w_xT": np.ascontiguousarray(I["L0_attn_W_x"].T),
        "w_dtT": np.ascontiguousarray(I["L0_attn_W_dt"].T),
        "dt_bias": plan128(I["L0_attn_dt_bias"]),
        "a_log": plan128(I["L0_attn_A_log"]),
        "onehot": host_onehot_bc(),
    }


def build_l2_mamba_post(stop_after=None):
    nc = bass.Bass("TRN2", target_bir_lowering=False, debug=False)
    stack = contextlib.ExitStack()
    TH = 256
    NTOK = 2 * TB

    y0_in = nc.dram_tensor("y0_in", [DI, NTOK], F32, kind="ExternalInput").ap()
    u_in = nc.dram_tensor("u_in", [DI, NTOK], BF16, kind="ExternalInput").ap()
    z_in = nc.dram_tensor("z_in", [DI, NTOK], BF16, kind="ExternalInput").ap()
    d_in = nc.dram_tensor("d_in", [DI, NTOK], BF16, kind="ExternalInput").ap()
    c_in = nc.dram_tensor("c_in", [32, NTOK], F32, kind="ExternalInput").ap()
    sums_all = nc.dram_tensor("sums_all", [DI, NBLK * 32], F32, kind="ExternalInput").ap()
    a_log = nc.dram_tensor("a_log", [128, 512], F32, kind="ExternalInput").ap()
    dvec = nc.dram_tensor("dvec", [128, 32], F32, kind="ExternalInput").ap()
    onehot = nc.dram_tensor("onehot", [32, 32 * 128], BF16, kind="ExternalInput").ap()
    h_mat = nc.dram_tensor("h_mat", [DI, DI], BF16, kind="ExternalInput").ap()
    w_outT = nc.dram_tensor("w_outT", [DI, DM], F32, kind="ExternalInput").ap()
    x_res = nc.dram_tensor("x_res", [DM, NTOK], F32, kind="ExternalInput").ap()
    sel_in = nc.dram_tensor("sel_in", [128, 32], F32, kind="ExternalInput").ap()

    x1_out = nc.dram_tensor("x1_out", [DM, NTOK], F32, kind="ExternalOutput").ap()
    yg_scr = nc.dram_tensor("yg_scr", [DI, NTOK], BF16, kind="ExternalOutput").ap()

    with stack:
        p = Prog(nc, stack)
        mm = p.sem("mm")
        ve = p.sem("ve")
        ac = p.sem("ac")
        s_c = p.sem("s_c")
        s_cl = p.sem("s_cl")
        s_sums = p.sem("s_sums")
        s_act = p.sem("s_act")
        s_yg = p.sem("s_yg")
        s_hw = [p.sem("s_hw0"), p.sem("s_hw1")]
        s_ygf = p.sem("s_ygf")
        s_x = [p.sem("s_x0"), p.sem("s_x1")]
        s_o = p.sem("s_o")

        consts = p.sbuf("consts", [128, 544], F32)
        onehot_sb = p.sbuf("onehot_sb", [32, 32 * 128], BF16)
        c_sb = p.sbuf("c_sb", [32, NTOK], BF16)
        sin_sb = p.sbuf("sin_sb", [128, 2 * 32 * DS], F32)
        sel_sb = p.sbuf("sel_sb", [128, 32], F32)
        ones_sb = p.sbuf("ones_sb", [128, TH], F32)
        onecol = p.sbuf("onecol", [128, 1], F32)
        onerow = p.sbuf("onerow", [1, 128], F32)
        miscc = p.sbuf("miscc", [128, 2], F32)

        psA = [p.psum(f"psA{i}", [128, 512], F32) for i in range(2)]
        ps_rep = [psA[0][:, 0:TH], psA[1][:, 0:TH]]

        def Acol(j, n):
            return consts[:, j * DS + n:j * DS + n + 1]

        v_al = p.dma("sync", consts[:, 0:512], a_log, s_c)
        p.dma("sync", sel_sb[:], sel_in, s_c)
        p.dma("sync", consts[:, 512:544], dvec, s_c)
        p.dma("sync", onehot_sb[:], onehot, s_c)
        v_cl = p.dma("gpsimd", c_sb[:], c_in, s_cl)
        p.wait("scalar", s_c, v_al)
        v_expA = p.op("scalar", lambda e: e.activation(out=consts[:, 0:512], in_=consts[:, 0:512], func=AF.Exp), sem=ac)
        p.wait("vector", ac, v_expA)
        p.op("vector", lambda e: e.tensor_scalar_mul(consts[:, 0:512], consts[:, 0:512], -1.0))
        p.op("vector", lambda e: e.memset(ones_sb[:], 1.0))
        p.op("vector", lambda e: e.memset(onecol[:], 1.0))
        p.op("vector", lambda e: e.memset(onerow[:], 1.0))
        p.op("vector", lambda e: e.memset(miscc[:, 0:1], 1e-5))
        v_msc = p.op("vector", lambda e: e.memset(miscc[:, 1:2], -0.5), sem=ve)
        p.wait("scalar", ve, v_msc)

        stackP = contextlib.ExitStack()
        sums_sb = stackP.enter_context(nc.sbuf_tensor("sums_sb", [128, 32 * NBLK * 32], F32))
        St = stackP.enter_context(nc.sbuf_tensor("St", [128, 32 * DS], F32))
        sin_all = stackP.enter_context(nc.sbuf_tensor("sin_all", [128, 32 * NBLK * DS], F32))
        v_sums = p.dma("sync", sums_sb[:].rearrange("p (j c) -> p j c", c=NBLK * 32),
                       sums_all.rearrange("(j p) c -> p j c", p=128), s_sums)
        p.wait("vector", s_sums, v_sums)
        p.wait("vector", s_c, s_c.v)
        p.op("vector", lambda v: v.memset(St[:], 0.0))

        def sums_col(j, b, kind):
            base = j * (NBLK * 32) + b * 32 + kind * DS
            return sums_sb[:, base:base + DS]

        def sin_all_col(j, b):
            return sin_all[:, (j * NBLK + b) * DS:(j * NBLK + b + 1) * DS]

        for b in range(NBLK):
            for j in range(32):
                p.op("vector", lambda v, j=j, b=b: v.tensor_copy(sin_all_col(j, b), planar(St, j, DS)))
            if b < NBLK - 1:
                for j in range(32):
                    p.op("vector", lambda v, j=j, b=b: v.tensor_tensor(
                        planar(St, j, DS), planar(St, j, DS), sums_col(j, b, 1), ALU.mult))
                    p.op("vector", lambda v, j=j, b=b: v.tensor_tensor(
                        planar(St, j, DS), planar(St, j, DS), sums_col(j, b, 0), ALU.add))
        for bi in range(2):
            for j in range(32):
                dst = sin_sb[:, (bi * 32 + j) * DS:(bi * 32 + j + 1) * DS]
                p.op("vector", lambda v, j=j, bi=bi, dst=dst: v.tensor_scalar(
                    out=dst, in0=sin_all_col(j, 0), scalar1=sel_sb[:, bi * 16:bi * 16 + 1],
                    scalar2=None, op0=ALU.mult))
                for b in range(1, NBLK):
                    p.op("vector", lambda v, j=j, b=b, bi=bi, dst=dst: v.scalar_tensor_tensor(
                        out=dst, in0=sin_all_col(j, b), scalar=sel_sb[:, bi * 16 + b:bi * 16 + b + 1],
                        in1=dst, op0=ALU.mult, op1=ALU.add))
        v_pref = p.op("vector", lambda v: v.tensor_copy(St[:, 0:1], St[:, 0:1]), sem=ve)
        stackP.close()

        def sin_col(bi, j, n):
            return sin_sb[:, (bi * 32 + j) * DS + n:(bi * 32 + j) * DS + n + 1]

        stackC = contextlib.ExitStack()
        E = stackC.enter_context
        y0q = E(nc.sbuf_tensor("y0q", [128, 32 * TH], F32))
        uq = E(nc.sbuf_tensor("uq", [128, 32 * TH], BF16))
        zq = E(nc.sbuf_tensor("zq", [128, 32 * TH], BF16))
        dq = E(nc.sbuf_tensor("dq", [128, 32 * TH], BF16))
        cumd = E(nc.sbuf_tensor("cumd", [128, 32 * TH], F32))
        ygq = E(nc.sbuf_tensor("ygq", [128, 32 * TH], BF16))
        ccar = E(nc.sbuf_tensor("ccar", [128, 32], F32))
        crep = [E(nc.sbuf_tensor(f"crep{i}", [128, TH], BF16)) for i in range(2)]
        pt = [E(nc.sbuf_tensor(f"pt{i}", [128, TH], F32)) for i in range(2)]
        ct = [E(nc.sbuf_tensor(f"ct{i}", [128, TH], F32)) for i in range(2)]
        szt = [E(nc.sbuf_tensor(f"szt{i}", [128, TH], F32)) for i in range(2)]
        gt = [E(nc.sbuf_tensor(f"gt{i}", [128, TH], F32)) for i in range(2)]

        p.wait("vector", ve, v_pref)
        p.wait("tensor", s_c, s_c.v)
        v_gate_done_prev = None
        for bi in range(2):
            for h in range(2):
                c0 = bi * TB + h * TH
                if v_gate_done_prev is not None:
                    p.wait("sync", ve, v_gate_done_prev)
                    p.wait("sync", s_yg, s_yg.v)
                else:
                    p.wait("sync", ve, v_pref)
                p.dma("sync", sb3(y0q, TH), y0_in[:, c0:c0 + TH].rearrange("(j p) t -> p j t", p=128), s_act)
                p.dma("sync", sb3(uq, TH), u_in[:, c0:c0 + TH].rearrange("(j p) t -> p j t", p=128), s_act)
                p.dma("sync", sb3(zq, TH), z_in[:, c0:c0 + TH].rearrange("(j p) t -> p j t", p=128), s_act)
                v_ld = p.dma("sync", sb3(dq, TH), d_in[:, c0:c0 + TH].rearrange("(j p) t -> p j t", p=128), s_act)
                p.wait("vector", s_act, v_ld)
                v_cum_last = None
                for j in range(32):
                    init = 0.0 if h == 0 else ccar[:, j:j + 1]
                    vs = p.op("vector", lambda v, j=j, init=init: v.tensor_tensor_scan(
                        planar(cumd, j, TH), ones_sb[:], planar(dq, j, TH), init, ALU.mult, ALU.add), sem=ve)
                    p.wait("vector", ve, vs)
                    v_cum_last = vs
                    if h == 0:
                        p.op("vector", lambda v, j=j: v.tensor_copy(ccar[:, j:j + 1], planar(cumd, j, TH)[:, TH - 1:TH]))
                p.wait("scalar", ve, v_cum_last)
                v_it = {}
                v_repe = {}
                isc = 0
                for n in range(DS):
                    pr = ps_rep[n % 2]
                    if n - 2 >= 0:
                        p.wait("tensor", ac, v_repe[n - 2])
                    p.wait("tensor", s_cl, v_cl)
                    vc_ = p.op("tensor", lambda t, n=n, c0=c0, pr=pr: t.matmul(
                        pr, onehot_sb[:, (16 + n) * 128:(17 + n) * 128],
                        c_sb[:, c0:c0 + TH], start=True, stop=True), sem=mm)
                    p.wait("scalar", mm, vc_)
                    if n - 1 >= 0:
                        p.wait("scalar", ve, v_it[(n - 1, 31)])
                    v_repe[n] = p.op("scalar", lambda sc, pr=pr, n=n: sc.activation(
                        out=crep[n % 2][:], in_=pr, func=AF.Copy), sem=ac)
                    p.wait("vector", ac, v_repe[n])
                    for j in range(32):
                        pi = pt[j % 2]; ci = ct[j % 2]
                        if isc - 2 >= 0:
                            pn, pj = divmod(isc - 2, 32)
                            p.wait("scalar", ve, v_it[(pn, pj)])
                        vp = p.op("scalar", lambda sc, j=j, n=n, pi=pi: sc.activation(
                            out=pi[:], in_=planar(cumd, j, TH), func=AF.Exp, scale=Acol(j, n)), sem=ac)
                        p.wait("vector", ac, vp)
                        p.op("vector", lambda v, pi=pi, ci=ci, j=j, n=n, bi=bi: v.scalar_tensor_tensor(
                            out=ci[:], in0=pi[:], scalar=sin_col(bi, j, n), in1=crep[n % 2][:],
                            op0=ALU.mult, op1=ALU.mult))
                        v_it[(n, j)] = p.op("vector", lambda v, ci=ci, j=j: v.tensor_tensor(
                            planar(y0q, j, TH), planar(y0q, j, TH), ci[:], ALU.add), sem=ve)
                        isc += 1
                v_gate = {}
                for j in range(32):
                    sz = szt[j % 2]; g = gt[j % 2]
                    if j - 2 >= 0:
                        p.wait("scalar", ve, v_gate[j - 2])
                    vs2 = p.op("scalar", lambda sc, j=j, sz=sz: sc.activation(
                        out=sz[:], in_=planar(zq, j, TH), func=AF.Silu), sem=ac)
                    p.wait("vector", ac, vs2)
                    p.op("vector", lambda v, j=j, g=g: v.scalar_tensor_tensor(
                        out=g[:], in0=planar(uq, j, TH), scalar=consts[:, 512 + j:513 + j],
                        in1=planar(y0q, j, TH), op0=ALU.mult, op1=ALU.add))
                    v_gate[j] = p.op("vector", lambda v, j=j, g=g, sz=sz: v.tensor_tensor(
                        planar(ygq, j, TH), g[:], sz[:], ALU.mult), sem=ve)
                p.wait("sync", ve, v_gate[31])
                p.dma("sync", yg_scr[:, c0:c0 + TH].rearrange("(j p) t -> p j t", p=128), sb3(ygq, TH), s_yg)
                v_gate_done_prev = v_gate[31]
        if stop_after == "gate":
            p.wait("sync", s_yg, s_yg.v)
            stackC.close()
            p.finalize()
            return nc
        stackC.close()

        stackH = contextlib.ExitStack()
        E = stackH.enter_context
        yh_sb = E(nc.sbuf_tensor("yh_sb", [128, 32 * NTOK], BF16))
        stackH2 = contextlib.ExitStack()
        yg_sb = stackH2.enter_context(nc.sbuf_tensor("yg_sb", [128, 32 * NTOK], BF16))
        hbuf = [stackH2.enter_context(nc.sbuf_tensor(f"hbuf{i}", [128, 32 * 128], BF16)) for i in range(2)]
        ps_h = psA

        p.wait("sync", s_yg, s_yg.v)
        v_ygf = p.dma("sync", sb3(yg_sb, NTOK), yg_scr.rearrange("(j p) t -> p j t", p=128), s_ygf)

        def hstripe_ap(e):
            return h_mat[:, e * 128:(e + 1) * 128].rearrange("(k p) e -> p k e", p=128)

        v_hl = {0: p.dma("sync", hbuf[0][:].rearrange("p (k e) -> p k e", e=128), hstripe_ap(0), s_hw[0])}
        v_hmm = {}
        v_hev = {}
        p.wait("tensor", s_ygf, v_ygf)
        git = 0
        for e in range(32):
            if e + 1 < 32:
                if e >= 1:
                    p.wait("sync", mm, v_hmm[(e - 1, 1)])
                v_hl[e + 1] = p.dma("sync", hbuf[(e + 1) % 2][:].rearrange("p (k e) -> p k e", e=128),
                                    hstripe_ap(e + 1), s_hw[(e + 1) % 2])
            p.wait("tensor", s_hw[e % 2], v_hl[e])
            for tg in range(2):
                pb = ps_h[git % 2]
                if git - 2 >= 0:
                    p.wait("tensor", ac, v_hev[git - 2])
                for k in range(32):
                    def mmop(t, pb=pb, e=e, k=k, tg=tg, s=(k == 0), z=(k == 31)):
                        return t.matmul(pb[:], hbuf[e % 2][:, k * 128:(k + 1) * 128],
                                        yg_sb[:, k * NTOK + tg * 512:k * NTOK + (tg + 1) * 512], start=s, stop=z)
                    if k == 31:
                        v_hmm[(e, tg)] = p.op("tensor", mmop, sem=mm)
                    else:
                        p.op("tensor", mmop)
                p.wait("scalar", mm, v_hmm[(e, tg)])
                v_hev[git] = p.op("scalar", lambda sc, pb=pb, e=e, tg=tg: sc.activation(
                    out=yh_sb[:, e * NTOK + tg * 512:e * NTOK + (tg + 1) * 512], in_=pb[:], func=AF.Copy), sem=ac)
                git += 1
        v_h_done = v_hev[git - 1]
        if stop_after == "H":
            p.wait("sync", ac, v_h_done)
            for k in range(32):
                p.dma("sync", yg_scr[k * 128:(k + 1) * 128, :], yh_sb[:, k * NTOK:(k + 1) * NTOK], s_yg)
            p.wait("sync", s_yg, s_yg.v)
            stackH2.close()
            stackH.close()
            p.finalize()
            return nc
        stackH2.close()

        stackW = contextlib.ExitStack()
        E = stackW.enter_context
        r_sb = E(nc.sbuf_tensor("r_sb", [128, 16 * NTOK], F32))
        wbuf = [E(nc.sbuf_tensor(f"wbuf{i}", [128, 32 * 128], BF16)) for i in range(2)]
        xbuf = [E(nc.sbuf_tensor(f"xbuf{i}", [128, NTOK], F32)) for i in range(2)]
        sq = [E(nc.sbuf_tensor("sq0", [128, 512], F32))] * 2
        rows = E(nc.sbuf_tensor("rows", [1, 3 * NTOK], F32))
        bca = E(nc.sbuf_tensor("bca", [128, NTOK], F32))
        bcb = E(nc.sbuf_tensor("bcb", [128, NTOK], F32))
        stgx = [E(nc.sbuf_tensor("stgx0", [128, NTOK], F32))] * 2
        ps_s = [p.psum(f"ps_s{i}", [1, NTOK], F32) for i in range(2)]
        ps_bb = p.psum("ps_bb", [128, 512], F32)

        def wstripe_ap(do):
            return w_outT[:, do * 128:(do + 1) * 128].rearrange("(k p) e -> p k e", p=128)

        s_wo = [p.sem("s_wo0"), p.sem("s_wo1")]
        s_xr = [p.sem("s_xr0"), p.sem("s_xr1")]
        p.wait("gpsimd", ac, v_h_done)
        p.wait("sync", ac, v_h_done)
        v_wl = {0: p.dma("gpsimd", wbuf[0][:].rearrange("p (k e) -> p k e", e=128), wstripe_ap(0), s_wo[0])}
        v_xl = {0: p.dma("sync", xbuf[0][:], x_res[0:128, :], s_xr[0])}
        v_wmm = {}
        v_res = {}
        v_sq = {}
        v_smm = {}
        p.wait("tensor", ac, v_h_done)
        for do in range(16):
            if do + 1 < 16:
                if do >= 1:
                    p.wait("gpsimd", mm, v_wmm[(do - 1, 1)])
                    p.wait("sync", ve, v_res[(do - 1, 1)])
                v_wl[do + 1] = p.dma("gpsimd", wbuf[(do + 1) % 2][:].rearrange("p (k e) -> p k e", e=128),
                                     wstripe_ap(do + 1), s_wo[(do + 1) % 2])
                v_xl[do + 1] = p.dma("sync", xbuf[(do + 1) % 2][:], x_res[(do + 1) * 128:(do + 2) * 128, :], s_xr[(do + 1) % 2])
            p.wait("tensor", s_wo[do % 2], v_wl[do])
            for tg in range(2):
                pb = ps_h[(do * 2 + tg) % 2]
                if (do * 2 + tg) - 2 >= 0:
                    p.wait("tensor", ve, v_res.get((do - 1, tg)))
                for k in range(32):
                    def mmop(t, pb=pb, do=do, k=k, tg=tg, s=(k == 0), z=(k == 31)):
                        return t.matmul(pb[:], wbuf[do % 2][:, k * 128:(k + 1) * 128],
                                        yh_sb[:, k * NTOK + tg * 512:k * NTOK + (tg + 1) * 512], start=s, stop=z)
                    if k == 31:
                        v_wmm[(do, tg)] = p.op("tensor", mmop, sem=mm)
                    else:
                        p.op("tensor", mmop)
                p.wait("vector", mm, v_wmm[(do, tg)])
                p.wait("vector", s_xr[do % 2], v_xl[do])
                v_res[(do, tg)] = p.op("vector", lambda v, pb=pb, do=do, tg=tg: v.tensor_tensor(
                    r_sb[:, do * NTOK + tg * 512:do * NTOK + (tg + 1) * 512], pb[:],
                    xbuf[do % 2][:, tg * 512:(tg + 1) * 512], ALU.add), sem=ve)
                p.wait("scalar", ve, v_res[(do, tg)])
                if (do * 2 + tg) - 1 >= 0:
                    p.wait("scalar", mm, v_smm.get((do * 2 + tg) - 1, 0))
                v_sq[(do, tg)] = p.op("scalar", lambda sc, do=do, tg=tg: sc.activation(
                    out=sq[0][:, 0:512],
                    in_=r_sb[:, do * NTOK + tg * 512:do * NTOK + (tg + 1) * 512], func=AF.Square), sem=ac)
                p.wait("tensor", ve, v_res[(do, tg)])
                p.op("tensor", lambda t, do=do, tg=tg: t.matmul(
                    ps_s[0][:, tg * 512:(tg + 1) * 512], onecol[:],
                    r_sb[:, do * NTOK + tg * 512:do * NTOK + (tg + 1) * 512],
                    start=(do == 0), stop=(do == 15)))
                p.wait("tensor", ac, v_sq[(do, tg)])
                v_smm[do * 2 + tg] = p.op("tensor", lambda t, do=do, tg=tg: t.matmul(
                    ps_s[1][:, tg * 512:(tg + 1) * 512], onecol[:],
                    sq[0][:, 0:512],
                    start=(do == 0), stop=(do == 15)), sem=mm)

        v_stats = v_smm[31]
        R = NTOK
        p.wait("vector", mm, v_stats)
        p.op("vector", lambda v: v.tensor_scalar_mul(rows[:, 0:R], ps_s[0][:], 1.0 / DM))
        p.op("vector", lambda v: v.tensor_scalar_mul(rows[:, R:2 * R], ps_s[1][:], 1.0 / DM))
        p.op("vector", lambda v: v.tensor_tensor(rows[:, 2 * R:3 * R], rows[:, 0:R], rows[:, 0:R], ALU.mult))
        v_var = p.op("vector", lambda v: v.tensor_tensor(rows[:, R:2 * R], rows[:, R:2 * R], rows[:, 2 * R:3 * R], ALU.subtract), sem=ve)
        p.wait("scalar", ve, v_var)
        p.op("scalar", lambda sc: sc.activation(out=rows[:, R:2 * R], in_=rows[:, R:2 * R], func=AF.Ln, bias=miscc[0:1, 0:1]))
        v_rstd = p.op("scalar", lambda sc: sc.activation(out=rows[:, R:2 * R], in_=rows[:, R:2 * R], func=AF.Exp, scale=miscc[0:1, 1:2]), sem=ac)
        p.wait("vector", ac, v_rstd)
        v_mr = p.op("vector", lambda v: v.tensor_tensor(rows[:, 2 * R:3 * R], rows[:, 0:R], rows[:, R:2 * R], ALU.mult), sem=ve)
        p.wait("tensor", ve, v_mr)
        v_bc = {}
        for tg in range(2):
            if tg > 0:
                p.wait("tensor", ac, v_bc[tg - 1])
            v1 = p.op("tensor", lambda t, tg=tg: t.matmul(
                ps_bb[:], onerow[:], rows[:, R + tg * 512:R + (tg + 1) * 512], start=True, stop=True), sem=mm)
            p.wait("scalar", mm, v1)
            vba = p.op("scalar", lambda sc, tg=tg: sc.activation(out=bca[:, tg * 512:(tg + 1) * 512], in_=ps_bb[:], func=AF.Copy), sem=ac)
            p.wait("tensor", ac, vba)
            v2 = p.op("tensor", lambda t, tg=tg: t.matmul(
                ps_bb[:], onerow[:], rows[:, 2 * R + tg * 512:2 * R + (tg + 1) * 512], start=True, stop=True), sem=mm)
            p.wait("scalar", mm, v2)
            v_bc[tg] = p.op("scalar", lambda sc, tg=tg: sc.activation(out=bcb[:, tg * 512:(tg + 1) * 512], in_=ps_bb[:], func=AF.Copy), sem=ac)
        p.wait("vector", ac, v_bc[1])
        v_x1 = {}
        for do in range(16):
            sx = stgx[do % 2]
            if do - 1 >= 0:
                p.wait("vector", s_o, 16 * do)
            p.op("vector", lambda v, do=do, sx=sx: v.tensor_tensor(sx[:], r_sb[:, do * NTOK:(do + 1) * NTOK], bca[:], ALU.mult))
            v_x1[do] = p.op("vector", lambda v, do=do, sx=sx: v.tensor_tensor(sx[:], sx[:], bcb[:], ALU.subtract), sem=ve)
            p.wait("sync", ve, v_x1[do])
            p.dma("sync", x1_out[do * 128:(do + 1) * 128, :], sx[:], s_o)
        p.wait("sync", s_o, s_o.v)
        p.wait("sync", s_yg, s_yg.v)
        stackW.close()
        stackH.close()
        p.finalize()
    return nc


def host_inputs_l2(core, I, l1outs, sums_all):
    b0, b1 = blocks_of(core)
    x = I["L0_attn_hidden_states"].reshape(L, DM).T
    toks = np.r_[b0 * TB:(b0 + 1) * TB, b1 * TB:(b1 + 1) * TB]
    o = l1outs[core]
    sel = np.zeros((128, 32), np.float32)
    sel[:, 0 * 16 + b0] = 1
    sel[:, 1 * 16 + b1] = 1
    return {
        "sel_in": sel,
        "y0_in": o["y0_out"], "u_in": o["u_out"], "z_in": o["z_out"], "d_in": o["d_out"],
        "c_in": o["bc_out"],
        "sums_all": sums_all,
        "a_log": plan128(I["L0_attn_A_log"]),
        "dvec": plan128(I["L0_attn_D"]),
        "onehot": host_onehot_bc(),
        "h_mat": hadamard_bf16(),
        "w_outT": np.ascontiguousarray(I["L0_attn_W_out"].T),
        "x_res": np.ascontiguousarray(x[:, toks]),
    }


_HAD = {}


def hadamard_bf16():
    if "H" not in _HAD:
        import math
        H = np.array([[1.0]], dtype=np.float32)
        while H.shape[0] < DI:
            H = np.block([[H, H], [H, -H]])
        _HAD["H"] = (H / math.sqrt(DI)).astype(BF)
    return _HAD["H"]


def host_sums_all(l1outs):
    sums = np.zeros((DI, NBLK * 32), np.float32)
    for c in range(NCORES):
        b0, b1 = blocks_of(c)
        sums[:, b0 * 32:(b0 + 1) * 32] = l1outs[c]["sum_out"][:, 0:32]
        sums[:, b1 * 32:(b1 + 1) * 32] = l1outs[c]["sum_out"][:, 32:64]
    return sums




NTOK = 2 * TB
IN8 = 8192


def _ffn_common(p, nc, x_in, out_t, lhsT_g, lhsT_u, lhsT_d, NE, scale_mode,
                consts_extra):
    raise NotImplementedError


def build_l3_ffn0():
    nc = bass.Bass("TRN2", target_bir_lowering=False, debug=False)
    stack = contextlib.ExitStack()

    x_in = nc.dram_tensor("x_in", [DM, NTOK], F32, kind="ExternalInput").ap()
    wg = nc.dram_tensor("wg", [DM, IN8], I8, kind="ExternalInput").ap()
    wu = nc.dram_tensor("wu", [DM, IN8], I8, kind="ExternalInput").ap()
    wd = nc.dram_tensor("wd", [IN8, DM], I8, kind="ExternalInput").ap()
    sg = nc.dram_tensor("sg", [128, 64], F32, kind="ExternalInput").ap()
    su = nc.dram_tensor("su", [128, 64], F32, kind="ExternalInput").ap()
    sd = nc.dram_tensor("sd", [128, 16], F32, kind="ExternalInput").ap()
    x2_out = nc.dram_tensor("x2_out", [DM, NTOK], F32, kind="ExternalOutput").ap()

    with stack:
        p = Prog(nc, stack)
        mm = p.sem("mm")
        ve = p.sem("ve")
        ac = p.sem("ac")
        s_c = p.sem("s_c")
        s_xb = p.sem("s_xb")
        s_xf = p.sem("s_xf")
        s_wg = [p.sem("s_wg0"), p.sem("s_wg1")]
        s_wu = [p.sem("s_wu0"), p.sem("s_wu1")]
        s_wd = [p.sem("s_wd0"), p.sem("s_wd1")]
        s_o = p.sem("s_o")

        consts = p.sbuf("consts", [128, 146], F32)
        onecol = p.sbuf("onecol", [128, 1], F32)
        onerow = p.sbuf("onerow", [1, 128], F32)
        x1b = p.sbuf("x1b", [128, 16 * NTOK], BF16)
        fuse = p.sbuf("fuse", [128, 64 * TB], BF16)
        r_sb = p.sbuf("r_sb", [128, 16 * NTOK], F32)
        stPH = contextlib.ExitStack()
        EP = stPH.enter_context
        wgbuf = [EP(nc.sbuf_tensor(f"wgbuf{i}", [128, 16 * 128], BF16)) for i in range(2)]
        wubuf = [EP(nc.sbuf_tensor(f"wubuf{i}", [128, 16 * 128], BF16)) for i in range(2)]
        wdbuf = [EP(nc.sbuf_tensor(f"wdbuf{i}", [128, DM], BF16)) for i in range(2)]
        gbuf = [EP(nc.sbuf_tensor(f"gbuf{i}", [128, TB], BF16)) for i in range(2)]
        ubuf = [EP(nc.sbuf_tensor(f"ubuf{i}", [128, TB], BF16)) for i in range(2)]
        xfb = EP(nc.sbuf_tensor("xfb", [128, 8 * TB], F32))

        psD = [p.psum(f"psD{i}", [128, 512], F32) for i in range(8)]

        p.dma("sync", consts[:, 0:64], sg, s_c)
        p.dma("sync", consts[:, 64:128], su, s_c)
        v_c = p.dma("sync", consts[:, 128:144], sd, s_c)
        p.op("vector", lambda v: v.memset(onecol[:], 1.0))
        p.op("vector", lambda v: v.memset(onerow[:], 1.0))
        p.op("vector", lambda v: v.memset(consts[:, 144:145], 1e-5))
        v_ms = p.op("vector", lambda v: v.memset(consts[:, 145:146], -0.5), sem=ve)
        v_xb = p.dma("gpsimd", sb3(x1b, NTOK), x_in.rearrange("(j p) t -> p j t", p=128), s_xb)

        def cstripe(w, e):
            return w[:, e * 128:(e + 1) * 128].rearrange("(k p) e -> p k e", p=128)

        v_wgl = {}
        v_wul = {}
        v_gmm = {}
        v_umm = {}
        v_fu = {}
        v_wgl[(0, 0)] = p.dma("gpsimd", sb3(wgbuf[0], 128), cstripe(wg, 0), s_wg[0])
        v_wul[(0, 0)] = p.dma("gpsimd", sb3(wubuf[0], 128), cstripe(wu, 0), s_wu[0])
        p.wait("tensor", s_xb, v_xb)
        p.wait("tensor", ve, v_ms)
        p.wait("scalar", s_c, v_c)

        v_res = {}
        v_smm = {}
        v_dn_done = None
        for tg in range(2):
            t0 = tg * TB
            for e in range(64):
                key = (tg, e)
                nkey = (tg, e + 1) if e + 1 < 64 else ((tg + 1, 0) if tg == 0 else None)
                if nkey is not None:
                    if (tg, e) != (0, 0):
                        p.wait("gpsimd", mm, v_umm[pkey])
                    v_wgl[nkey] = p.dma("gpsimd", sb3(wgbuf[(e + 1) % 2], 128),
                                        cstripe(wg, nkey[1] if nkey[0] == tg else 0), s_wg[(e + 1) % 2])
                    v_wul[nkey] = p.dma("gpsimd", sb3(wubuf[(e + 1) % 2], 128),
                                        cstripe(wu, nkey[1] if nkey[0] == tg else 0), s_wu[(e + 1) % 2])
                p.wait("tensor", s_wg[e % 2], v_wgl[key])
                p.wait("tensor", s_wu[e % 2], v_wul[key])
                if (tg, e) != (0, 0):
                    p.wait("tensor", ac, v_ev_prev)
                if tg == 1 and e == 0:
                    p.wait("tensor", ve, v_dn_done)
                for k in range(16):
                    def mmop(t, e=e, k=k, t0=t0, s=(k == 0), z=(k == 15)):
                        return t.matmul(psD[0][:], wgbuf[e % 2][:, k * 128:(k + 1) * 128],
                                        x1b[:, k * NTOK + t0:k * NTOK + t0 + TB], start=s, stop=z)
                    if k == 15:
                        v_gmm[key] = p.op("tensor", mmop, sem=mm)
                    else:
                        p.op("tensor", mmop)
                for k in range(16):
                    def mmop(t, e=e, k=k, t0=t0, s=(k == 0), z=(k == 15)):
                        return t.matmul(psD[1][:], wubuf[e % 2][:, k * 128:(k + 1) * 128],
                                        x1b[:, k * NTOK + t0:k * NTOK + t0 + TB], start=s, stop=z)
                    if k == 15:
                        v_umm[key] = p.op("tensor", mmop, sem=mm)
                    else:
                        p.op("tensor", mmop)
                pkey = key
                p.wait("scalar", mm, v_gmm[key])
                if (tg, e) != (0, 0):
                    p.wait("scalar", ve, v_fu[pfkey])
                p.op("scalar", lambda sc, e=e: sc.activation(
                    out=gbuf[e % 2][:], in_=psD[0][:], func=AF.Silu, scale=consts[:, e:e + 1]))
                p.wait("scalar", mm, v_umm[key])
                v_ev_prev = p.op("scalar", lambda sc, e=e: sc.activation(
                    out=ubuf[e % 2][:], in_=psD[1][:], func=AF.Copy, scale=consts[:, 64 + e:65 + e]), sem=ac)
                p.wait("vector", ac, v_ev_prev)
                v_fu[key] = p.op("vector", lambda v, e=e: v.tensor_tensor(
                    planar(fuse, e, TB), gbuf[e % 2][:], ubuf[e % 2][:], ALU.mult), sem=ve)
                pfkey = key

            for dmh in range(2):
                if not (tg == 0 and dmh == 0):
                    p.wait("sync", ve, v_res_last)
                v_x = p.dma("sync", sb3(xfb, TB),
                            x_in[dmh * 1024:(dmh + 1) * 1024, t0:t0 + TB].rearrange("(j p) t -> p j t", p=128), s_xf)
                v_wdl = {}
                p.wait("gpsimd", ve, v_fu[(tg, 63)])
                if not (tg == 0 and dmh == 0):
                    p.wait("gpsimd", ve, v_res_last)
                v_wdl[0] = p.dma("gpsimd", wdbuf[0][:], wd[0:128, :], s_wd[0])
                v_dmm = {}
                for e in range(64):
                    if e + 1 < 64:
                        if e >= 1:
                            p.wait("gpsimd", mm, v_dmm[e - 1])
                        v_wdl[e + 1] = p.dma("gpsimd", wdbuf[(e + 1) % 2][:], wd[(e + 1) * 128:(e + 2) * 128, :], s_wd[(e + 1) % 2])
                    p.wait("tensor", s_wd[e % 2], v_wdl[e])
                    if e == 0:
                        p.wait("tensor", ve, v_fu[(tg, 63)])
                        if not (tg == 0 and dmh == 0):
                            p.wait("tensor", ve, v_res_last)
                    for d8 in range(8):
                        def mmop(t, e=e, d8=d8, dmh=dmh, t0=t0, s=(e == 0), z=(e == 63)):
                            return t.matmul(psD[d8][:],
                                            wdbuf[e % 2][:, (dmh * 8 + d8) * 128:(dmh * 8 + d8 + 1) * 128],
                                            fuse[:, e * TB:(e + 1) * TB], start=s, stop=z)
                        if d8 == 7:
                            v_dmm[e] = p.op("tensor", mmop, sem=mm)
                        else:
                            p.op("tensor", mmop)
                p.wait("vector", mm, v_dmm[63])
                p.wait("vector", s_xf, v_x)
                ic = 2 * tg + dmh
                for d8 in range(8):
                    dm = dmh * 8 + d8
                    v_res[(ic, d8)] = p.op("vector", lambda v, dm=dm, d8=d8, t0=t0: v.scalar_tensor_tensor(
                        out=r_sb[:, dm * NTOK + t0:dm * NTOK + t0 + TB],
                        in0=psD[d8][:],
                        scalar=consts[:, 128 + dm:129 + dm],
                        in1=planar(xfb, d8, TB), op0=ALU.mult, op1=ALU.add), sem=ve)
                v_res_last = v_res[(ic, 7)]
            v_dn_done = v_res_last

        stPH.close()
        stT = contextlib.ExitStack()
        ET = stT.enter_context
        sq = ET(nc.sbuf_tensor("sq", [128, 512], F32))
        sqb = ET(nc.sbuf_tensor("sqb", [128, 512], F32))
        rows = ET(nc.sbuf_tensor("rows", [1, 3 * NTOK], F32))
        bca = ET(nc.sbuf_tensor("bca", [128, NTOK], F32))
        bcb = ET(nc.sbuf_tensor("bcb", [128, NTOK], F32))
        stgx = ET(nc.sbuf_tensor("stgx", [128, NTOK], F32))
        R = NTOK
        v_sqp = {}
        v_s1l = None
        for dm in range(16):
            p.wait("scalar", ve, v_res[(3, 7)])
            if dm >= 1:
                p.wait("scalar", mm, v_sqp[dm - 1])
            v_sq2 = p.op("scalar", lambda sc, dm=dm: sc.activation(
                out=sq[:], in_=r_sb[:, dm * NTOK:dm * NTOK + 512], func=AF.Square), sem=ac)
            v_sq3 = p.op("scalar", lambda sc, dm=dm: sc.activation(
                out=sqb[:], in_=r_sb[:, dm * NTOK + 512:dm * NTOK + NTOK], func=AF.Square), sem=ac)
            if dm == 0:
                p.wait("tensor", ve, v_res[(3, 7)])
            for tgb in range(2):
                p.op("tensor", lambda t, dm=dm, tgb=tgb, s=(dm == 0), z=(dm == 15): t.matmul(
                    psD[tgb][0:1, :], onecol[:], r_sb[:, dm * NTOK + tgb * 512:dm * NTOK + (tgb + 1) * 512],
                    start=s, stop=z))
            p.wait("tensor", ac, v_sq3)
            p.op("tensor", lambda t, dm=dm, s=(dm == 0), z=(dm == 15): t.matmul(
                psD[2][0:1, :], onecol[:], sq[:], start=s, stop=z))
            v_sqp[dm] = p.op("tensor", lambda t, dm=dm, s=(dm == 0), z=(dm == 15): t.matmul(
                psD[3][0:1, :], onecol[:], sqb[:], start=s, stop=z), sem=mm)
        v_stats = v_sqp[15]
        p.wait("vector", mm, v_stats)
        p.op("vector", lambda v: v.tensor_copy(rows[:, 0:512], psD[0][0:1, :]))
        p.op("vector", lambda v: v.tensor_copy(rows[:, 512:1024], psD[1][0:1, :]))
        p.op("vector", lambda v: v.tensor_copy(rows[:, R:R + 512], psD[2][0:1, :]))
        p.op("vector", lambda v: v.tensor_copy(rows[:, R + 512:R + 1024], psD[3][0:1, :]))
        p.op("vector", lambda v: v.tensor_scalar_mul(rows[:, 0:R], rows[:, 0:R], 1.0 / DM))
        p.op("vector", lambda v: v.tensor_scalar_mul(rows[:, R:2 * R], rows[:, R:2 * R], 1.0 / DM))
        p.op("vector", lambda v: v.tensor_tensor(rows[:, 2 * R:3 * R], rows[:, 0:R], rows[:, 0:R], ALU.mult))
        v_var = p.op("vector", lambda v: v.tensor_tensor(rows[:, R:2 * R], rows[:, R:2 * R], rows[:, 2 * R:3 * R], ALU.subtract), sem=ve)
        p.wait("scalar", ve, v_var)
        p.op("scalar", lambda sc: sc.activation(out=rows[:, R:2 * R], in_=rows[:, R:2 * R], func=AF.Ln, bias=consts[0:1, 144:145]))
        v_rstd = p.op("scalar", lambda sc: sc.activation(out=rows[:, R:2 * R], in_=rows[:, R:2 * R], func=AF.Exp, scale=consts[0:1, 145:146]), sem=ac)
        p.wait("vector", ac, v_rstd)
        v_mr = p.op("vector", lambda v: v.tensor_tensor(rows[:, 2 * R:3 * R], rows[:, 0:R], rows[:, R:2 * R], ALU.mult), sem=ve)
        p.wait("tensor", ve, v_mr)
        v_bc = {}
        for tgb in range(2):
            if tgb > 0:
                p.wait("tensor", ac, v_bc[tgb - 1])
            v1 = p.op("tensor", lambda t, tgb=tgb: t.matmul(
                psD[0][:], onerow[:], rows[:, R + tgb * 512:R + (tgb + 1) * 512], start=True, stop=True), sem=mm)
            p.wait("scalar", mm, v1)
            vba = p.op("scalar", lambda sc, tgb=tgb: sc.activation(out=bca[:, tgb * 512:(tgb + 1) * 512], in_=psD[0][:], func=AF.Copy), sem=ac)
            p.wait("tensor", ac, vba)
            v2 = p.op("tensor", lambda t, tgb=tgb: t.matmul(
                psD[0][:], onerow[:], rows[:, 2 * R + tgb * 512:2 * R + (tgb + 1) * 512], start=True, stop=True), sem=mm)
            p.wait("scalar", mm, v2)
            v_bc[tgb] = p.op("scalar", lambda sc, tgb=tgb: sc.activation(out=bcb[:, tgb * 512:(tgb + 1) * 512], in_=psD[0][:], func=AF.Copy), sem=ac)
        p.wait("vector", ac, v_bc[1])
        for do in range(16):
            if do - 1 >= 0:
                p.wait("vector", s_o, 16 * do)
            p.op("vector", lambda v, do=do: v.tensor_tensor(stgx[:], r_sb[:, do * NTOK:(do + 1) * NTOK], bca[:], ALU.mult))
            vx1 = p.op("vector", lambda v, do=do: v.tensor_tensor(stgx[:], stgx[:], bcb[:], ALU.subtract), sem=ve)
            p.wait("sync", ve, vx1)
            p.dma("sync", x2_out[do * 128:(do + 1) * 128, :], stgx[:], s_o)
        p.wait("sync", s_o, s_o.v)
        stT.close()
        p.finalize()
    return nc


def host_inputs_l3(core, I, x1_by_core):
    return {
        "x_in": x1_by_core[core],
        "wg": I["L0_ffn_w_gate"],
        "wu": I["L0_ffn_w_up"],
        "wd": I["L0_ffn_w_down"],
        "sg": plan128(I["L0_ffn_s_gate"]),
        "su": plan128(I["L0_ffn_s_up"]),
        "sd": plan128(I["L0_ffn_s_down"]),
    }


def build_l4_qkv():
    nc = bass.Bass("TRN2", target_bir_lowering=False, debug=False)
    stack = contextlib.ExitStack()
    KVL, QL = 512, 1536

    x_in = nc.dram_tensor("x_in", [DM, NTOK], F32, kind="ExternalInput").ap()
    w_kvd = nc.dram_tensor("w_kvd", [DM, KVL], F32, kind="ExternalInput").ap()
    w_qd = nc.dram_tensor("w_qd", [DM, QL], F32, kind="ExternalInput").ap()
    w_kr = nc.dram_tensor("w_kr", [DM, 64], F32, kind="ExternalInput").ap()
    w_kc = nc.dram_tensor("w_kc", [KVL, 1024], F32, kind="ExternalInput").ap()
    w_qc = nc.dram_tensor("w_qc", [QL, 1024], F32, kind="ExternalInput").ap()
    w_qr = nc.dram_tensor("w_qr", [QL, 1024], F32, kind="ExternalInput").ap()
    w_v = nc.dram_tensor("w_v", [KVL, DM], F32, kind="ExternalInput").ap()
    rotm = nc.dram_tensor("rotm", [128, 128], BF16, kind="ExternalInput").ap()
    cosp = nc.dram_tensor("cosp", [128, NTOK], F32, kind="ExternalInput").ap()
    sinp = nc.dram_tensor("sinp", [128, NTOK], F32, kind="ExternalInput").ap()

    q_out = nc.dram_tensor("q_out", [DM, NTOK], BF16, kind="ExternalOutput").ap()
    k_out = nc.dram_tensor("k_out", [DM, NTOK], BF16, kind="ExternalOutput").ap()
    v_out = nc.dram_tensor("v_out", [NTOK, DM], BF16, kind="ExternalOutput").ap()

    with stack:
        p = Prog(nc, stack)
        mm = p.sem("mm")
        ve = p.sem("ve")
        ac = p.sem("ac")
        s_c = p.sem("s_c")
        s_w = [p.sem(f"s_w{i}") for i in range(3)]
        s_xb = p.sem("s_xb")
        s_o = p.sem("s_o")
        s_o2 = p.sem("s_o2")
        s_o3 = p.sem("s_o3")

        x2b = p.sbuf("x2b", [128, 16 * NTOK], BF16)
        ckv = p.sbuf("ckv", [128, 4 * NTOK], BF16)
        cq = p.sbuf("cq", [128, 12 * NTOK], BF16)
        rot_sb = p.sbuf("rot_sb", [128, 128], BF16)
        cos_sb = p.sbuf("cos_sb", [128, NTOK], F32)
        sin_sb = p.sbuf("sin_sb", [128, NTOK], F32)
        scl = p.sbuf("scl", [128, 1], F32)
        qr_sb = p.sbuf("qr_sb", [128, 8 * NTOK], BF16)
        qc2 = p.sbuf("qc2", [128, 8 * NTOK], BF16)
        kc3 = p.sbuf("kc3", [128, 8 * NTOK], BF16)
        rtmp = p.sbuf("rtmp", [128, NTOK], BF16)
        f1 = p.sbuf("f1", [128, NTOK], F32)
        f2 = p.sbuf("f2", [128, NTOK], F32)
        kr_sb = p.sbuf("kr_sb", [64, NTOK], BF16)
        asm = p.sbuf("asm", [128, NTOK], BF16)
        asm2 = p.sbuf("asm2", [128, NTOK], BF16)
        vst = p.sbuf("vst", [128, DM], BF16)

        ps = [p.psum(f"ps{i}", [128, 512], F32) for i in range(4)]

        p.dma("sync", rot_sb[:], rotm, s_c)
        p.dma("sync", cos_sb[:], cosp, s_c)
        v_cc = p.dma("sync", sin_sb[:], sinp, s_c)
        v_scl = p.op("vector", lambda v: v.memset(scl[:], 0.08838834764831845), sem=ve)
        v_xb = p.dma("gpsimd", sb3(x2b, NTOK), x_in.rearrange("(j p) t -> p j t", p=128), s_xb)

        p.wait("tensor", s_xb, v_xb)
        p.wait("tensor", s_c, v_cc)
        p.wait("tensor", ve, v_scl)

        v_ev = {}
        git = [0]

        def proj(dst, wsb, wstride, oc, src_chunks, rhs, scale=None, out_rows=128,
                 dst_off=None, wcoloff=0, wait_first=None):
            last = None
            for tg in range(2):
                pb = ps[git[0] % 4]
                if git[0] - 4 >= 0:
                    p.wait("tensor", ac, v_ev[git[0] - 4])
                if wait_first is not None and tg == 0:
                    p.wait("tensor", wait_first[0], wait_first[1])
                for k in range(src_chunks):
                    def mmop(t, pb=pb, k=k, oc=oc, tg=tg, wsb=wsb, wstride=wstride, rhs=rhs,
                             out_rows=out_rows, wcoloff=wcoloff,
                             s=(k == 0), z=(k == src_chunks - 1)):
                        return t.matmul(pb[0:out_rows, :],
                                        wsb[:, k * wstride + wcoloff + oc * 128:k * wstride + wcoloff + oc * 128 + out_rows],
                                        rhs[:, k * NTOK + tg * TB:k * NTOK + (tg + 1) * TB], start=s, stop=z)
                    if k == src_chunks - 1:
                        vmm = p.op("tensor", mmop, sem=mm)
                    else:
                        p.op("tensor", mmop)
                p.wait("scalar", mm, vmm)
                off = (dst_off if dst_off is not None else oc * NTOK) + tg * TB
                v_ev[git[0]] = p.op("scalar", lambda sc, pb=pb, dst=dst, off=off, scale=scale, out_rows=out_rows:
                                    sc.activation(out=dst[0:out_rows, off:off + TB],
                                                  in_=pb[0:out_rows, :], func=AF.Copy,
                                                  **({"scale": scale} if scale is not None else {})), sem=ac)
                last = v_ev[git[0]]
                git[0] += 1
            return last

        stW1 = contextlib.ExitStack()
        wkvd_sb = stW1.enter_context(nc.sbuf_tensor("wkvd_sb", [128, 16 * KVL], BF16))
        wqd_sb = stW1.enter_context(nc.sbuf_tensor("wqd_sb", [128, 16 * QL], BF16))
        wkr_sb = stW1.enter_context(nc.sbuf_tensor("wkr_sb", [128, 16 * 64], BF16))
        p.dma("gpsimd", wkvd_sb[:].rearrange("p (j e) -> p j e", e=KVL), w_kvd.rearrange("(j p) e -> p j e", p=128), s_w[0])
        p.dma("gpsimd", wqd_sb[:].rearrange("p (j e) -> p j e", e=QL), w_qd.rearrange("(j p) e -> p j e", p=128), s_w[0])
        v_w1 = p.dma("gpsimd", wkr_sb[:].rearrange("p (j e) -> p j e", e=64), w_kr.rearrange("(j p) e -> p j e", p=128), s_w[0])
        p.wait("tensor", s_w[0], v_w1)
        for oc in range(4):
            proj(ckv, wkvd_sb, KVL, oc, 16, x2b)
        for oc in range(12):
            proj(cq, wqd_sb, QL, oc, 16, x2b)
        v_kr = proj(kr_sb, wkr_sb, 64, 0, 16, x2b, out_rows=64, dst_off=0)
        if git[0] - 4 >= 0:
            p.wait("tensor", ac, v_ev[git[0] - 4])
        v_w1_mm_done = p.op("tensor", lambda t, pb=ps[git[0] % 4]: t.matmul(pb[0:1, 0:1], rot_sb[0:1, 0:1], rot_sb[0:1, 0:1], start=True, stop=True), sem=mm)
        stW1.close()

        stW2 = contextlib.ExitStack()
        wqr_sb = stW2.enter_context(nc.sbuf_tensor("wqr_sb", [128, 12 * 1024], BF16))
        wqc_sb = stW2.enter_context(nc.sbuf_tensor("wqc_sb", [128, 12 * 1024], BF16))
        p.wait("gpsimd", mm, v_w1_mm_done)
        p.dma("gpsimd", wqr_sb[:].rearrange("p (j e) -> p j e", e=1024), w_qr.rearrange("(j p) e -> p j e", p=128), s_w[1])
        v_w2 = p.dma("gpsimd", wqc_sb[:].rearrange("p (j e) -> p j e", e=1024), w_qc.rearrange("(j p) e -> p j e", p=128), s_w[1])
        v_qr = None
        for oc in range(8):
            v_qr = proj(qr_sb, wqr_sb, 1024, oc, 12, cq, scale=scl[:, 0:1],
                        wait_first=((s_w[1], v_w2) if oc == 0 else None))
        v_qc = None
        for oc in range(8):
            v_qc = proj(qc2, wqc_sb, 1024, oc, 12, cq, scale=scl[:, 0:1])

        v_rope = {}
        v_rope_prev = None
        for oc in range(8):
            for tg in range(2):
                pb = ps[git[0] % 4]
                if git[0] - 4 >= 0:
                    p.wait("tensor", ac, v_ev[git[0] - 4])
                p.wait("tensor", ac, v_qr)
                vmm = p.op("tensor", lambda t, pb=pb, oc=oc, tg=tg: t.matmul(
                    pb[:], rot_sb[:], qr_sb[:, oc * NTOK + tg * TB:oc * NTOK + (tg + 1) * TB],
                    start=True, stop=True), sem=mm)
                p.wait("scalar", mm, vmm)
                if v_rope_prev is not None:
                    p.wait("scalar", ve, v_rope_prev)
                v_ev[git[0]] = p.op("scalar", lambda sc, pb=pb: sc.activation(
                    out=rtmp[:, 0:TB], in_=pb[:], func=AF.Copy), sem=ac)
                p.wait("vector", ac, v_ev[git[0]])
                git[0] += 1
                p.op("vector", lambda v, oc=oc, tg=tg: v.tensor_tensor(
                    f1[:, 0:TB], qr_sb[:, oc * NTOK + tg * TB:oc * NTOK + (tg + 1) * TB],
                    cos_sb[:, tg * TB:(tg + 1) * TB], ALU.mult))
                p.op("vector", lambda v, tg=tg: v.tensor_tensor(
                    f2[:, 0:TB], rtmp[:, 0:TB], sin_sb[:, tg * TB:(tg + 1) * TB], ALU.mult))
                v_rope[(oc, tg)] = p.op("vector", lambda v, oc=oc, tg=tg: v.tensor_tensor(
                    qr_sb[:, oc * NTOK + tg * TB:oc * NTOK + (tg + 1) * TB], f1[:, 0:TB], f2[:, 0:TB], ALU.add), sem=ve)
                v_rope_prev = v_rope[(oc, tg)]

        p.wait("vector", ac, v_kr)
        p.op("vector", lambda v: v.tensor_tensor(f1[0:64, 0:NTOK], kr_sb[:], cos_sb[0:64, :], ALU.mult))
        for tg in range(2):
            pb = ps[git[0] % 4]
            if git[0] - 4 >= 0:
                p.wait("tensor", ac, v_ev[git[0] - 4])
            p.wait("tensor", ac, v_kr)
            vmm = p.op("tensor", lambda t, pb=pb, tg=tg: t.matmul(
                pb[0:64, :], rot_sb[0:64, 0:64], kr_sb[:, tg * TB:(tg + 1) * TB], start=True, stop=True), sem=mm)
            p.wait("scalar", mm, vmm)
            p.wait("scalar", ve, v_rope_prev)
            v_ev[git[0]] = p.op("scalar", lambda sc, pb=pb, tg=tg: sc.activation(
                out=rtmp[0:64, tg * TB:(tg + 1) * TB], in_=pb[0:64, :], func=AF.Copy), sem=ac)
            git[0] += 1
        p.wait("vector", ac, v_ev[git[0] - 1])
        p.op("vector", lambda v: v.tensor_tensor(f2[0:64, 0:NTOK], rtmp[0:64, 0:NTOK], sin_sb[0:64, :], ALU.mult))
        p.op("vector", lambda v: v.tensor_tensor(f1[0:64, 0:NTOK], f1[0:64, 0:NTOK], f2[0:64, 0:NTOK], ALU.add))
        v_krb = p.op("vector", lambda v: v.tensor_copy(kr_sb[:], f1[0:64, 0:NTOK]), sem=ve)

        for h in range(16):
            til, half = h // 2, (h % 2) * 64
            if h == 0:
                p.wait("vector", ac, v_qc)
                p.wait("vector", ve, v_rope[(7, 1)])
            if h > 0:
                p.wait("vector", s_o, 16 * h)
            p.op("vector", lambda v, til=til, half=half: v.tensor_copy(
                asm[0:64, :], qc2[half:half + 64, til * NTOK:(til + 1) * NTOK]))
            vq = p.op("vector", lambda v, til=til, half=half: v.tensor_copy(
                asm[64:128, :], qr_sb[half:half + 64, til * NTOK:(til + 1) * NTOK]), sem=ve)
            p.wait("sync", ve, vq)
            p.dma("sync", q_out[h * 128:(h + 1) * 128, :], asm[:], s_o)
        v_w2_mm_done = v_ev[git[0] - 1]
        stW2.close()

        stW3 = contextlib.ExitStack()
        wkc_sb = stW3.enter_context(nc.sbuf_tensor("wkc_sb", [128, 4 * 1024], BF16))
        wv_sb = stW3.enter_context(nc.sbuf_tensor("wv_sb", [128, 4 * DM], BF16))
        p.wait("gpsimd", ac, v_w2_mm_done)
        p.dma("gpsimd", wkc_sb[:].rearrange("p (j e) -> p j e", e=1024), w_kc.rearrange("(j p) e -> p j e", p=128), s_w[2])
        v_w3 = p.dma("gpsimd", wv_sb[:].rearrange("p (j e) -> p j e", e=DM), w_v.rearrange("(j p) e -> p j e", p=128), s_w[2])
        v_kc = None
        for oc in range(8):
            v_kc = proj(kc3, wkc_sb, 1024, oc, 4, ckv, wait_first=((s_w[2], v_w3) if oc == 0 else None))
        for h in range(16):
            til, half = h // 2, (h % 2) * 64
            if h == 0:
                p.wait("vector", ac, v_kc)
                p.wait("vector", ve, v_krb)
            if h > 0:
                p.wait("vector", s_o2, 16 * h)
            p.op("vector", lambda v, til=til, half=half: v.tensor_copy(
                asm2[0:64, :], kc3[half:half + 64, til * NTOK:(til + 1) * NTOK]))
            vk = p.op("vector", lambda v: v.tensor_copy(asm2[64:128, :], kr_sb[:]), sem=ve)
            p.wait("sync", ve, vk)
            p.dma("sync", k_out[h * 128:(h + 1) * 128, :], asm2[:], s_o2)

        vidx = 0
        v_vev = {}
        for tc in range(8):
            for ng in range(4):
                pb = ps[git[0] % 4]
                if git[0] - 4 >= 0:
                    p.wait("tensor", ac, v_ev[git[0] - 4])
                for k in range(4):
                    def mmop(t, pb=pb, tc=tc, ng=ng, k=k, s=(k == 0), z=(k == 3)):
                        return t.matmul(pb[:], ckv[:, k * NTOK + tc * 128:k * NTOK + (tc + 1) * 128],
                                        wv_sb[:, k * DM + ng * 512:k * DM + (ng + 1) * 512], start=s, stop=z)
                    if k == 3:
                        vmm = p.op("tensor", mmop, sem=mm)
                    else:
                        p.op("tensor", mmop)
                p.wait("scalar", mm, vmm)
                if vidx >= 4:
                    p.wait("scalar", s_o3, 16 * (vidx - 3))
                v_ev[git[0]] = p.op("scalar", lambda sc, pb=pb, ng=ng: sc.activation(
                    out=vst[:, ng * 512:(ng + 1) * 512], in_=pb[:], func=AF.Copy), sem=ac)
                p.wait("sync", ac, v_ev[git[0]])
                git[0] += 1
                p.dma("sync", v_out[tc * 128:(tc + 1) * 128, ng * 512:(ng + 1) * 512], vst[:, ng * 512:(ng + 1) * 512], s_o3)
                vidx += 1

        p.wait("sync", s_o, s_o.v)
        p.wait("sync", s_o2, s_o2.v)
        p.wait("sync", s_o3, s_o3.v)
        stW3.close()
        p.finalize()
    return nc


def host_rope_tables(core):
    b0, b1 = blocks_of(core)
    toks = np.r_[b0 * TB:(b0 + 1) * TB, b1 * TB:(b1 + 1) * TB].astype(np.float32)
    RD = 64
    inv = 1.0 / 10000.0 ** (np.arange(0, RD, 2, dtype=np.float32) / RD)
    fr = np.outer(toks, inv)
    c = np.cos(fr).T
    s = np.sin(fr).T
    cos64 = np.concatenate([c, c], 0)
    sin64 = np.concatenate([s, s], 0)
    cosp = np.concatenate([cos64, cos64], 0).astype(np.float32)
    sinp = np.concatenate([sin64, sin64], 0).astype(np.float32)
    rot = np.zeros((128, 128), np.float32)
    for r in range(128):
        base = (r // 64) * 64
        i = r % 64
        if i < 32:
            rot[base + i + 32, r] = -1.0
        else:
            rot[base + i - 32, r] = 1.0
    return cosp, sinp, rot.astype(BF)


def host_inputs_l4(core, I, x2_by_core):
    cosp, sinp, rot = host_rope_tables(core)
    return {
        "x_in": x2_by_core[core],
        "w_kvd": I["L1_attn_W_kv_down"],
        "w_qd": I["L1_attn_W_q_down"],
        "w_kr": I["L1_attn_W_kr"],
        "w_kc": I["L1_attn_W_kc"],
        "w_qc": I["L1_attn_W_qc"],
        "w_qr": I["L1_attn_W_qr"],
        "w_v": I["L1_attn_W_v"],
        "rotm": rot, "cosp": cosp, "sinp": sinp,
    }

NKS = 64


def build_l5_attn(dbg=False):
    nc = bass.Bass("TRN2", target_bir_lowering=False, debug=False)
    stack = contextlib.ExitStack()

    q_in = nc.dram_tensor("q_in", [DM, NTOK], BF16, kind="ExternalInput").ap()
    k_in = nc.dram_tensor("k_in", [DM, NKS * 128], BF16, kind="ExternalInput").ap()
    v_in = nc.dram_tensor("v_in", [NKS * 128, DM], BF16, kind="ExternalInput").ap()
    m_in = nc.dram_tensor("m_in", [NKS * 128, NTOK], BF16, kind="ExternalInput").ap()
    x_res = nc.dram_tensor("x_res", [DM, NTOK], F32, kind="ExternalInput").ap()
    w_o = nc.dram_tensor("w_o", [DM, DM], F32, kind="ExternalInput").ap()
    b_o = nc.dram_tensor("b_o", [128, 16], F32, kind="ExternalInput").ap()
    x3_out = nc.dram_tensor("x3_out", [DM, NTOK], F32, kind="ExternalOutput").ap()
    if dbg:
        l_dbg = nc.dram_tensor("l_dbg", [16, NTOK], F32, kind="ExternalOutput").ap()
        pt_dbg = nc.dram_tensor("pt_dbg", [128, NTOK], BF16, kind="ExternalOutput").ap()

    with stack:
        p = Prog(nc, stack)
        mm = p.sem("mm")
        ve = p.sem("ve")
        ac = p.sem("ac")
        s_c = p.sem("s_c")
        s_q = p.sem("s_q")
        s_k = [p.sem("s_k0"), p.sem("s_k1")]
        s_v = [p.sem("s_v0"), p.sem("s_v1")]
        s_m = [p.sem("s_m0"), p.sem("s_m1")]
        s_wo = [p.sem("s_wo0"), p.sem("s_wo1")]
        s_xr = [p.sem("s_xr0"), p.sem("s_xr1")]
        s_o = p.sem("s_o")

        consts = p.sbuf("consts", [128, 18], F32)
        onecol = p.sbuf("onecol", [128, 1], BF16)
        onecol_f = p.sbuf("onecol_f", [128, 1], F32)
        onerow = p.sbuf("onerow", [1, 128], F32)
        q_sb = p.sbuf("q_sb", [128, 16 * NTOK], BF16)
        ctx = p.sbuf("ctx", [128, 16 * NTOK], BF16)
        kbuf = [p.sbuf(f"kbuf{i}", [128, 128], BF16) for i in range(2)]
        vbuf = [p.sbuf(f"vbuf{i}", [128, DM], BF16) for i in range(2)]
        mbuf = [p.sbuf(f"mbuf{i}", [128, NTOK], BF16) for i in range(2)]
        pt_sb = p.sbuf("pt_sb", [128, NTOK], BF16)
        linv = p.sbuf("linv", [1, NTOK], F32)
        lbc = p.sbuf("lbc", [128, NTOK], F32)
        r_sb = p.sbuf("r_sb", [128, 16 * NTOK], F32)
        wbuf = [p.sbuf(f"wbuf{i}", [128, 16 * 128], BF16) for i in range(2)]
        xbuf = [p.sbuf(f"xbuf{i}", [128, NTOK], F32) for i in range(2)]
        sq = p.sbuf("sq", [128, 512], F32)
        rows = p.sbuf("rows", [1, 3 * NTOK], F32)
        bca = p.sbuf("bca", [128, NTOK], F32)
        bcb = p.sbuf("bcb", [128, NTOK], F32)
        stgx = p.sbuf("stgx", [128, NTOK], F32)

        ps_st = [p.psum(f"ps_st{i}", [128, NTOK], F32) for i in range(2)]
        ps_o = p.psum("ps_o", [128, NTOK], F32)
        ps_l = p.psum("ps_l", [1, NTOK], F32)

        v_c1 = p.dma("sync", consts[:, 0:16], b_o, s_c)
        p.op("vector", lambda v: v.memset(onecol[:], 1.0))
        p.op("vector", lambda v: v.memset(onecol_f[:], 1.0))
        p.op("vector", lambda v: v.memset(onerow[:], 1.0))
        p.op("vector", lambda v: v.memset(consts[:, 16:17], 1e-5))
        v_ms = p.op("vector", lambda v: v.memset(consts[:, 17:18], -0.5), sem=ve)
        p.wait("tensor", ve, v_ms)
        v_q = p.dma("sync", sb3(q_sb, NTOK), q_in.rearrange("(j p) t -> p j t", p=128), s_q)

        v_kl = {}
        v_vl = {}
        v_ml = {}
        v_smm = {}
        v_pvmm = {}
        v_msk = {}

        def kidx(h, kk):
            return h * NKS + kk

        v_kl[0] = p.dma("sync", kbuf[0][:], k_in[0:128, 0:128], s_k[0])
        v_vl[0] = p.dma("gpsimd", vbuf[0][:], v_in[0:128, :], s_v[0])
        v_ml[0] = p.dma("sync", mbuf[0][:], m_in[0:128, :], s_m[0])

        p.wait("tensor", s_q, v_q)
        v_ctx = {}
        for h in range(16):
            for kk in range(NKS):
                it = kidx(h, kk)
                nit = it + 1
                if nit < 16 * NKS:
                    nh, nkk = divmod(nit, NKS)
                    if it >= 1:
                        p.wait("sync", mm, v_smm[it - 1])
                        p.wait("gpsimd", mm, v_pvmm[it - 1])
                        p.wait("sync", ve, v_msk[it - 1])
                    v_kl[nit] = p.dma("sync", kbuf[nit % 2][:],
                                      k_in[nh * 128:(nh + 1) * 128, nkk * 128:(nkk + 1) * 128], s_k[nit % 2])
                    v_vl[nit] = p.dma("gpsimd", vbuf[nit % 2][:],
                                      v_in[nkk * 128:(nkk + 1) * 128, nh * 128:(nh + 1) * 128].rearrange("a b -> a b") if False else v_in[nkk * 128:(nkk + 1) * 128, :], s_v[nit % 2])
                    v_ml[nit] = p.dma("sync", mbuf[nit % 2][:],
                                      m_in[nkk * 128:(nkk + 1) * 128, :], s_m[nit % 2])
                pb = ps_st[it % 2]
                p.wait("tensor", s_k[it % 2], v_kl[it])
                if it - 2 >= 0:
                    p.wait("tensor", ac, v_exp[it - 2])
                for tg in range(2):
                    def mmop(t, pb=pb, h=h, kk=kk, tg=tg):
                        return t.matmul(pb[:, tg * TB:(tg + 1) * TB], kbuf[kk if False else (h * NKS + kk) % 2][:],
                                        q_sb[:, h * NTOK + tg * TB:h * NTOK + (tg + 1) * TB], start=True, stop=True)
                    if tg == 1:
                        v_smm[it] = p.op("tensor", mmop, sem=mm)
                    else:
                        p.op("tensor", mmop)
                p.wait("scalar", mm, v_smm[it])
                if it - 1 >= 0:
                    p.wait("scalar", ve, v_msk[it - 1])
                v_exp_cur = p.op("scalar", lambda sc, pb=pb: sc.activation(
                    out=pt_sb[:], in_=pb[:], func=AF.Exp), sem=ac)
                if it == 0:
                    v_exp = {}
                v_exp[it] = v_exp_cur
                p.wait("vector", ac, v_exp[it])
                p.wait("vector", s_m[it % 2], v_ml[it])
                v_msk[it] = p.op("vector", lambda v, it=it: v.tensor_tensor(pt_sb[:], pt_sb[:], mbuf[it % 2][:], ALU.mult), sem=ve)
                if dbg and it == 0:
                    p.wait("sync", ve, v_msk[it])
                    p.dma("sync", pt_dbg, pt_sb[:], s_o)
                    p.wait("tensor", s_o, 16)
                p.wait("tensor", ve, v_msk[it])
                p.wait("tensor", s_v[it % 2], v_vl[it])
                for tg in range(2):
                    p.op("tensor", lambda t, tg=tg, kk=kk, h=h: t.matmul(
                        ps_l[:, tg * TB:(tg + 1) * TB], onecol[:], pt_sb[:, tg * TB:(tg + 1) * TB],
                        start=(kk == 0), stop=(kk == NKS - 1)))
                for tg in range(2):
                    def mmop(t, tg=tg, kk=kk, h=h, it=it):
                        return t.matmul(ps_o[:, tg * TB:(tg + 1) * TB],
                                        vbuf[it % 2][:, h * 128:(h + 1) * 128],
                                        pt_sb[:, tg * TB:(tg + 1) * TB],
                                        start=(kk == 0), stop=(kk == NKS - 1))
                    if tg == 1:
                        v_pvmm[it] = p.op("tensor", mmop, sem=mm)
                    else:
                        p.op("tensor", mmop)
            p.wait("vector", mm, v_pvmm[kidx(h, NKS - 1)])
            if dbg:
                vdbg = p.op("vector", lambda v, h=h: v.tensor_copy(lbc[0:1, :], ps_l[:]), sem=ve)
                p.wait("sync", ve, vdbg)
                p.dma("sync", l_dbg[h:h + 1, :], lbc[0:1, :], s_o)
            v_li = p.op("vector", lambda v: v.reciprocal(linv[:], ps_l[:]), sem=ve)
            p.wait("tensor", ve, v_li)
            vb1 = p.op("tensor", lambda t: t.matmul(ps_st[0][:, 0:512], onerow[:], linv[:, 0:512], start=True, stop=True))
            vb2 = p.op("tensor", lambda t: t.matmul(ps_st[0][:, 512:1024], onerow[:], linv[:, 512:1024], start=True, stop=True), sem=mm)
            p.wait("scalar", mm, vb2)
            vlb = p.op("scalar", lambda sc: sc.activation(out=lbc[:], in_=ps_st[0][:], func=AF.Copy), sem=ac)
            p.wait("vector", ac, vlb)
            v_ctx[h] = p.op("vector", lambda v, h=h: v.tensor_tensor(
                ctx[:, h * NTOK:(h + 1) * NTOK], ps_o[:], lbc[:], ALU.mult), sem=ve)
            p.wait("tensor", ac, vlb)
            p.wait("tensor", ve, v_ctx[h])

        def wstripe_ap(do):
            return w_o[:, do * 128:(do + 1) * 128].rearrange("(k p) e -> p k e", p=128)

        ps_s = [ps_st[0], ps_l]
        v_wl = {0: None}
        v_xl = {0: None}
        p.wait("gpsimd", ve, v_ctx[15])
        p.wait("sync", ve, v_ctx[15])
        v_wl[0] = p.dma("gpsimd", sb3(wbuf[0], 128), wstripe_ap(0), s_wo[0])
        v_xl[0] = p.dma("sync", xbuf[0][:], x_res[0:128, :], s_xr[0])
        v_wmm = {}
        v_res = {}
        v_sq = {}
        v_smm2 = {}
        for do in range(16):
            if do + 1 < 16:
                if do >= 1:
                    p.wait("gpsimd", mm, v_wmm[(do - 1, 1)])
                    p.wait("sync", ve, v_res[(do - 1, 1)])
                v_wl[do + 1] = p.dma("gpsimd", sb3(wbuf[(do + 1) % 2], 128), wstripe_ap(do + 1), s_wo[(do + 1) % 2])
                v_xl[do + 1] = p.dma("sync", xbuf[(do + 1) % 2][:], x_res[(do + 1) * 128:(do + 2) * 128, :], s_xr[(do + 1) % 2])
            p.wait("tensor", s_wo[do % 2], v_wl[do])
            for tg in range(2):
                pb = ps_o[:, tg * 512:(tg + 1) * 512]
                if do >= 1:
                    p.wait("tensor", ve, v_res[(do - 1, tg)])
                for k in range(16):
                    def mmop(t, pb=pb, do=do, k=k, tg=tg, s=(k == 0), z=(k == 15)):
                        return t.matmul(pb, wbuf[do % 2][:, k * 128:(k + 1) * 128],
                                        ctx[:, k * NTOK + tg * 512:k * NTOK + tg * 512 + 512], start=s, stop=z)
                    if k == 15:
                        v_wmm[(do, tg)] = p.op("tensor", mmop, sem=mm)
                    else:
                        p.op("tensor", mmop)
                p.wait("vector", mm, v_wmm[(do, tg)])
                p.wait("vector", s_xr[do % 2], v_xl[do])
                v_res[(do, tg)] = p.op("vector", lambda v, pb=pb, do=do, tg=tg: v.scalar_tensor_tensor(
                    out=r_sb[:, do * NTOK + tg * 512:do * NTOK + (tg + 1) * 512], in0=pb,
                    scalar=consts[:, do:do + 1], in1=xbuf[do % 2][:, tg * 512:(tg + 1) * 512],
                    op0=ALU.add, op1=ALU.add), sem=ve)
                p.wait("scalar", ve, v_res[(do, tg)])
                if (do, tg) != (0, 0):
                    p.wait("scalar", mm, v_smm2.get((do * 2 + tg) - 1, 0))
                v_sq[(do, tg)] = p.op("scalar", lambda sc, do=do, tg=tg: sc.activation(
                    out=sq[:], in_=r_sb[:, do * NTOK + tg * 512:do * NTOK + (tg + 1) * 512], func=AF.Square), sem=ac)
                p.wait("tensor", ve, v_res[(do, tg)])
                p.op("tensor", lambda t, do=do, tg=tg, s=(do == 0), z=(do == 15): t.matmul(
                    ps_l[:, tg * 512:(tg + 1) * 512], onecol_f[:],
                    r_sb[:, do * NTOK + tg * 512:do * NTOK + (tg + 1) * 512], start=s, stop=z))
                p.wait("tensor", ac, v_sq[(do, tg)])
                v_smm2[do * 2 + tg] = p.op("tensor", lambda t, do=do, tg=tg, s=(do == 0), z=(do == 15): t.matmul(
                    ps_st[1][0:1, tg * 512:(tg + 1) * 512], onecol_f[:], sq[:], start=s, stop=z), sem=mm)

        R = NTOK
        p.wait("vector", mm, v_smm2[31])
        p.op("vector", lambda v: v.tensor_scalar_mul(rows[:, 0:R], ps_l[:], 1.0 / DM))
        p.op("vector", lambda v: v.tensor_scalar_mul(rows[:, R:2 * R], ps_st[1][0:1, 0:NTOK], 1.0 / DM))
        p.op("vector", lambda v: v.tensor_tensor(rows[:, 2 * R:3 * R], rows[:, 0:R], rows[:, 0:R], ALU.mult))
        v_var = p.op("vector", lambda v: v.tensor_tensor(rows[:, R:2 * R], rows[:, R:2 * R], rows[:, 2 * R:3 * R], ALU.subtract), sem=ve)
        p.wait("scalar", ve, v_var)
        p.op("scalar", lambda sc: sc.activation(out=rows[:, R:2 * R], in_=rows[:, R:2 * R], func=AF.Ln, bias=consts[0:1, 16:17]))
        v_rstd = p.op("scalar", lambda sc: sc.activation(out=rows[:, R:2 * R], in_=rows[:, R:2 * R], func=AF.Exp, scale=consts[0:1, 17:18]), sem=ac)
        p.wait("vector", ac, v_rstd)
        v_mr = p.op("vector", lambda v: v.tensor_tensor(rows[:, 2 * R:3 * R], rows[:, 0:R], rows[:, R:2 * R], ALU.mult), sem=ve)
        p.wait("tensor", ve, v_mr)
        v_bc = {}
        for tgb in range(2):
            if tgb > 0:
                p.wait("tensor", ac, v_bc[tgb - 1])
            v1 = p.op("tensor", lambda t, tgb=tgb: t.matmul(
                ps_o[:, 0:512], onerow[:], rows[:, R + tgb * 512:R + (tgb + 1) * 512], start=True, stop=True), sem=mm)
            p.wait("scalar", mm, v1)
            vba = p.op("scalar", lambda sc, tgb=tgb: sc.activation(out=bca[:, tgb * 512:(tgb + 1) * 512], in_=ps_o[:, 0:512], func=AF.Copy), sem=ac)
            p.wait("tensor", ac, vba)
            v2 = p.op("tensor", lambda t, tgb=tgb: t.matmul(
                ps_o[:, 0:512], onerow[:], rows[:, 2 * R + tgb * 512:2 * R + (tgb + 1) * 512], start=True, stop=True), sem=mm)
            p.wait("scalar", mm, v2)
            v_bc[tgb] = p.op("scalar", lambda sc, tgb=tgb: sc.activation(out=bcb[:, tgb * 512:(tgb + 1) * 512], in_=ps_o[:, 0:512], func=AF.Copy), sem=ac)
        p.wait("vector", ac, v_bc[1])
        for do in range(16):
            if do - 1 >= 0:
                p.wait("vector", s_o, 16 * do)
            p.op("vector", lambda v, do=do: v.tensor_tensor(stgx[:], r_sb[:, do * NTOK:(do + 1) * NTOK], bca[:], ALU.mult))
            vx1 = p.op("vector", lambda v, do=do: v.tensor_tensor(stgx[:], stgx[:], bcb[:], ALU.subtract), sem=ve)
            p.wait("sync", ve, vx1)
            p.dma("sync", x3_out[do * 128:(do + 1) * 128, :], stgx[:], s_o)
        p.wait("sync", s_o, s_o.v)
        p.finalize()
    return nc


def host_inputs_l5(core, I, q_by_core, k_full, v_full, x3res_by_core):
    b0, b1 = blocks_of(core)
    nsub = (max(b0, b1) + 1) * 4
    idx = list(range(nsub)) + [0] * (NKS - nsub)
    assert len(idx) == NKS and nsub <= NKS, (nsub, NKS)
    toks = np.r_[b0 * TB:(b0 + 1) * TB, b1 * TB:(b1 + 1) * TB]
    kperm = np.empty((DM, NKS * 128), dtype=BF)
    vperm = np.empty((NKS * 128, DM), dtype=BF)
    masks = np.zeros((NKS * 128, NTOK), dtype=BF)
    for i, sid in enumerate(idx):
        kperm[:, i * 128:(i + 1) * 128] = k_full[:, sid * 128:(sid + 1) * 128]
        vperm[i * 128:(i + 1) * 128, :] = v_full[sid * 128:(sid + 1) * 128, :]
        if i < nsub:
            kpos = np.arange(sid * 128, (sid + 1) * 128)
            masks[i * 128:(i + 1) * 128, :] = (kpos[:, None] <= toks[None, :]).astype(BF)
    return {
        "q_in": q_by_core[core],
        "k_in": kperm, "v_in": vperm, "m_in": masks,
        "x_res": x3res_by_core[core],
        "w_o": I["L1_attn_W_o"],
        "b_o": plan128(I["L1_attn_b_o"]),
    }


def build_l6a_quant():
    nc = bass.Bass("TRN2", target_bir_lowering=False, debug=False)
    stack = contextlib.ExitStack()
    wgu = nc.dram_tensor("wgu", [2048, 2048], F32, kind="ExternalInput").ap()
    wp = nc.dram_tensor("wp", [256, 8192], F32, kind="ExternalInput").ap()
    qgu = nc.dram_tensor("qgu", [2048, 2048], BF16, kind="ExternalOutput").ap()
    qp = nc.dram_tensor("qp", [256, 8192], BF16, kind="ExternalOutput").ap()

    with stack:
        p = Prog(nc, stack)
        ve = p.sem("ve")
        ac = p.sem("ac")
        s_i = p.sem("s_i")
        s_o = p.sem("s_o")

        w_sb = [p.sbuf(f"w_sb{i}", [128, 2048], F32) for i in range(2)]
        wb = p.sbuf("wb", [128, 2048], BF16)
        ab = p.sbuf("ab", [128, 2048], F32)
        gsum = p.sbuf("gsum", [128, 32], F32)
        gsb = p.sbuf("gsb", [128, 32], BF16)
        grc = p.sbuf("grc", [128, 32], F32)
        tb = p.sbuf("tb", [128, 2048], BF16)
        ta = p.sbuf("ta", [128, 2048], F32)
        msk = p.sbuf("msk", [128, 2048], F32)
        sgn = p.sbuf("sgn", [128, 2048], F32)
        qv = p.sbuf("qv", [128, 2048], F32)
        outb = p.sbuf("outb", [128, 2048], BF16)

        v_ld = {}
        v_st = {}
        v_done = {}
        tiles = [("wgu", wgu, qgu, 16, 2048), ("wp", wp, qp, 2, 8192)]
        it = 0
        for name, src, dst, nt, W in tiles:
            for ti in range(nt):
                pass
        jobs = []
        for ti in range(16):
            jobs.append((wgu, qgu, ti * 128, 0, 2048))
        for ti in range(2):
            for cc in range(4):
                jobs.append((wp, qp, ti * 128, cc * 2048, 2048))
        for jx, (src, dst, r0, c0, W) in enumerate(jobs):
            buf = w_sb[jx % 2]
            if jx >= 2:
                p.wait("sync", ve, v_done[jx - 2])
            v_ld[jx] = p.dma("sync", buf[:], src[r0:r0 + 128, c0:c0 + W], s_i)
            p.wait("vector", s_i, v_ld[jx])
            if jx >= 1:
                p.wait("vector", ve, v_done.get(jx - 1, 0))
            p.op("vector", lambda v, buf=buf: v.tensor_copy(wb[:], buf[:]))
            v_wb = p.op("vector", lambda v: v.tensor_copy(wb[:, 0:1], wb[:, 0:1]), sem=ve)
            p.wait("scalar", ve, v_wb)
            v_ab = p.op("scalar", lambda sc: sc.activation(out=ab[:], in_=wb[:], func=AF.Abs), sem=ac)
            p.wait("vector", ac, v_ab)
            ag = ab[:].rearrange("p (g s) -> p g s", s=64)
            vr1 = p.op("vector", lambda v, ag=ag: v.tensor_reduce(gsum[:], ab[:].rearrange("p (g s) -> p g s", s=64) if False else ag, axis=AX.X, op=ALU.add), sem=ve)
            p.wait("vector", ve, vr1)
            vr2 = p.op("vector", lambda v: v.tensor_scalar_mul(gsum[:], gsum[:], 1.0 / 64.0), sem=ve)
            p.wait("vector", ve, vr2)
            vr3 = p.op("vector", lambda v: v.tensor_copy(gsb[:], gsum[:]), sem=ve)
            p.wait("vector", ve, vr3)
            vr4 = p.op("vector", lambda v: v.tensor_scalar(out=gsum[:], in0=gsb[:], scalar1=1e-8, scalar2=None, op0=ALU.max), sem=ve)
            p.wait("vector", ve, vr4)
            v_rc = p.op("vector", lambda v: v.reciprocal(grc[:], gsum[:]), sem=ve)
            p.wait("vector", ve, v_rc)
            grx = grc[:].rearrange("p g -> p g 1" if False else "p (g o) -> p g o", o=1)
            for g in range(32):
                p.op("vector", lambda v, g=g: v.tensor_scalar(
                    out=tb[:, g * 64:(g + 1) * 64], in0=wb[:, g * 64:(g + 1) * 64],
                    scalar1=grc[:, g:g + 1], scalar2=None, op0=ALU.mult))
            v_tb = p.op("vector", lambda v: v.tensor_copy(tb[:, 0:1], tb[:, 0:1]), sem=ve)
            p.wait("scalar", ve, v_tb)
            p.op("scalar", lambda sc: sc.activation(out=ta[:], in_=tb[:], func=AF.Abs))
            v_sg = p.op("scalar", lambda sc: sc.activation(out=sgn[:], in_=tb[:], func=AF.Sign), sem=ac)
            p.wait("vector", ac, v_sg)
            p.op("vector", lambda v: v.tensor_single_scalar(out=msk[:], in_=ta[:], scalar=0.5, op=ALU.is_gt))
            p.op("vector", lambda v: v.tensor_tensor(qv[:], msk[:], sgn[:], ALU.mult))
            for g in range(32):
                p.op("vector", lambda v, g=g: v.tensor_scalar(
                    out=outb[:, g * 64:(g + 1) * 64], in0=qv[:, g * 64:(g + 1) * 64],
                    scalar1=gsum[:, g:g + 1], scalar2=None, op0=ALU.mult))
            v_done[jx] = p.op("vector", lambda v: v.tensor_copy(outb[:, 0:1], outb[:, 0:1]), sem=ve)
            p.wait("sync", ve, v_done[jx])
            v_st[jx] = p.dma("sync", dst[r0:r0 + 128, c0:c0 + W], outb[:], s_o)
        p.wait("sync", s_o, s_o.v)
        p.finalize()
    return nc


def host_inputs_l6a(core, I):
    return {
        "wgu": np.ascontiguousarray(I["L1_ffn_w_gate_up"][core * 2048:(core + 1) * 2048]),
        "wp": np.ascontiguousarray(I["L1_ffn_w_proj"][core * 256:(core + 1) * 256]),
    }


def build_l6b_ffn1():
    nc = bass.Bass("TRN2", target_bir_lowering=False, debug=False)
    stack = contextlib.ExitStack()

    x_in = nc.dram_tensor("x_in", [DM, NTOK], F32, kind="ExternalInput").ap()
    wguT = nc.dram_tensor("wguT", [DM, 2 * IN8], BF16, kind="ExternalInput").ap()
    wpT = nc.dram_tensor("wpT", [IN8, DM], BF16, kind="ExternalInput").ap()
    out_o = nc.dram_tensor("out_o", [DM, NTOK], F32, kind="ExternalOutput").ap()

    with stack:
        p = Prog(nc, stack)
        mm = p.sem("mm")
        ve = p.sem("ve")
        ac = p.sem("ac")
        s_xb = p.sem("s_xb")
        s_xf = p.sem("s_xf")
        s_wg = [p.sem("s_wg0"), p.sem("s_wg1")]
        s_wu = [p.sem("s_wu0"), p.sem("s_wu1")]
        s_wd = [p.sem("s_wd0"), p.sem("s_wd1")]
        s_o = p.sem("s_o")

        consts = p.sbuf("consts", [128, 3], F32)
        onecol = p.sbuf("onecol", [128, 1], F32)
        onerow = p.sbuf("onerow", [1, 128], F32)
        x1b = p.sbuf("x1b", [128, 16 * TB], BF16)
        fuse = p.sbuf("fuse", [128, 64 * TB], BF16)
        r_sb = p.sbuf("r_sb", [128, 16 * NTOK], F32)
        rmsrow = p.sbuf("rmsrow", [1, NTOK], F32)
        rmsbc = p.sbuf("rmsbc", [128, TB], F32)
        sq = p.sbuf("sq", [128, 512], F32)
        stPH = contextlib.ExitStack()
        EP = stPH.enter_context
        wgbuf = [EP(nc.sbuf_tensor(f"wgbuf{i}", [128, 16 * 128], BF16)) for i in range(2)]
        wubuf = [EP(nc.sbuf_tensor(f"wubuf{i}", [128, 16 * 128], BF16)) for i in range(2)]
        wdbuf = [EP(nc.sbuf_tensor(f"wdbuf{i}", [128, DM], BF16)) for i in range(2)]
        gbuf = [EP(nc.sbuf_tensor(f"gbuf{i}", [128, TB], BF16)) for i in range(2)]
        ubuf = [EP(nc.sbuf_tensor(f"ubuf{i}", [128, TB], BF16)) for i in range(2)]
        xfb = EP(nc.sbuf_tensor("xfb", [128, 16 * 256], F32))

        psD = [p.psum(f"psD{i}", [128, 512], F32) for i in range(8)]

        p.op("vector", lambda v: v.memset(onecol[:], 1.0))
        p.op("vector", lambda v: v.memset(onerow[:], 1.0))
        p.op("vector", lambda v: v.memset(consts[:, 0:1], 1.1920929e-7))
        p.op("vector", lambda v: v.memset(consts[:, 2:3], 1e-5))
        v_ms = p.op("vector", lambda v: v.memset(consts[:, 1:2], -0.5), sem=ve)

        def cstripe(e):
            return wguT[:, e * 128:(e + 1) * 128].rearrange("(k p) e -> p k e", p=128)

        def ustripe(e):
            return wguT[:, IN8 + e * 128:IN8 + (e + 1) * 128].rearrange("(k p) e -> p k e", p=128)

        v_wgl = {}
        v_wul = {}
        v_gmm = {}
        v_umm = {}
        v_fu = {}
        v_wgl[(0, 0)] = p.dma("sync", sb3(wgbuf[0], 128), cstripe(0), s_wg[0])
        v_wul[(0, 0)] = p.dma("sync", sb3(wubuf[0], 128), ustripe(0), s_wu[0])
        p.wait("tensor", ve, v_ms)

        v_res = {}
        v_dn_done = None
        v_sqs = {}
        for tg in range(2):
            t0 = tg * TB
            if tg == 1:
                p.wait("gpsimd", mm, v_umm[(0, 63)])
            v_xb = p.dma("gpsimd", sb3(x1b, TB), x_in[:, t0:t0 + TB].rearrange("(j p) t -> p j t", p=128), s_xb)
            p.wait("tensor", s_xb, v_xb)
            for e in range(64):
                key = (tg, e)
                nkey = (tg, e + 1) if e + 1 < 64 else ((tg + 1, 0) if tg == 0 else None)
                if nkey is not None:
                    if (tg, e) != (0, 0):
                        p.wait("sync", mm, v_umm[pkey])
                    v_wgl[nkey] = p.dma("sync", sb3(wgbuf[(e + 1) % 2], 128),
                                        cstripe(nkey[1]) if nkey[0] == tg else cstripe(0), s_wg[(e + 1) % 2])
                    v_wul[nkey] = p.dma("sync", sb3(wubuf[(e + 1) % 2], 128),
                                        ustripe(nkey[1]) if nkey[0] == tg else ustripe(0), s_wu[(e + 1) % 2])
                p.wait("tensor", s_wg[e % 2], v_wgl[key])
                p.wait("tensor", s_wu[e % 2], v_wul[key])
                if (tg, e) != (0, 0):
                    p.wait("tensor", ac, v_ev_prev)
                if tg == 1 and e == 0:
                    p.wait("tensor", ve, v_dn_done)
                for k in range(16):
                    def mmop(t, e=e, k=k, s=(k == 0), z=(k == 15)):
                        return t.matmul(psD[0][:], wgbuf[e % 2][:, k * 128:(k + 1) * 128],
                                        x1b[:, k * TB:(k + 1) * TB], start=s, stop=z)
                    if k == 15:
                        v_gmm[key] = p.op("tensor", mmop, sem=mm)
                    else:
                        p.op("tensor", mmop)
                for k in range(16):
                    def mmop(t, e=e, k=k, s=(k == 0), z=(k == 15)):
                        return t.matmul(psD[1][:], wubuf[e % 2][:, k * 128:(k + 1) * 128],
                                        x1b[:, k * TB:(k + 1) * TB], start=s, stop=z)
                    if k == 15:
                        v_umm[key] = p.op("tensor", mmop, sem=mm)
                    else:
                        p.op("tensor", mmop)
                pkey = key
                p.wait("scalar", mm, v_gmm[key])
                if (tg, e) != (0, 0):
                    p.wait("scalar", ve, v_fu[pfkey])
                p.op("scalar", lambda sc, e=e: sc.activation(out=gbuf[e % 2][:], in_=psD[0][:], func=AF.Silu))
                p.wait("scalar", mm, v_umm[key])
                v_ev_prev = p.op("scalar", lambda sc, e=e: sc.activation(out=ubuf[e % 2][:], in_=psD[1][:], func=AF.Copy), sem=ac)
                p.wait("vector", ac, v_ev_prev)
                v_fu[key] = p.op("vector", lambda v, e=e: v.tensor_tensor(
                    planar(fuse, e, TB), gbuf[e % 2][:], ubuf[e % 2][:], ALU.mult), sem=ve)
                pfkey = key
                p.wait("scalar", ve, v_fu[key])
                if (tg, e) != (0, 0):
                    p.wait("scalar", mm, v_sqs[pskey])
                v_sq = p.op("scalar", lambda sc, e=e: sc.activation(out=sq[:], in_=planar(fuse, e, TB), func=AF.Square), sem=ac)
                p.wait("tensor", ac, v_sq)
                v_sqs[key] = p.op("tensor", lambda t, e=e, s=(e == 0), z=(e == 63): t.matmul(
                    psD[7][0:1, :], onecol[:], sq[:], start=s, stop=z), sem=mm)
                pskey = key
            p.wait("vector", mm, v_sqs[(tg, 63)])
            p.op("vector", lambda v, t0=t0: v.tensor_scalar_mul(rmsrow[:, t0:t0 + TB], psD[7][0:1, :], 1.0 / IN8))
            v_rr = p.op("vector", lambda v, t0=t0: v.tensor_copy(rmsrow[:, t0:t0 + 1], rmsrow[:, t0:t0 + 1]), sem=ve)
            p.wait("scalar", ve, v_rr)
            p.op("scalar", lambda sc, t0=t0: sc.activation(out=rmsrow[:, t0:t0 + TB], in_=rmsrow[:, t0:t0 + TB], func=AF.Ln, bias=consts[0:1, 0:1]))
            v_rstd = p.op("scalar", lambda sc, t0=t0: sc.activation(out=rmsrow[:, t0:t0 + TB], in_=rmsrow[:, t0:t0 + TB], func=AF.Exp, scale=consts[0:1, 1:2]), sem=ac)
            p.wait("tensor", ac, v_rstd)
            vb1 = p.op("tensor", lambda t, t0=t0: t.matmul(psD[2][:], onerow[:], rmsrow[:, t0:t0 + TB], start=True, stop=True), sem=mm)
            p.wait("scalar", mm, vb1)
            v_rbc = p.op("scalar", lambda sc: sc.activation(out=rmsbc[:], in_=psD[2][:], func=AF.Copy), sem=ac)
            p.wait("vector", ac, v_rbc)
            v_hn = {}
            for e in range(64):
                v_hn[e] = p.op("vector", lambda v, e=e: v.tensor_tensor(
                    planar(fuse, e, TB), planar(fuse, e, TB), rmsbc[:], ALU.mult), sem=ve)

            for dmh in range(2):
                if not (tg == 0 and dmh == 0):
                    p.wait("sync", ve, v_res_last)
                v_x = p.dma("sync", sb3(xfb, TB),
                            x_in[dmh * 1024:(dmh + 1) * 1024, t0:t0 + TB].rearrange("(j p) t -> p j t", p=128), s_xf)
                v_wdl = {}
                p.wait("gpsimd", ve, v_hn[63])
                if not (tg == 0 and dmh == 0):
                    p.wait("gpsimd", ve, v_res_last)
                v_wdl[0] = p.dma("gpsimd", wdbuf[0][:], wpT[0:128, :], s_wd[0])
                v_dmm = {}
                for e in range(64):
                    if e + 1 < 64:
                        if e >= 1:
                            p.wait("gpsimd", mm, v_dmm[e - 1])
                        v_wdl[e + 1] = p.dma("gpsimd", wdbuf[(e + 1) % 2][:], wpT[(e + 1) * 128:(e + 2) * 128, :], s_wd[(e + 1) % 2])
                    p.wait("tensor", s_wd[e % 2], v_wdl[e])
                    if e == 0:
                        p.wait("tensor", ve, v_hn[63])
                        if not (tg == 0 and dmh == 0):
                            p.wait("tensor", ve, v_res_last)
                    for d8 in range(8):
                        def mmop(t, e=e, d8=d8, dmh=dmh, s=(e == 0), z=(e == 63)):
                            return t.matmul(psD[d8][:],
                                            wdbuf[e % 2][:, (dmh * 8 + d8) * 128:(dmh * 8 + d8 + 1) * 128],
                                            fuse[:, e * TB:(e + 1) * TB], start=s, stop=z)
                        if d8 == 7:
                            v_dmm[e] = p.op("tensor", mmop, sem=mm)
                        else:
                            p.op("tensor", mmop)
                p.wait("vector", mm, v_dmm[63])
                p.wait("vector", s_xf, v_x)
                ic = 2 * tg + dmh
                for d8 in range(8):
                    dm = dmh * 8 + d8
                    v_res[(ic, d8)] = p.op("vector", lambda v, dm=dm, d8=d8, t0=t0: v.tensor_tensor(
                        r_sb[:, dm * NTOK + t0:dm * NTOK + t0 + TB],
                        psD[d8][:],
                        planar(xfb, d8, TB), ALU.add), sem=ve)
                v_res_last = v_res[(ic, 7)]
            v_dn_done = v_res_last

        stPH.close()
        stT = contextlib.ExitStack()
        ET = stT.enter_context
        sqb = ET(nc.sbuf_tensor("sqb", [128, 512], F32))
        rows = ET(nc.sbuf_tensor("rows", [1, 3 * NTOK], F32))
        bca = ET(nc.sbuf_tensor("bca", [128, NTOK], F32))
        bcb = ET(nc.sbuf_tensor("bcb", [128, NTOK], F32))
        stgx = ET(nc.sbuf_tensor("stgx", [128, NTOK], F32))
        R = NTOK
        v_sqp = {}
        for dm in range(16):
            p.wait("scalar", ve, v_res[(3, 7)])
            if dm >= 1:
                p.wait("scalar", mm, v_sqp[dm - 1])
            v_sq2 = p.op("scalar", lambda sc, dm=dm: sc.activation(
                out=sq[:], in_=r_sb[:, dm * NTOK:dm * NTOK + 512], func=AF.Square), sem=ac)
            v_sq3 = p.op("scalar", lambda sc, dm=dm: sc.activation(
                out=sqb[:], in_=r_sb[:, dm * NTOK + 512:dm * NTOK + NTOK], func=AF.Square), sem=ac)
            if dm == 0:
                p.wait("tensor", ve, v_res[(3, 7)])
            for tgb in range(2):
                p.op("tensor", lambda t, dm=dm, tgb=tgb, s=(dm == 0), z=(dm == 15): t.matmul(
                    psD[tgb][0:1, :], onecol[:], r_sb[:, dm * NTOK + tgb * 512:dm * NTOK + (tgb + 1) * 512],
                    start=s, stop=z))
            p.wait("tensor", ac, v_sq3)
            p.op("tensor", lambda t, dm=dm, s=(dm == 0), z=(dm == 15): t.matmul(
                psD[2][0:1, :], onecol[:], sq[:], start=s, stop=z))
            v_sqp[dm] = p.op("tensor", lambda t, dm=dm, s=(dm == 0), z=(dm == 15): t.matmul(
                psD[3][0:1, :], onecol[:], sqb[:], start=s, stop=z), sem=mm)
        v_stats = v_sqp[15]
        p.wait("vector", mm, v_stats)
        p.op("vector", lambda v: v.tensor_copy(rows[:, 0:512], psD[0][0:1, :]))
        p.op("vector", lambda v: v.tensor_copy(rows[:, 512:1024], psD[1][0:1, :]))
        p.op("vector", lambda v: v.tensor_copy(rows[:, R:R + 512], psD[2][0:1, :]))
        p.op("vector", lambda v: v.tensor_copy(rows[:, R + 512:R + 1024], psD[3][0:1, :]))
        p.op("vector", lambda v: v.tensor_scalar_mul(rows[:, 0:R], rows[:, 0:R], 1.0 / DM))
        p.op("vector", lambda v: v.tensor_scalar_mul(rows[:, R:2 * R], rows[:, R:2 * R], 1.0 / DM))
        p.op("vector", lambda v: v.tensor_tensor(rows[:, 2 * R:3 * R], rows[:, 0:R], rows[:, 0:R], ALU.mult))
        v_var = p.op("vector", lambda v: v.tensor_tensor(rows[:, R:2 * R], rows[:, R:2 * R], rows[:, 2 * R:3 * R], ALU.subtract), sem=ve)
        p.wait("scalar", ve, v_var)
        p.op("scalar", lambda sc: sc.activation(out=rows[:, R:2 * R], in_=rows[:, R:2 * R], func=AF.Ln, bias=consts[0:1, 2:3]))
        v_rstd2 = p.op("scalar", lambda sc: sc.activation(out=rows[:, R:2 * R], in_=rows[:, R:2 * R], func=AF.Exp, scale=consts[0:1, 1:2]), sem=ac)
        p.wait("vector", ac, v_rstd2)
        v_mr = p.op("vector", lambda v: v.tensor_tensor(rows[:, 2 * R:3 * R], rows[:, 0:R], rows[:, R:2 * R], ALU.mult), sem=ve)
        p.wait("tensor", ve, v_mr)
        v_bc = {}
        for tgb in range(2):
            if tgb > 0:
                p.wait("tensor", ac, v_bc[tgb - 1])
            v1 = p.op("tensor", lambda t, tgb=tgb: t.matmul(
                psD[4][:], onerow[:], rows[:, R + tgb * 512:R + (tgb + 1) * 512], start=True, stop=True), sem=mm)
            p.wait("scalar", mm, v1)
            vba = p.op("scalar", lambda sc, tgb=tgb: sc.activation(out=bca[:, tgb * 512:(tgb + 1) * 512], in_=psD[4][:], func=AF.Copy), sem=ac)
            p.wait("tensor", ac, vba)
            v2 = p.op("tensor", lambda t, tgb=tgb: t.matmul(
                psD[4][:], onerow[:], rows[:, 2 * R + tgb * 512:2 * R + (tgb + 1) * 512], start=True, stop=True), sem=mm)
            p.wait("scalar", mm, v2)
            v_bc[tgb] = p.op("scalar", lambda sc, tgb=tgb: sc.activation(out=bcb[:, tgb * 512:(tgb + 1) * 512], in_=psD[4][:], func=AF.Copy), sem=ac)
        p.wait("vector", ac, v_bc[1])
        for do in range(16):
            if do - 1 >= 0:
                p.wait("vector", s_o, 16 * do)
            p.op("vector", lambda v, do=do: v.tensor_tensor(stgx[:], r_sb[:, do * NTOK:(do + 1) * NTOK], bca[:], ALU.mult))
            vx1 = p.op("vector", lambda v, do=do: v.tensor_tensor(stgx[:], stgx[:], bcb[:], ALU.subtract), sem=ve)
            p.wait("sync", ve, vx1)
            p.dma("sync", out_o[do * 128:(do + 1) * 128, :], stgx[:], s_o)
        p.wait("sync", s_o, s_o.v)
        stT.close()
        p.finalize()
    return nc


def host_inputs_l6b(core, I, x3_by_core, wq_gu, wq_p):
    return {
        "x_in": x3_by_core[core],
        "wguT": np.ascontiguousarray(wq_gu.T),
        "wpT": np.ascontiguousarray(wq_p.T),
    }


def _gather_tokens(per_core, width_axis):
    rows = per_core[0].shape[0]
    out = np.empty((rows, L), dtype=per_core[0].dtype)
    for c in range(NCORES):
        b0, b1 = blocks_of(c)
        out[:, b0 * TB:(b0 + 1) * TB] = per_core[c][:, 0:TB]
        out[:, b1 * TB:(b1 + 1) * TB] = per_core[c][:, TB:2 * TB]
    return out


def kernel(**I):
    cores = list(range(NCORES))

    nc = build_l1_mamba_main()
    r1 = run_bass_kernel_spmd(nc, [host_inputs_l1(c, I) for c in cores], core_ids=cores)
    l1o = r1.results

    sums_all = host_sums_all(l1o)
    nc = build_l2_mamba_post()
    r2 = run_bass_kernel_spmd(nc, [host_inputs_l2(c, I, l1o, sums_all) for c in cores], core_ids=cores)
    x1_by_core = {c: r2.results[c]["x1_out"] for c in cores}

    nc = build_l3_ffn0()
    r3 = run_bass_kernel_spmd(nc, [host_inputs_l3(c, I, x1_by_core) for c in cores], core_ids=cores)
    x2_by_core = {c: r3.results[c]["x2_out"] for c in cores}

    nc = build_l4_qkv()
    r4 = run_bass_kernel_spmd(nc, [host_inputs_l4(c, I, x2_by_core) for c in cores], core_ids=cores)
    q_by_core = {c: r4.results[c]["q_out"] for c in cores}
    k_full = _gather_tokens([r4.results[c]["k_out"] for c in cores], 1)
    v_full = _gather_tokens([np.ascontiguousarray(r4.results[c]["v_out"].T) for c in cores], 1).T
    v_full = np.ascontiguousarray(v_full)

    nc = build_l6a_quant()
    r6a = run_bass_kernel_spmd(nc, [host_inputs_l6a(c, I) for c in cores], core_ids=cores)
    wq_gu = np.concatenate([np.asarray(r6a.results[c]["qgu"]) for c in cores], axis=0)
    wq_p = np.concatenate([np.asarray(r6a.results[c]["qp"]) for c in cores], axis=0)

    nc = build_l5_attn()
    r5 = run_bass_kernel_spmd(nc, [host_inputs_l5(c, I, q_by_core, k_full, v_full, x2_by_core) for c in cores], core_ids=cores)
    x3_by_core = {c: r5.results[c]["x3_out"] for c in cores}

    nc = build_l6b_ffn1()
    r6 = run_bass_kernel_spmd(nc, [host_inputs_l6b(c, I, x3_by_core, wq_gu, wq_p) for c in cores], core_ids=cores)

    out = _gather_tokens([r6.results[c]["out_o"] for c in cores], 1)
    return np.ascontiguousarray(out.T).reshape(1, L, DM).astype(np.float32)


# version 05
# speedup vs baseline: 1.0148x; 1.0148x over previous
import contextlib
import numpy as np
import ml_dtypes
import concourse.bass as bass
import concourse.mybir as mybir
from concourse.bass_utils import run_bass_kernel_spmd

F32 = mybir.dt.float32
BF16 = mybir.dt.bfloat16
I8 = mybir.dt.int8
AF = mybir.ActivationFunctionType
ALU = mybir.AluOpType
AX = mybir.AxisListType
BF = ml_dtypes.bfloat16

NCORES = 8
NBLK = 16
TB = 512
L = 8192
DM = 2048
DI = 4096
DS = 16
DTR = 128

ENGINES = ("sync", "scalar", "vector", "tensor", "gpsimd")


def blocks_of(core):
    return (core, NBLK - 1 - core)


class SemCtr:
    def __init__(self, h):
        self.h = h
        self.v = 0

    def inc(self, n=1):
        self.v += n
        return self.v


class Prog:
    def __init__(self, nc, stack):
        self.nc = nc
        self.stack = stack
        self.sem_stack = stack
        self.pre = ""
        self.ops = {e: [] for e in ENGINES}

    def sem(self, name=None):
        return SemCtr(self.sem_stack.enter_context(self.nc.semaphore(self.pre + (name or "s"))))

    def sbuf(self, name, shape, dtype):
        return self.stack.enter_context(self.nc.sbuf_tensor(self.pre + name, shape, dtype))

    def psum(self, name, shape, dtype=F32):
        return self.stack.enter_context(self.nc.psum_tensor(self.pre + name, shape, dtype))

    def dma(self, eng, out, in_, sem):
        tgt = sem.inc(16)
        def f(e, out=out, in_=in_, h=sem.h):
            e.dma_start(out=out, in_=in_).then_inc(h, 16)
        self.ops[eng].append(f)
        return tgt

    def wait(self, eng, sem, v):
        if v is None or v <= 0:
            return
        self.ops[eng].append(lambda e, h=sem.h, v=v: e.wait_ge(h, v))

    def op(self, eng, fn, sem=None, n=1):
        if sem is None:
            self.ops[eng].append(lambda e, fn=fn: fn(e))
            return None
        tgt = sem.inc(n)
        def f(e, fn=fn, h=sem.h, n=n):
            fn(e).then_inc(h, n)
        self.ops[eng].append(f)
        return tgt

    def fence(self):
        f = self.sem(f"F{len(self.ops['sync'])}")
        for e in ENGINES:
            self.ops[e].append(lambda eng, h=f.h: eng.sem_inc(h, 1))
        for e in ENGINES:
            self.ops[e].append(lambda eng, h=f.h: eng.wait_ge(h, len(ENGINES)))

    def finalize(self):
        def runner(ops):
            def go(eng):
                for f in ops:
                    f(eng)
            return go
        with self.nc.Block() as block:
            for e in ENGINES:
                getattr(block, e)(runner(self.ops[e]))


def planar(sb, j, T):
    return sb[:, j * T:(j + 1) * T]


def sb3(sb, T):
    return sb[:].rearrange("p (j t) -> p j t", t=T)


def host_onehot_bc():
    oh = np.zeros((32, 32 * 128), dtype=BF)
    for n in range(32):
        oh[n, n * 128:(n + 1) * 128] = 1
    return oh


def declare_l1(nc, internal=(), alias=None):
    TX = 515; NT = 2 * TX
    D = {}
    alias = alias or {}
    if "x" in alias:
        D["x"] = alias["x"]
    else:
        k = "Internal" if "x" in internal else "ExternalInput"
        D["x"] = nc.dram_tensor("l1_x", [DM, NT], F32, kind=k).ap()
    if "w_inT" in alias:
        D["w_inT"] = alias["w_inT"]
    else:
        k = "Internal" if "w_inT" in internal else "ExternalInput"
        D["w_inT"] = nc.dram_tensor("l1_w_inT", [DM, 2 * DI], F32, kind=k).ap()
    if "conv_w" in alias:
        D["conv_w"] = alias["conv_w"]
    else:
        k = "Internal" if "conv_w" in internal else "ExternalInput"
        D["conv_w"] = nc.dram_tensor("l1_conv_w", [128, 128], F32, kind=k).ap()
    if "conv_b" in alias:
        D["conv_b"] = alias["conv_b"]
    else:
        k = "Internal" if "conv_b" in internal else "ExternalInput"
        D["conv_b"] = nc.dram_tensor("l1_conv_b", [128, 32], F32, kind=k).ap()
    if "w_xT" in alias:
        D["w_xT"] = alias["w_xT"]
    else:
        k = "Internal" if "w_xT" in internal else "ExternalInput"
        D["w_xT"] = nc.dram_tensor("l1_w_xT", [DI, 160], F32, kind=k).ap()
    if "w_dtT" in alias:
        D["w_dtT"] = alias["w_dtT"]
    else:
        k = "Internal" if "w_dtT" in internal else "ExternalInput"
        D["w_dtT"] = nc.dram_tensor("l1_w_dtT", [DTR, DI], F32, kind=k).ap()
    if "dt_bias" in alias:
        D["dt_bias"] = alias["dt_bias"]
    else:
        k = "Internal" if "dt_bias" in internal else "ExternalInput"
        D["dt_bias"] = nc.dram_tensor("l1_dt_bias", [128, 32], F32, kind=k).ap()
    if "a_log" in alias:
        D["a_log"] = alias["a_log"]
    else:
        k = "Internal" if "a_log" in internal else "ExternalInput"
        D["a_log"] = nc.dram_tensor("l1_a_log", [128, 512], F32, kind=k).ap()
    if "onehot" in alias:
        D["onehot"] = alias["onehot"]
    else:
        k = "Internal" if "onehot" in internal else "ExternalInput"
        D["onehot"] = nc.dram_tensor("l1_onehot", [32, 32 * 128], BF16, kind=k).ap()
    if "u_out" in alias:
        D["u_out"] = alias["u_out"]
    else:
        k = "Internal" if "u_out" in internal else "ExternalOutput"
        D["u_out"] = nc.dram_tensor("l1_u_out", [DI, 2 * TB], BF16, kind=k).ap()
    if "z_out" in alias:
        D["z_out"] = alias["z_out"]
    else:
        k = "Internal" if "z_out" in internal else "ExternalOutput"
        D["z_out"] = nc.dram_tensor("l1_z_out", [DI, 2 * TB], BF16, kind=k).ap()
    if "d_out" in alias:
        D["d_out"] = alias["d_out"]
    else:
        k = "Internal" if "d_out" in internal else "ExternalOutput"
        D["d_out"] = nc.dram_tensor("l1_d_out", [DI, 2 * TB], BF16, kind=k).ap()
    if "y0_out" in alias:
        D["y0_out"] = alias["y0_out"]
    else:
        k = "Internal" if "y0_out" in internal else "ExternalOutput"
        D["y0_out"] = nc.dram_tensor("l1_y0_out", [DI, 2 * TB], F32, kind=k).ap()
    if "bc_out" in alias:
        D["bc_out"] = alias["bc_out"]
    else:
        k = "Internal" if "bc_out" in internal else "ExternalOutput"
        D["bc_out"] = nc.dram_tensor("l1_bc_out", [32, 2 * TB], F32, kind=k).ap()
    if "sum_out" in alias:
        D["sum_out"] = alias["sum_out"]
    else:
        k = "Internal" if "sum_out" in internal else "ExternalOutput"
        D["sum_out"] = nc.dram_tensor("l1_sum_out", [DI, 64], F32, kind=k).ap()
    if "xc_scr" in alias:
        D["xc_scr"] = alias["xc_scr"]
    else:
        k = "Internal" if "xc_scr" in internal else "ExternalOutput"
        D["xc_scr"] = nc.dram_tensor("l1_xc_scr", [DI, NT], BF16, kind=k).ap()
    return D


def emit_l1(nc, p, D, stop_after=None):
    stack = contextlib.ExitStack()
    _oldstack = p.stack
    p.stack = stack

    TX = 515
    NT = 2 * TX
    TH = 256

    x_in = D["x"]
    w_inT = D["w_inT"]
    conv_w = D["conv_w"]
    conv_b = D["conv_b"]
    w_xT = D["w_xT"]
    w_dtT = D["w_dtT"]
    dt_bias = D["dt_bias"]
    a_log = D["a_log"]
    onehot = D["onehot"]

    u_out = D["u_out"]
    z_out = D["z_out"]
    d_out = D["d_out"]
    y0_out = D["y0_out"]
    bc_out = D["bc_out"]
    sum_out = D["sum_out"]

    xc_scr = D["xc_scr"]

    with stack:
        pass
        mm = p.sem("mm")
        ve = p.sem("ve")
        ac = p.sem("ac")
        s_c = p.sem("s_c")
        s_al = p.sem("s_al")
        s_oh = p.sem("s_oh")
        s_wx = p.sem("s_wx")
        s_wdt = p.sem("s_wdt")
        s_x = p.sem("s_x")
        s_w = [p.sem("s_w0"), p.sem("s_w1")]
        s_stg = [p.sem("s_stg0"), p.sem("s_stg1")]
        s_xc = p.sem("s_xc")
        s_y0 = p.sem("s_y0")
        s_big = p.sem("s_big")

        consts = p.sbuf("consts", [128, 704], F32)
        onehot_sb = p.sbuf("onehot_sb", [32, 32 * 128], BF16)
        wx_sb = p.sbuf("wx_sb", [128, 32 * 160], BF16)
        wdt_sb = p.sbuf("wdt_sb", [128, 32 * 128], BF16)
        summ = p.sbuf("summ", [128, 32 * 64], F32)

        ps = [p.psum(f"ps{i}", [128, 512], F32) for i in range(2)]
        ps_bc = p.psum("ps_bc", [32, TB], F32)
        ps_rep = [p.psum(f"ps_rep{i}", [128, TH], F32) for i in range(2)]

        def Acol(j, n):
            return consts[:, 192 + j * DS + n:193 + j * DS + n]

        p.dma("sync", consts[:, 0:128], conv_w, s_c)
        p.dma("sync", consts[:, 128:160], conv_b, s_c)
        v_c = p.dma("sync", consts[:, 160:192], dt_bias, s_c)
        v_al = p.dma("sync", consts[:, 192:704], a_log, s_al)
        v_oh = p.dma("sync", onehot_sb[:], onehot, s_oh)
        v_wx = p.dma("gpsimd", wx_sb[:].rearrange("p (j e) -> p j e", e=160),
                     w_xT.rearrange("(j p) e -> p j e", p=128), s_wx)
        v_wdt = p.dma("gpsimd", wdt_sb[:].rearrange("p (j e) -> p j e", e=128),
                      w_dtT.rearrange("p (j e) -> p j e", e=128), s_wdt)
        p.wait("scalar", s_al, v_al)
        v_expA = p.op("scalar", lambda e: e.activation(out=consts[:, 192:704], in_=consts[:, 192:704], func=AF.Exp), sem=ac)
        p.wait("vector", ac, v_expA)
        p.op("vector", lambda e: e.tensor_scalar_mul(consts[:, 192:704], consts[:, 192:704], -1.0), sem=ve)

        stackA = contextlib.ExitStack()
        x_bf = stackA.enter_context(nc.sbuf_tensor(f"{p.pre}x_bf", [128, 16 * TX], BF16))
        wbuf = [stackA.enter_context(nc.sbuf_tensor(f"{p.pre}wbuf{i}", [128, 16 * 128], BF16)) for i in range(2)]
        stg = [stackA.enter_context(nc.sbuf_tensor(f"{p.pre}stg{i}", [128, 258], BF16)) for i in range(2)]

        GRP = [(0, 258), (258, 257)]

        def stripe_ap(e):
            return w_inT[:, e * 128:(e + 1) * 128].rearrange("(k p) e -> p k e", p=128)

        v_evac = {}
        v_mm_stripe = {}
        git = 0
        for half in range(2):
            if half == 1:
                p.wait("gpsimd", mm, v_mm_stripe[(0, 63)])
            v_x = p.dma("gpsimd", x_bf[:].rearrange("p (j t) -> p j t", t=TX),
                        x_in[:, half * TX:(half + 1) * TX].rearrange("(j p) t -> p j t", p=128), s_x)
            v_wq = {0: p.dma("gpsimd", wbuf[0][:].rearrange("p (k e) -> p k e", e=128), stripe_ap(0), s_w[0])}
            p.wait("tensor", s_x, v_x)
            for e in range(64):
                buf = wbuf[e % 2]
                if e + 1 < 64:
                    if e >= 1:
                        p.wait("gpsimd", mm, v_mm_stripe[(half, e - 1)])
                    v_wq[e + 1] = p.dma("gpsimd", wbuf[(e + 1) % 2][:].rearrange("p (k e) -> p k e", e=128),
                                        stripe_ap(e + 1), s_w[(e + 1) % 2])
                p.wait("tensor", s_w[e % 2], v_wq[e])
                for gi, (c0, cn) in enumerate(GRP):
                    pb = ps[git % 2]
                    if git - 2 >= 0:
                        p.wait("tensor", ac, v_evac[git - 2])
                    for k in range(16):
                        def mmop(t, pb=pb, buf=buf, k=k, c0=c0, cn=cn, s=(k == 0), z=(k == 15)):
                            return t.matmul(pb[:, 0:cn], buf[:, k * 128:(k + 1) * 128],
                                            x_bf[:, k * TX + c0:k * TX + c0 + cn], start=s, stop=z)
                        if k == 15:
                            v_mm_stripe[(half, e)] = p.op("tensor", mmop, sem=mm)
                        else:
                            p.op("tensor", mmop)
                    sg = stg[git % 2]
                    sgsem = s_stg[git % 2]
                    p.wait("scalar", mm, v_mm_stripe[(half, e)])
                    p.wait("scalar", sgsem, sgsem.v)
                    v_evac[git] = p.op("scalar", lambda sc, pb=pb, sg=sg, cn=cn:
                                       sc.activation(out=sg[:, 0:cn], in_=pb[:, 0:cn], func=AF.Copy), sem=ac)
                    p.wait("sync", ac, v_evac[git])
                    if e < 32:
                        p.dma("sync", xc_scr[e * 128:(e + 1) * 128, half * TX + c0:half * TX + c0 + cn],
                              sg[:, 0:cn], sgsem)
                    else:
                        ecol = e - 32
                        a0, a1 = max(c0, 3), c0 + cn
                        if a0 < a1:
                            zcol = half * TB + (a0 - 3)
                            p.dma("sync", z_out[ecol * 128:(ecol + 1) * 128, zcol:zcol + (a1 - a0)],
                                  sg[:, a0 - c0:a1 - c0], sgsem)
                    git += 1
        stackA.close()

        if stop_after == "A":
            p.wait("sync", s_stg[0], s_stg[0].v)
            p.wait("sync", s_stg[1], s_stg[1].v)
            p.stack = _oldstack
            return

        stackB = contextlib.ExitStack()
        E = stackB.enter_context
        u_sb = E(nc.sbuf_tensor(f"{p.pre}u_sb", [128, 32 * TB], BF16))
        del_sb = E(nc.sbuf_tensor(f"{p.pre}del_sb", [128, 32 * TB], BF16))
        xcdu_sb = E(nc.sbuf_tensor(f"{p.pre}xcdu_sb", [128, 32 * TX], BF16))
        y0_sb = E(nc.sbuf_tensor(f"{p.pre}y0_sb", [128, 32 * TH], F32))
        cscr = y0_sb
        carry = E(nc.sbuf_tensor(f"{p.pre}carry", [128, 32 * DS], F32))
        dtr_sb = E(nc.sbuf_tensor(f"{p.pre}dtr_sb", [128, TB], BF16))
        bc_f = E(nc.sbuf_tensor(f"{p.pre}bc_f", [32, TB], F32))
        bc_b = E(nc.sbuf_tensor(f"{p.pre}bc_b", [32, TB], BF16))
        sumd = E(nc.sbuf_tensor(f"{p.pre}sumd", [128, 32], F32))
        brep = [E(nc.sbuf_tensor(f"{p.pre}brep{i}", [128, TH], BF16)) for i in range(2)]
        crep = [E(nc.sbuf_tensor(f"{p.pre}crep{i}", [128, TH], BF16)) for i in range(2)]
        at = [E(nc.sbuf_tensor(f"{p.pre}at{i}", [128, TH], F32)) for i in range(2)]
        bt = [E(nc.sbuf_tensor(f"{p.pre}bt{i}", [128, TH], BF16)) for i in range(2)]
        sst = [E(nc.sbuf_tensor(f"{p.pre}sst{i}", [128, TH], F32)) for i in range(2)]
        sct = [E(nc.sbuf_tensor(f"{p.pre}sct{i}", [128, TH], F32)) for i in range(2)]
        spa = [E(nc.sbuf_tensor(f"{p.pre}spa{i}", [128, TB], F32)) for i in range(2)]
        spb = [E(nc.sbuf_tensor(f"{p.pre}spb{i}", [128, TB], F32)) for i in range(2)]

        def xc(j, off):
            return xcdu_sb[:, j * TX + off:j * TX + off + TH]

        def du(j, h):
            return xcdu_sb[:, j * TB + h * TH:j * TB + (h + 1) * TH]

        def du_full(j):
            return xcdu_sb[:, j * TB:(j + 1) * TB]

        v_blk_prev_scan = None
        ps_dt = ps
        for bi in range(2):
            tb0 = bi * TB
            p.wait("sync", s_stg[0], s_stg[0].v)
            p.wait("sync", s_stg[1], s_stg[1].v)
            if bi == 1:
                p.wait("sync", ve, v_blk_prev_scan)
                p.wait("sync", s_big, s_big.v)
            v_xc = p.dma("sync", xcdu_sb[:].rearrange("p (j t) -> p j t", t=TX),
                         xc_scr[:, bi * TX:(bi + 1) * TX].rearrange("(j p) t -> p j t", p=128), s_xc)

            p.wait("vector", s_xc, v_xc)
            p.wait("vector", s_c, v_c)
            if bi == 1:
                p.wait("vector", s_y0, s_y0.v)
            v_silu = {}
            for h in range(2):
                v_conv = {}
                for j in range(32):
                    cs = planar(cscr, j, TH)
                    p.op("vector", lambda v, j=j, h=h, cs=cs: v.tensor_scalar(
                        out=cs, in0=xc(j, h * TH), scalar1=consts[:, j * 4:j * 4 + 1], scalar2=None, op0=ALU.mult))
                    for tau in range(1, 3):
                        p.op("vector", lambda v, j=j, h=h, tau=tau, cs=cs: v.scalar_tensor_tensor(
                            out=cs, in0=xc(j, h * TH + tau), scalar=consts[:, j * 4 + tau:j * 4 + tau + 1],
                            in1=cs, op0=ALU.mult, op1=ALU.add))
                    v_conv[j] = p.op("vector", lambda v, j=j, h=h, cs=cs: v.scalar_tensor_tensor(
                        out=cs, in0=xc(j, h * TH + 3), scalar=consts[:, j * 4 + 3:j * 4 + 4],
                        in1=cs, op0=ALU.mult, op1=ALU.add), sem=ve)
                for j in range(32):
                    p.wait("scalar", ve, v_conv[j])
                    if bi == 1:
                        p.wait("scalar", s_big, s_big.v)
                    v_silu[(h, j)] = p.op("scalar", lambda sc, j=j, h=h: sc.activation(
                        out=u_sb[:, j * TB + h * TH:j * TB + (h + 1) * TH], in_=planar(cscr, j, TH),
                        func=AF.Silu, bias=consts[:, 128 + j:129 + j]), sem=ac)

            if stop_after == "conv":
                for jj in range(32):
                    p.wait("sync", ac, v_silu[(1, jj)])
                p.wait("sync", ac, v_silu[(0, 31)])
                vu_dma = p.dma("sync", u_out[:, tb0:tb0 + TB].rearrange("(j p) t -> p j t", p=128), sb3(u_sb, TB), s_big)
                p.wait("sync", s_big, s_big.v)
                stackB.close()
                p.stack = _oldstack
                return

            p.wait("tensor", s_wx, v_wx)
            for j in range(32):
                p.wait("tensor", ac, v_silu[(1, j)])
                p.op("tensor", lambda t, j=j: t.matmul(ps_dt[0][:], wx_sb[:, j * 160:j * 160 + 128],
                                                       planar(u_sb, j, TB), start=(j == 0), stop=(j == 31)))
            for j in range(32):
                def mmbc(t, j=j):
                    return t.matmul(ps_bc[:], wx_sb[:, j * 160 + 128:(j + 1) * 160],
                                    planar(u_sb, j, TB), start=(j == 0), stop=(j == 31))
                if j == 31:
                    v_bcmm = p.op("tensor", mmbc, sem=mm)
                else:
                    p.op("tensor", mmbc)
            p.wait("scalar", mm, v_bcmm)
            p.op("scalar", lambda sc: sc.activation(out=dtr_sb[:], in_=ps_dt[0][:], func=AF.Copy))
            p.op("scalar", lambda sc: sc.activation(out=bc_f[:], in_=ps_bc[:], func=AF.Copy))
            v_bce = p.op("scalar", lambda sc: sc.activation(out=bc_b[:], in_=ps_bc[:], func=AF.Copy), sem=ac)
            p.wait("sync", ac, v_bce)
            p.dma("sync", bc_out[:, tb0:tb0 + TB], bc_f[:], s_big)

            if stop_after == "xdbl":
                p.wait("sync", ac, v_bce)
                p.dma("sync", d_out[0:128, 0:TB], dtr_sb[:], s_big)
                p.dma("sync", bc_out[:, tb0:tb0 + TB], bc_f[:], s_big)
                p.wait("sync", s_big, s_big.v)
                stackB.close()
                p.stack = _oldstack
                return

            p.wait("tensor", s_wdt, v_wdt)
            p.wait("tensor", ac, v_bce)
            v_dmm = {}
            v_dsp = {}
            v_sp1 = {}
            for j in range(32):
                pb = ps_dt[j % 2]
                if j - 2 >= 0:
                    p.wait("tensor", ve, v_dsp[j - 2])
                    p.wait("tensor", ac, v_sp1[j - 2])
                v_dmm[j] = p.op("tensor", lambda t, j=j, pb=pb: t.matmul(
                    pb[:], wdt_sb[:, j * 128:(j + 1) * 128], dtr_sb[:], start=True, stop=True), sem=mm)
                bcol = consts[:, 160 + j:161 + j]
                A_ = spa[j % 2]
                B_ = spb[j % 2]
                p.wait("scalar", mm, v_dmm[j])
                if j - 2 >= 0:
                    p.wait("scalar", ve, v_dsp[j - 2])
                v_abs = p.op("scalar", lambda sc, pb=pb, bcol=bcol, A_=A_: sc.activation(out=A_[:], in_=pb[:], func=AF.Abs, bias=bcol), sem=ac)
                p.op("scalar", lambda sc, A_=A_: sc.activation(out=A_[:], in_=A_[:], func=AF.Exp, scale=-1.0))
                v_sp1[j] = p.op("scalar", lambda sc, A_=A_: sc.activation(out=A_[:], in_=A_[:], func=AF.Ln, bias=1.0), sem=ac)
                p.wait("vector", mm, v_dmm[j])
                p.wait("vector", ac, v_abs)
                p.op("vector", lambda v, pb=pb, bcol=bcol, B_=B_: v.tensor_scalar(
                    out=B_[:], in0=pb[:], scalar1=bcol, scalar2=0.0, op0=ALU.add, op1=ALU.max))
                p.wait("vector", ac, v_sp1[j])
                p.op("vector", lambda v, j=j, A_=A_, B_=B_: v.tensor_tensor(planar(del_sb, j, TB), A_[:], B_[:], ALU.add))
                v_dsp[j] = p.op("vector", lambda v, j=j: v.reduce_sum(sumd[:, j:j + 1], planar(del_sb, j, TB), axis=AX.X), sem=ve)

            if stop_after == "delta":
                p.wait("sync", ve, v_dsp[31])
                p.dma("sync", d_out[:, tb0:tb0 + TB].rearrange("(j p) t -> p j t", p=128), sb3(del_sb, TB), s_big)
                p.dma("sync", bc_out[:, tb0:tb0 + TB], bc_f[:], s_big)
                p.wait("sync", s_big, s_big.v)
                stackB.close()
                p.stack = _oldstack
                return

            v_du = {}
            for j in range(32):
                v_du[j] = p.op("vector", lambda v, j=j: v.tensor_tensor(
                    du_full(j), planar(del_sb, j, TB), planar(u_sb, j, TB), ALU.mult), sem=ve)

            p.wait("scalar", ve, v_du[31])
            for j in range(32):
                p.op("scalar", lambda sc, j=j, bi=bi: sc.activation(
                    out=summ[:, j * 64 + bi * 32 + DS:j * 64 + bi * 32 + 2 * DS],
                    in_=consts[:, 192 + j * DS:192 + (j + 1) * DS], func=AF.Exp, scale=sumd[:, j:j + 1]))

            p.wait("tensor", s_oh, v_oh)
            v_scan = {}
            v_repe = {}
            isc = 0
            v_y0_dma = {}
            for h in range(2):
                for n in range(DS):
                    hn = h * DS + n
                    pr = ps_rep[hn % 2]
                    if hn - 2 >= 0:
                        p.wait("tensor", ac, v_repe[hn - 2])
                    vb = p.op("tensor", lambda t, n=n, h=h, pr=pr: t.matmul(
                        pr[:], onehot_sb[:, n * 128:(n + 1) * 128],
                        bc_b[:, h * TH:(h + 1) * TH], start=True, stop=True), sem=mm)
                    p.wait("scalar", mm, vb)
                    if hn - 2 >= 0:
                        p.wait("scalar", ve, v_scan[hn - 1, 31])
                    v_brepe = p.op("scalar", lambda sc, pr=pr, hn=hn: sc.activation(
                        out=brep[hn % 2][:], in_=pr[:], func=AF.Copy), sem=ac)
                    p.wait("tensor", ac, v_brepe)
                    vc = p.op("tensor", lambda t, n=n, h=h, pr=pr: t.matmul(
                        pr[:], onehot_sb[:, (16 + n) * 128:(17 + n) * 128],
                        bc_b[:, h * TH:(h + 1) * TH], start=True, stop=True), sem=mm)
                    p.wait("scalar", mm, vc)
                    v_repe[hn] = p.op("scalar", lambda sc, pr=pr, hn=hn: sc.activation(
                        out=crep[hn % 2][:], in_=pr[:], func=AF.Copy), sem=ac)
                    p.wait("vector", ac, v_repe[hn])
                    if h == 1 and n == 0:
                        p.wait("vector", s_y0, v_y0_dma[0])
                    for j in range(32):
                        ai = at[j % 2]; bj = bt[j % 2]; si = sst[j % 2]; ci = sct[j % 2]
                        if isc - 2 >= 0:
                            pn, pj = divmod(isc - 2, 32)
                            p.wait("scalar", ve, v_scan[pn, pj])
                        va = p.op("scalar", lambda sc, j=j, n=n, h=h, ai=ai: sc.activation(
                            out=ai[:], in_=del_sb[:, j * TB + h * TH:j * TB + (h + 1) * TH],
                            func=AF.Exp, scale=Acol(j, n)), sem=ac)
                        p.op("vector", lambda v, j=j, h=h, bj=bj, hn=hn: v.tensor_tensor(
                            bj[:], du(j, h), brep[hn % 2][:], ALU.mult))
                        p.wait("vector", ac, va)
                        init = 0.0 if h == 0 else carry[:, j * DS + n:j * DS + n + 1]
                        vscan = p.op("vector", lambda v, ai=ai, bj=bj, si=si, init=init: v.tensor_tensor_scan(
                            si[:], ai[:], bj[:], init, ALU.mult, ALU.add), sem=ve)
                        p.wait("vector", ve, vscan)
                        if h == 0:
                            p.op("vector", lambda v, si=si, j=j, n=n: v.tensor_copy(
                                carry[:, j * DS + n:j * DS + n + 1], si[:, TH - 1:TH]))
                        else:
                            p.op("vector", lambda v, si=si, j=j, n=n, bi=bi: v.tensor_copy(
                                summ[:, j * 64 + bi * 32 + n:j * 64 + bi * 32 + n + 1], si[:, TH - 1:TH]))
                        p.op("vector", lambda v, si=si, ci=ci, hn=hn: v.tensor_tensor(ci[:], si[:], crep[hn % 2][:], ALU.mult))
                        if n == 0:
                            v_scan[hn, j] = p.op("vector", lambda v, ci=ci, j=j: v.tensor_copy(planar(y0_sb, j, TH), ci[:]), sem=ve)
                        else:
                            v_scan[hn, j] = p.op("vector", lambda v, ci=ci, j=j: v.tensor_tensor(
                                planar(y0_sb, j, TH), planar(y0_sb, j, TH), ci[:], ALU.add), sem=ve)
                        isc += 1
                p.wait("sync", ve, v_scan[h * DS + DS - 1, 31])
                v_y0_dma[h] = p.dma("sync", y0_out[:, tb0 + h * TH:tb0 + (h + 1) * TH].rearrange(
                    "(j p) t -> p j t", p=128), sb3(y0_sb, TH), s_y0)

            v_scan_done = v_scan[2 * DS - 1, 31]
            p.dma("sync", u_out[:, tb0:tb0 + TB].rearrange("(j p) t -> p j t", p=128), sb3(u_sb, TB), s_big)
            p.dma("sync", d_out[:, tb0:tb0 + TB].rearrange("(j p) t -> p j t", p=128), sb3(del_sb, TB), s_big)
            v_blk_prev_scan = v_scan_done

        p.dma("sync", sum_out.rearrange("(j p) c -> p j c", p=128), sb3(summ, 64), s_big)
        p.wait("sync", s_big, s_big.v)
        p.wait("sync", s_y0, s_y0.v)
        stackB.close()
        pass
    p.stack = _oldstack
    return


def build_l1_mamba_main(stop_after=None):
    nc = bass.Bass("TRN2", target_bir_lowering=False, debug=False)
    stack0 = contextlib.ExitStack()
    with stack0:
        p = Prog(nc, stack0)
        D = declare_l1(nc)
        emit_l1(nc, p, D, stop_after=stop_after)
        p.finalize()
    return nc

def plan128(v):
    if v.ndim == 1:
        n = v.shape[0] // 128
        return np.ascontiguousarray(v.reshape(n, 128).T)
    n = v.shape[0] // 128
    k = v.shape[1]
    return np.ascontiguousarray(v.reshape(n, 128, k).transpose(1, 0, 2).reshape(128, n * k))


def host_inputs_l1(core, I):
    b0, b1 = blocks_of(core)
    x = I["L0_attn_hidden_states"].reshape(L, DM).T
    TX = 515
    xs = np.zeros((DM, 2 * TX), dtype=np.float32)
    for bi, blk in enumerate((b0, b1)):
        t0 = blk * TB
        lo = max(0, t0 - 3)
        xs[:, bi * TX + (3 - (t0 - lo)):(bi + 1) * TX] = x[:, lo:t0 + TB]
    return {
        "x": np.ascontiguousarray(xs),
        "w_inT": np.ascontiguousarray(I["L0_attn_W_in"].T),
        "conv_w": plan128(I["L0_attn_conv_w"]),
        "conv_b": plan128(I["L0_attn_conv_b"]),
        "w_xT": np.ascontiguousarray(I["L0_attn_W_x"].T),
        "w_dtT": np.ascontiguousarray(I["L0_attn_W_dt"].T),
        "dt_bias": plan128(I["L0_attn_dt_bias"]),
        "a_log": plan128(I["L0_attn_A_log"]),
        "onehot": host_onehot_bc(),
    }


def declare_l2(nc, internal=(), alias=None):
    D = {}
    alias = alias or {}
    if "y0_in" in alias:
        D["y0_in"] = alias["y0_in"]
    else:
        k = "Internal" if "y0_in" in internal else "ExternalInput"
        D["y0_in"] = nc.dram_tensor("l2_y0_in", [DI, NTOK], F32, kind=k).ap()
    if "u_in" in alias:
        D["u_in"] = alias["u_in"]
    else:
        k = "Internal" if "u_in" in internal else "ExternalInput"
        D["u_in"] = nc.dram_tensor("l2_u_in", [DI, NTOK], BF16, kind=k).ap()
    if "z_in" in alias:
        D["z_in"] = alias["z_in"]
    else:
        k = "Internal" if "z_in" in internal else "ExternalInput"
        D["z_in"] = nc.dram_tensor("l2_z_in", [DI, NTOK], BF16, kind=k).ap()
    if "d_in" in alias:
        D["d_in"] = alias["d_in"]
    else:
        k = "Internal" if "d_in" in internal else "ExternalInput"
        D["d_in"] = nc.dram_tensor("l2_d_in", [DI, NTOK], BF16, kind=k).ap()
    if "c_in" in alias:
        D["c_in"] = alias["c_in"]
    else:
        k = "Internal" if "c_in" in internal else "ExternalInput"
        D["c_in"] = nc.dram_tensor("l2_c_in", [32, NTOK], F32, kind=k).ap()
    if "sums_all" in alias:
        D["sums_all"] = alias["sums_all"]
    else:
        k = "Internal" if "sums_all" in internal else "ExternalInput"
        D["sums_all"] = nc.dram_tensor("l2_sums_all", [DI, NBLK * 32], F32, kind=k).ap()
    if "a_log" in alias:
        D["a_log"] = alias["a_log"]
    else:
        k = "Internal" if "a_log" in internal else "ExternalInput"
        D["a_log"] = nc.dram_tensor("l2_a_log", [128, 512], F32, kind=k).ap()
    if "dvec" in alias:
        D["dvec"] = alias["dvec"]
    else:
        k = "Internal" if "dvec" in internal else "ExternalInput"
        D["dvec"] = nc.dram_tensor("l2_dvec", [128, 32], F32, kind=k).ap()
    if "onehot" in alias:
        D["onehot"] = alias["onehot"]
    else:
        k = "Internal" if "onehot" in internal else "ExternalInput"
        D["onehot"] = nc.dram_tensor("l2_onehot", [32, 32 * 128], BF16, kind=k).ap()
    if "h_mat" in alias:
        D["h_mat"] = alias["h_mat"]
    else:
        k = "Internal" if "h_mat" in internal else "ExternalInput"
        D["h_mat"] = nc.dram_tensor("l2_h_mat", [DI, DI], BF16, kind=k).ap()
    if "w_outT" in alias:
        D["w_outT"] = alias["w_outT"]
    else:
        k = "Internal" if "w_outT" in internal else "ExternalInput"
        D["w_outT"] = nc.dram_tensor("l2_w_outT", [DI, DM], F32, kind=k).ap()
    if "x_res" in alias:
        D["x_res"] = alias["x_res"]
    else:
        k = "Internal" if "x_res" in internal else "ExternalInput"
        D["x_res"] = nc.dram_tensor("l2_x_res", [DM, NTOK], F32, kind=k).ap()
    if "sel_in" in alias:
        D["sel_in"] = alias["sel_in"]
    else:
        k = "Internal" if "sel_in" in internal else "ExternalInput"
        D["sel_in"] = nc.dram_tensor("l2_sel_in", [128, 32], F32, kind=k).ap()
    if "x1_out" in alias:
        D["x1_out"] = alias["x1_out"]
    else:
        k = "Internal" if "x1_out" in internal else "ExternalOutput"
        D["x1_out"] = nc.dram_tensor("l2_x1_out", [DM, NTOK], F32, kind=k).ap()
    if "yg_scr" in alias:
        D["yg_scr"] = alias["yg_scr"]
    else:
        k = "Internal" if "yg_scr" in internal else "ExternalOutput"
        D["yg_scr"] = nc.dram_tensor("l2_yg_scr", [DI, NTOK], BF16, kind=k).ap()
    return D


def emit_l2(nc, p, D, stop_after=None):
    stack = contextlib.ExitStack()
    _oldstack = p.stack
    p.stack = stack
    TH = 256
    NTOK = 2 * TB

    y0_in = D["y0_in"]
    u_in = D["u_in"]
    z_in = D["z_in"]
    d_in = D["d_in"]
    c_in = D["c_in"]
    sums_all = D["sums_all"]
    a_log = D["a_log"]
    dvec = D["dvec"]
    onehot = D["onehot"]
    h_mat = D["h_mat"]
    w_outT = D["w_outT"]
    x_res = D["x_res"]
    sel_in = D["sel_in"]

    x1_out = D["x1_out"]
    yg_scr = D["yg_scr"]

    with stack:
        pass
        mm = p.sem("mm")
        ve = p.sem("ve")
        ac = p.sem("ac")
        s_c = p.sem("s_c")
        s_cl = p.sem("s_cl")
        s_sums = p.sem("s_sums")
        s_act = p.sem("s_act")
        s_yg = p.sem("s_yg")
        s_hw = [p.sem("s_hw0"), p.sem("s_hw1")]
        s_ygf = p.sem("s_ygf")
        s_x = [p.sem("s_x0"), p.sem("s_x1")]
        s_o = p.sem("s_o")

        consts = p.sbuf("consts", [128, 544], F32)
        onehot_sb = p.sbuf("onehot_sb", [32, 32 * 128], BF16)
        c_sb = p.sbuf("c_sb", [32, NTOK], BF16)
        sin_sb = p.sbuf("sin_sb", [128, 2 * 32 * DS], F32)
        sel_sb = p.sbuf("sel_sb", [128, 32], F32)
        ones_sb = p.sbuf("ones_sb", [128, TH], F32)
        onecol = p.sbuf("onecol", [128, 1], F32)
        onerow = p.sbuf("onerow", [1, 128], F32)
        miscc = p.sbuf("miscc", [128, 2], F32)

        psA = [p.psum(f"psA{i}", [128, 512], F32) for i in range(2)]
        ps_rep = [psA[0][:, 0:TH], psA[1][:, 0:TH]]

        def Acol(j, n):
            return consts[:, j * DS + n:j * DS + n + 1]

        v_al = p.dma("sync", consts[:, 0:512], a_log, s_c)
        p.dma("sync", sel_sb[:], sel_in, s_c)
        p.dma("sync", consts[:, 512:544], dvec, s_c)
        p.dma("sync", onehot_sb[:], onehot, s_c)
        v_cl = p.dma("gpsimd", c_sb[:], c_in, s_cl)
        p.wait("scalar", s_c, v_al)
        v_expA = p.op("scalar", lambda e: e.activation(out=consts[:, 0:512], in_=consts[:, 0:512], func=AF.Exp), sem=ac)
        p.wait("vector", ac, v_expA)
        p.op("vector", lambda e: e.tensor_scalar_mul(consts[:, 0:512], consts[:, 0:512], -1.0))
        p.op("vector", lambda e: e.memset(ones_sb[:], 1.0))
        p.op("vector", lambda e: e.memset(onecol[:], 1.0))
        p.op("vector", lambda e: e.memset(onerow[:], 1.0))
        p.op("vector", lambda e: e.memset(miscc[:, 0:1], 1e-5))
        v_msc = p.op("vector", lambda e: e.memset(miscc[:, 1:2], -0.5), sem=ve)
        p.wait("scalar", ve, v_msc)

        stackP = contextlib.ExitStack()
        sums_sb = stackP.enter_context(nc.sbuf_tensor(f"{p.pre}sums_sb", [128, 32 * NBLK * 32], F32))
        St = stackP.enter_context(nc.sbuf_tensor(f"{p.pre}St", [128, 32 * DS], F32))
        sin_all = stackP.enter_context(nc.sbuf_tensor(f"{p.pre}sin_all", [128, 32 * NBLK * DS], F32))
        v_sums = p.dma("sync", sums_sb[:].rearrange("p (j c) -> p j c", c=NBLK * 32),
                       sums_all.rearrange("(j p) c -> p j c", p=128), s_sums)
        p.wait("vector", s_sums, v_sums)
        p.wait("vector", s_c, s_c.v)
        p.op("vector", lambda v: v.memset(St[:], 0.0))

        def sums_col(j, b, kind):
            base = j * (NBLK * 32) + b * 32 + kind * DS
            return sums_sb[:, base:base + DS]

        def sin_all_col(j, b):
            return sin_all[:, (j * NBLK + b) * DS:(j * NBLK + b + 1) * DS]

        for b in range(NBLK):
            for j in range(32):
                p.op("vector", lambda v, j=j, b=b: v.tensor_copy(sin_all_col(j, b), planar(St, j, DS)))
            if b < NBLK - 1:
                for j in range(32):
                    p.op("vector", lambda v, j=j, b=b: v.tensor_tensor(
                        planar(St, j, DS), planar(St, j, DS), sums_col(j, b, 1), ALU.mult))
                    p.op("vector", lambda v, j=j, b=b: v.tensor_tensor(
                        planar(St, j, DS), planar(St, j, DS), sums_col(j, b, 0), ALU.add))
        for bi in range(2):
            for j in range(32):
                dst = sin_sb[:, (bi * 32 + j) * DS:(bi * 32 + j + 1) * DS]
                p.op("vector", lambda v, j=j, bi=bi, dst=dst: v.tensor_scalar(
                    out=dst, in0=sin_all_col(j, 0), scalar1=sel_sb[:, bi * 16:bi * 16 + 1],
                    scalar2=None, op0=ALU.mult))
                for b in range(1, NBLK):
                    p.op("vector", lambda v, j=j, b=b, bi=bi, dst=dst: v.scalar_tensor_tensor(
                        out=dst, in0=sin_all_col(j, b), scalar=sel_sb[:, bi * 16 + b:bi * 16 + b + 1],
                        in1=dst, op0=ALU.mult, op1=ALU.add))
        v_pref = p.op("vector", lambda v: v.tensor_copy(St[:, 0:1], St[:, 0:1]), sem=ve)
        stackP.close()

        def sin_col(bi, j, n):
            return sin_sb[:, (bi * 32 + j) * DS + n:(bi * 32 + j) * DS + n + 1]

        stackC = contextlib.ExitStack()
        E = stackC.enter_context
        y0q = E(nc.sbuf_tensor(f"{p.pre}y0q", [128, 32 * TH], F32))
        uq = E(nc.sbuf_tensor(f"{p.pre}uq", [128, 32 * TH], BF16))
        zq = E(nc.sbuf_tensor(f"{p.pre}zq", [128, 32 * TH], BF16))
        dq = E(nc.sbuf_tensor(f"{p.pre}dq", [128, 32 * TH], BF16))
        cumd = E(nc.sbuf_tensor(f"{p.pre}cumd", [128, 32 * TH], F32))
        ygq = E(nc.sbuf_tensor(f"{p.pre}ygq", [128, 32 * TH], BF16))
        ccar = E(nc.sbuf_tensor(f"{p.pre}ccar", [128, 32], F32))
        crep = [E(nc.sbuf_tensor(f"{p.pre}crep{i}", [128, TH], BF16)) for i in range(2)]
        pt = [E(nc.sbuf_tensor(f"{p.pre}pt{i}", [128, TH], F32)) for i in range(2)]
        ct = [E(nc.sbuf_tensor(f"{p.pre}ct{i}", [128, TH], F32)) for i in range(2)]
        szt = [E(nc.sbuf_tensor(f"{p.pre}szt{i}", [128, TH], F32)) for i in range(2)]
        gt = [E(nc.sbuf_tensor(f"{p.pre}gt{i}", [128, TH], F32)) for i in range(2)]

        p.wait("vector", ve, v_pref)
        p.wait("tensor", s_c, s_c.v)
        v_gate_done_prev = None
        for bi in range(2):
            for h in range(2):
                c0 = bi * TB + h * TH
                if v_gate_done_prev is not None:
                    p.wait("sync", ve, v_gate_done_prev)
                    p.wait("sync", s_yg, s_yg.v)
                else:
                    p.wait("sync", ve, v_pref)
                p.dma("sync", sb3(y0q, TH), y0_in[:, c0:c0 + TH].rearrange("(j p) t -> p j t", p=128), s_act)
                p.dma("sync", sb3(uq, TH), u_in[:, c0:c0 + TH].rearrange("(j p) t -> p j t", p=128), s_act)
                p.dma("sync", sb3(zq, TH), z_in[:, c0:c0 + TH].rearrange("(j p) t -> p j t", p=128), s_act)
                v_ld = p.dma("sync", sb3(dq, TH), d_in[:, c0:c0 + TH].rearrange("(j p) t -> p j t", p=128), s_act)
                p.wait("vector", s_act, v_ld)
                v_cum_last = None
                for j in range(32):
                    init = 0.0 if h == 0 else ccar[:, j:j + 1]
                    vs = p.op("vector", lambda v, j=j, init=init: v.tensor_tensor_scan(
                        planar(cumd, j, TH), ones_sb[:], planar(dq, j, TH), init, ALU.mult, ALU.add), sem=ve)
                    p.wait("vector", ve, vs)
                    v_cum_last = vs
                    if h == 0:
                        p.op("vector", lambda v, j=j: v.tensor_copy(ccar[:, j:j + 1], planar(cumd, j, TH)[:, TH - 1:TH]))
                p.wait("scalar", ve, v_cum_last)
                v_it = {}
                v_repe = {}
                isc = 0
                for n in range(DS):
                    pr = ps_rep[n % 2]
                    if n - 2 >= 0:
                        p.wait("tensor", ac, v_repe[n - 2])
                    p.wait("tensor", s_cl, v_cl)
                    vc_ = p.op("tensor", lambda t, n=n, c0=c0, pr=pr: t.matmul(
                        pr, onehot_sb[:, (16 + n) * 128:(17 + n) * 128],
                        c_sb[:, c0:c0 + TH], start=True, stop=True), sem=mm)
                    p.wait("scalar", mm, vc_)
                    if n - 1 >= 0:
                        p.wait("scalar", ve, v_it[(n - 1, 31)])
                    v_repe[n] = p.op("scalar", lambda sc, pr=pr, n=n: sc.activation(
                        out=crep[n % 2][:], in_=pr, func=AF.Copy), sem=ac)
                    p.wait("vector", ac, v_repe[n])
                    for j in range(32):
                        pi = pt[j % 2]; ci = ct[j % 2]
                        if isc - 2 >= 0:
                            pn, pj = divmod(isc - 2, 32)
                            p.wait("scalar", ve, v_it[(pn, pj)])
                        vp = p.op("scalar", lambda sc, j=j, n=n, pi=pi: sc.activation(
                            out=pi[:], in_=planar(cumd, j, TH), func=AF.Exp, scale=Acol(j, n)), sem=ac)
                        p.wait("vector", ac, vp)
                        p.op("vector", lambda v, pi=pi, ci=ci, j=j, n=n, bi=bi: v.scalar_tensor_tensor(
                            out=ci[:], in0=pi[:], scalar=sin_col(bi, j, n), in1=crep[n % 2][:],
                            op0=ALU.mult, op1=ALU.mult))
                        v_it[(n, j)] = p.op("vector", lambda v, ci=ci, j=j: v.tensor_tensor(
                            planar(y0q, j, TH), planar(y0q, j, TH), ci[:], ALU.add), sem=ve)
                        isc += 1
                v_gate = {}
                for j in range(32):
                    sz = szt[j % 2]; g = gt[j % 2]
                    if j - 2 >= 0:
                        p.wait("scalar", ve, v_gate[j - 2])
                    vs2 = p.op("scalar", lambda sc, j=j, sz=sz: sc.activation(
                        out=sz[:], in_=planar(zq, j, TH), func=AF.Silu), sem=ac)
                    p.wait("vector", ac, vs2)
                    p.op("vector", lambda v, j=j, g=g: v.scalar_tensor_tensor(
                        out=g[:], in0=planar(uq, j, TH), scalar=consts[:, 512 + j:513 + j],
                        in1=planar(y0q, j, TH), op0=ALU.mult, op1=ALU.add))
                    v_gate[j] = p.op("vector", lambda v, j=j, g=g, sz=sz: v.tensor_tensor(
                        planar(ygq, j, TH), g[:], sz[:], ALU.mult), sem=ve)
                p.wait("sync", ve, v_gate[31])
                p.dma("sync", yg_scr[:, c0:c0 + TH].rearrange("(j p) t -> p j t", p=128), sb3(ygq, TH), s_yg)
                v_gate_done_prev = v_gate[31]
        if stop_after == "gate":
            p.wait("sync", s_yg, s_yg.v)
            stackC.close()
            p.stack = _oldstack
            return
        stackC.close()

        stackH = contextlib.ExitStack()
        E = stackH.enter_context
        yh_sb = E(nc.sbuf_tensor(f"{p.pre}yh_sb", [128, 32 * NTOK], BF16))
        stackH2 = contextlib.ExitStack()
        yg_sb = stackH2.enter_context(nc.sbuf_tensor(f"{p.pre}yg_sb", [128, 32 * NTOK], BF16))
        hbuf = [stackH2.enter_context(nc.sbuf_tensor(f"{p.pre}hbuf{i}", [128, 32 * 128], BF16)) for i in range(2)]
        ps_h = psA

        p.wait("sync", s_yg, s_yg.v)
        v_ygf = p.dma("sync", sb3(yg_sb, NTOK), yg_scr.rearrange("(j p) t -> p j t", p=128), s_ygf)

        def hstripe_ap(e):
            return h_mat[:, e * 128:(e + 1) * 128].rearrange("(k p) e -> p k e", p=128)

        v_hl = {0: p.dma("sync", hbuf[0][:].rearrange("p (k e) -> p k e", e=128), hstripe_ap(0), s_hw[0])}
        v_hmm = {}
        v_hev = {}
        p.wait("tensor", s_ygf, v_ygf)
        git = 0
        for e in range(32):
            if e + 1 < 32:
                if e >= 1:
                    p.wait("sync", mm, v_hmm[(e - 1, 1)])
                v_hl[e + 1] = p.dma("sync", hbuf[(e + 1) % 2][:].rearrange("p (k e) -> p k e", e=128),
                                    hstripe_ap(e + 1), s_hw[(e + 1) % 2])
            p.wait("tensor", s_hw[e % 2], v_hl[e])
            for tg in range(2):
                pb = ps_h[git % 2]
                if git - 2 >= 0:
                    p.wait("tensor", ac, v_hev[git - 2])
                for k in range(32):
                    def mmop(t, pb=pb, e=e, k=k, tg=tg, s=(k == 0), z=(k == 31)):
                        return t.matmul(pb[:], hbuf[e % 2][:, k * 128:(k + 1) * 128],
                                        yg_sb[:, k * NTOK + tg * 512:k * NTOK + (tg + 1) * 512], start=s, stop=z)
                    if k == 31:
                        v_hmm[(e, tg)] = p.op("tensor", mmop, sem=mm)
                    else:
                        p.op("tensor", mmop)
                p.wait("scalar", mm, v_hmm[(e, tg)])
                v_hev[git] = p.op("scalar", lambda sc, pb=pb, e=e, tg=tg: sc.activation(
                    out=yh_sb[:, e * NTOK + tg * 512:e * NTOK + (tg + 1) * 512], in_=pb[:], func=AF.Copy), sem=ac)
                git += 1
        v_h_done = v_hev[git - 1]
        if stop_after == "H":
            p.wait("sync", ac, v_h_done)
            for k in range(32):
                p.dma("sync", yg_scr[k * 128:(k + 1) * 128, :], yh_sb[:, k * NTOK:(k + 1) * NTOK], s_yg)
            p.wait("sync", s_yg, s_yg.v)
            stackH2.close()
            stackH.close()
            p.stack = _oldstack
            return
        stackH2.close()

        stackW = contextlib.ExitStack()
        E = stackW.enter_context
        r_sb = E(nc.sbuf_tensor(f"{p.pre}r_sb", [128, 16 * NTOK], F32))
        wbuf = [E(nc.sbuf_tensor(f"{p.pre}wbuf{i}", [128, 32 * 128], BF16)) for i in range(2)]
        xbuf = [E(nc.sbuf_tensor(f"{p.pre}xbuf{i}", [128, NTOK], F32)) for i in range(2)]
        sq = [E(nc.sbuf_tensor(f"{p.pre}sq0", [128, 512], F32))] * 2
        rows = E(nc.sbuf_tensor(f"{p.pre}rows", [1, 3 * NTOK], F32))
        bca = E(nc.sbuf_tensor(f"{p.pre}bca", [128, NTOK], F32))
        bcb = E(nc.sbuf_tensor(f"{p.pre}bcb", [128, NTOK], F32))
        stgx = [E(nc.sbuf_tensor(f"{p.pre}stgx0", [128, NTOK], F32))] * 2
        ps_s = [p.psum(f"ps_s{i}", [1, NTOK], F32) for i in range(2)]
        ps_bb = p.psum("ps_bb", [128, 512], F32)

        def wstripe_ap(do):
            return w_outT[:, do * 128:(do + 1) * 128].rearrange("(k p) e -> p k e", p=128)

        s_wo = [p.sem("s_wo0"), p.sem("s_wo1")]
        s_xr = [p.sem("s_xr0"), p.sem("s_xr1")]
        p.wait("gpsimd", ac, v_h_done)
        p.wait("sync", ac, v_h_done)
        v_wl = {0: p.dma("gpsimd", wbuf[0][:].rearrange("p (k e) -> p k e", e=128), wstripe_ap(0), s_wo[0])}
        v_xl = {0: p.dma("sync", xbuf[0][:], x_res[0:128, :], s_xr[0])}
        v_wmm = {}
        v_res = {}
        v_sq = {}
        v_smm = {}
        p.wait("tensor", ac, v_h_done)
        for do in range(16):
            if do + 1 < 16:
                if do >= 1:
                    p.wait("gpsimd", mm, v_wmm[(do - 1, 1)])
                    p.wait("sync", ve, v_res[(do - 1, 1)])
                v_wl[do + 1] = p.dma("gpsimd", wbuf[(do + 1) % 2][:].rearrange("p (k e) -> p k e", e=128),
                                     wstripe_ap(do + 1), s_wo[(do + 1) % 2])
                v_xl[do + 1] = p.dma("sync", xbuf[(do + 1) % 2][:], x_res[(do + 1) * 128:(do + 2) * 128, :], s_xr[(do + 1) % 2])
            p.wait("tensor", s_wo[do % 2], v_wl[do])
            for tg in range(2):
                pb = ps_h[(do * 2 + tg) % 2]
                if (do * 2 + tg) - 2 >= 0:
                    p.wait("tensor", ve, v_res.get((do - 1, tg)))
                for k in range(32):
                    def mmop(t, pb=pb, do=do, k=k, tg=tg, s=(k == 0), z=(k == 31)):
                        return t.matmul(pb[:], wbuf[do % 2][:, k * 128:(k + 1) * 128],
                                        yh_sb[:, k * NTOK + tg * 512:k * NTOK + (tg + 1) * 512], start=s, stop=z)
                    if k == 31:
                        v_wmm[(do, tg)] = p.op("tensor", mmop, sem=mm)
                    else:
                        p.op("tensor", mmop)
                p.wait("vector", mm, v_wmm[(do, tg)])
                p.wait("vector", s_xr[do % 2], v_xl[do])
                v_res[(do, tg)] = p.op("vector", lambda v, pb=pb, do=do, tg=tg: v.tensor_tensor(
                    r_sb[:, do * NTOK + tg * 512:do * NTOK + (tg + 1) * 512], pb[:],
                    xbuf[do % 2][:, tg * 512:(tg + 1) * 512], ALU.add), sem=ve)
                p.wait("scalar", ve, v_res[(do, tg)])
                if (do * 2 + tg) - 1 >= 0:
                    p.wait("scalar", mm, v_smm.get((do * 2 + tg) - 1, 0))
                v_sq[(do, tg)] = p.op("scalar", lambda sc, do=do, tg=tg: sc.activation(
                    out=sq[0][:, 0:512],
                    in_=r_sb[:, do * NTOK + tg * 512:do * NTOK + (tg + 1) * 512], func=AF.Square), sem=ac)
                p.wait("tensor", ve, v_res[(do, tg)])
                p.op("tensor", lambda t, do=do, tg=tg: t.matmul(
                    ps_s[0][:, tg * 512:(tg + 1) * 512], onecol[:],
                    r_sb[:, do * NTOK + tg * 512:do * NTOK + (tg + 1) * 512],
                    start=(do == 0), stop=(do == 15)))
                p.wait("tensor", ac, v_sq[(do, tg)])
                v_smm[do * 2 + tg] = p.op("tensor", lambda t, do=do, tg=tg: t.matmul(
                    ps_s[1][:, tg * 512:(tg + 1) * 512], onecol[:],
                    sq[0][:, 0:512],
                    start=(do == 0), stop=(do == 15)), sem=mm)

        v_stats = v_smm[31]
        R = NTOK
        p.wait("vector", mm, v_stats)
        p.op("vector", lambda v: v.tensor_scalar_mul(rows[:, 0:R], ps_s[0][:], 1.0 / DM))
        p.op("vector", lambda v: v.tensor_scalar_mul(rows[:, R:2 * R], ps_s[1][:], 1.0 / DM))
        p.op("vector", lambda v: v.tensor_tensor(rows[:, 2 * R:3 * R], rows[:, 0:R], rows[:, 0:R], ALU.mult))
        v_var = p.op("vector", lambda v: v.tensor_tensor(rows[:, R:2 * R], rows[:, R:2 * R], rows[:, 2 * R:3 * R], ALU.subtract), sem=ve)
        p.wait("scalar", ve, v_var)
        p.op("scalar", lambda sc: sc.activation(out=rows[:, R:2 * R], in_=rows[:, R:2 * R], func=AF.Ln, bias=miscc[0:1, 0:1]))
        v_rstd = p.op("scalar", lambda sc: sc.activation(out=rows[:, R:2 * R], in_=rows[:, R:2 * R], func=AF.Exp, scale=miscc[0:1, 1:2]), sem=ac)
        p.wait("vector", ac, v_rstd)
        v_mr = p.op("vector", lambda v: v.tensor_tensor(rows[:, 2 * R:3 * R], rows[:, 0:R], rows[:, R:2 * R], ALU.mult), sem=ve)
        p.wait("tensor", ve, v_mr)
        v_bc = {}
        for tg in range(2):
            if tg > 0:
                p.wait("tensor", ac, v_bc[tg - 1])
            v1 = p.op("tensor", lambda t, tg=tg: t.matmul(
                ps_bb[:], onerow[:], rows[:, R + tg * 512:R + (tg + 1) * 512], start=True, stop=True), sem=mm)
            p.wait("scalar", mm, v1)
            vba = p.op("scalar", lambda sc, tg=tg: sc.activation(out=bca[:, tg * 512:(tg + 1) * 512], in_=ps_bb[:], func=AF.Copy), sem=ac)
            p.wait("tensor", ac, vba)
            v2 = p.op("tensor", lambda t, tg=tg: t.matmul(
                ps_bb[:], onerow[:], rows[:, 2 * R + tg * 512:2 * R + (tg + 1) * 512], start=True, stop=True), sem=mm)
            p.wait("scalar", mm, v2)
            v_bc[tg] = p.op("scalar", lambda sc, tg=tg: sc.activation(out=bcb[:, tg * 512:(tg + 1) * 512], in_=ps_bb[:], func=AF.Copy), sem=ac)
        p.wait("vector", ac, v_bc[1])
        v_x1 = {}
        for do in range(16):
            sx = stgx[do % 2]
            if do - 1 >= 0:
                p.wait("vector", s_o, 16 * do)
            p.op("vector", lambda v, do=do, sx=sx: v.tensor_tensor(sx[:], r_sb[:, do * NTOK:(do + 1) * NTOK], bca[:], ALU.mult))
            v_x1[do] = p.op("vector", lambda v, do=do, sx=sx: v.tensor_tensor(sx[:], sx[:], bcb[:], ALU.subtract), sem=ve)
            p.wait("sync", ve, v_x1[do])
            p.dma("sync", x1_out[do * 128:(do + 1) * 128, :], sx[:], s_o)
        p.wait("sync", s_o, s_o.v)
        p.wait("sync", s_yg, s_yg.v)
        stackW.close()
        stackH.close()
        pass
    p.stack = _oldstack
    return


def build_l2_mamba_post(stop_after=None):
    nc = bass.Bass("TRN2", target_bir_lowering=False, debug=False)
    stack0 = contextlib.ExitStack()
    with stack0:
        p = Prog(nc, stack0)
        D = declare_l2(nc)
        emit_l2(nc, p, D, stop_after=stop_after)
        p.finalize()
    return nc

def host_inputs_l2(core, I, l1outs, sums_all):
    b0, b1 = blocks_of(core)
    x = I["L0_attn_hidden_states"].reshape(L, DM).T
    toks = np.r_[b0 * TB:(b0 + 1) * TB, b1 * TB:(b1 + 1) * TB]
    o = l1outs[core]
    sel = np.zeros((128, 32), np.float32)
    sel[:, 0 * 16 + b0] = 1
    sel[:, 1 * 16 + b1] = 1
    return {
        "sel_in": sel,
        "y0_in": o["y0_out"], "u_in": o["u_out"], "z_in": o["z_out"], "d_in": o["d_out"],
        "c_in": o["bc_out"],
        "sums_all": sums_all,
        "a_log": plan128(I["L0_attn_A_log"]),
        "dvec": plan128(I["L0_attn_D"]),
        "onehot": host_onehot_bc(),
        "h_mat": hadamard_bf16(),
        "w_outT": np.ascontiguousarray(I["L0_attn_W_out"].T),
        "x_res": np.ascontiguousarray(x[:, toks]),
    }


_HAD = {}


def hadamard_bf16():
    if "H" not in _HAD:
        import math
        H = np.array([[1.0]], dtype=np.float32)
        while H.shape[0] < DI:
            H = np.block([[H, H], [H, -H]])
        _HAD["H"] = (H / math.sqrt(DI)).astype(BF)
    return _HAD["H"]


def host_sums_all(l1outs):
    sums = np.zeros((DI, NBLK * 32), np.float32)
    for c in range(NCORES):
        b0, b1 = blocks_of(c)
        sums[:, b0 * 32:(b0 + 1) * 32] = l1outs[c]["sum_out"][:, 0:32]
        sums[:, b1 * 32:(b1 + 1) * 32] = l1outs[c]["sum_out"][:, 32:64]
    return sums




NTOK = 2 * TB
IN8 = 8192


def _ffn_common(p, nc, x_in, out_t, lhsT_g, lhsT_u, lhsT_d, NE, scale_mode,
                consts_extra):
    raise NotImplementedError


def declare_l3(nc, internal=(), alias=None):
    D = {}
    alias = alias or {}
    if "x_in" in alias:
        D["x_in"] = alias["x_in"]
    else:
        k = "Internal" if "x_in" in internal else "ExternalInput"
        D["x_in"] = nc.dram_tensor("l3_x_in", [DM, NTOK], F32, kind=k).ap()
    if "wg" in alias:
        D["wg"] = alias["wg"]
    else:
        k = "Internal" if "wg" in internal else "ExternalInput"
        D["wg"] = nc.dram_tensor("l3_wg", [DM, IN8], I8, kind=k).ap()
    if "wu" in alias:
        D["wu"] = alias["wu"]
    else:
        k = "Internal" if "wu" in internal else "ExternalInput"
        D["wu"] = nc.dram_tensor("l3_wu", [DM, IN8], I8, kind=k).ap()
    if "wd" in alias:
        D["wd"] = alias["wd"]
    else:
        k = "Internal" if "wd" in internal else "ExternalInput"
        D["wd"] = nc.dram_tensor("l3_wd", [IN8, DM], I8, kind=k).ap()
    if "sg" in alias:
        D["sg"] = alias["sg"]
    else:
        k = "Internal" if "sg" in internal else "ExternalInput"
        D["sg"] = nc.dram_tensor("l3_sg", [128, 64], F32, kind=k).ap()
    if "su" in alias:
        D["su"] = alias["su"]
    else:
        k = "Internal" if "su" in internal else "ExternalInput"
        D["su"] = nc.dram_tensor("l3_su", [128, 64], F32, kind=k).ap()
    if "sd" in alias:
        D["sd"] = alias["sd"]
    else:
        k = "Internal" if "sd" in internal else "ExternalInput"
        D["sd"] = nc.dram_tensor("l3_sd", [128, 16], F32, kind=k).ap()
    if "x2_out" in alias:
        D["x2_out"] = alias["x2_out"]
    else:
        k = "Internal" if "x2_out" in internal else "ExternalOutput"
        D["x2_out"] = nc.dram_tensor("l3_x2_out", [DM, NTOK], F32, kind=k).ap()
    return D


def emit_l3(nc, p, D):
    stack = contextlib.ExitStack()
    _oldstack = p.stack
    p.stack = stack

    x_in = D["x_in"]
    wg = D["wg"]
    wu = D["wu"]
    wd = D["wd"]
    sg = D["sg"]
    su = D["su"]
    sd = D["sd"]
    x2_out = D["x2_out"]

    with stack:
        pass
        mm = p.sem("mm")
        ve = p.sem("ve")
        ac = p.sem("ac")
        s_c = p.sem("s_c")
        s_xb = p.sem("s_xb")
        s_xf = p.sem("s_xf")
        s_wg = [p.sem("s_wg0"), p.sem("s_wg1")]
        s_wu = [p.sem("s_wu0"), p.sem("s_wu1")]
        s_wd = [p.sem("s_wd0"), p.sem("s_wd1")]
        s_o = p.sem("s_o")

        consts = p.sbuf("consts", [128, 146], F32)
        onecol = p.sbuf("onecol", [128, 1], F32)
        onerow = p.sbuf("onerow", [1, 128], F32)
        x1b = p.sbuf("x1b", [128, 16 * NTOK], BF16)
        fuse = p.sbuf("fuse", [128, 64 * TB], BF16)
        r_sb = p.sbuf("r_sb", [128, 16 * NTOK], F32)
        stPH = contextlib.ExitStack()
        EP = stPH.enter_context
        wgbuf = [EP(nc.sbuf_tensor(f"{p.pre}wgbuf{i}", [128, 16 * 128], BF16)) for i in range(2)]
        wubuf = [EP(nc.sbuf_tensor(f"{p.pre}wubuf{i}", [128, 16 * 128], BF16)) for i in range(2)]
        wdbuf = [EP(nc.sbuf_tensor(f"{p.pre}wdbuf{i}", [128, DM], BF16)) for i in range(2)]
        gbuf = [EP(nc.sbuf_tensor(f"{p.pre}gbuf{i}", [128, TB], BF16)) for i in range(2)]
        ubuf = [EP(nc.sbuf_tensor(f"{p.pre}ubuf{i}", [128, TB], BF16)) for i in range(2)]
        xfb = EP(nc.sbuf_tensor(f"{p.pre}xfb", [128, 8 * TB], F32))

        psD = [p.psum(f"psD{i}", [128, 512], F32) for i in range(8)]

        p.dma("sync", consts[:, 0:64], sg, s_c)
        p.dma("sync", consts[:, 64:128], su, s_c)
        v_c = p.dma("sync", consts[:, 128:144], sd, s_c)
        p.op("vector", lambda v: v.memset(onecol[:], 1.0))
        p.op("vector", lambda v: v.memset(onerow[:], 1.0))
        p.op("vector", lambda v: v.memset(consts[:, 144:145], 1e-5))
        v_ms = p.op("vector", lambda v: v.memset(consts[:, 145:146], -0.5), sem=ve)
        v_xb = p.dma("gpsimd", sb3(x1b, NTOK), x_in.rearrange("(j p) t -> p j t", p=128), s_xb)

        def cstripe(w, e):
            return w[:, e * 128:(e + 1) * 128].rearrange("(k p) e -> p k e", p=128)

        v_wgl = {}
        v_wul = {}
        v_gmm = {}
        v_umm = {}
        v_fu = {}
        v_wgl[(0, 0)] = p.dma("gpsimd", sb3(wgbuf[0], 128), cstripe(wg, 0), s_wg[0])
        v_wul[(0, 0)] = p.dma("gpsimd", sb3(wubuf[0], 128), cstripe(wu, 0), s_wu[0])
        p.wait("tensor", s_xb, v_xb)
        p.wait("tensor", ve, v_ms)
        p.wait("scalar", s_c, v_c)

        v_res = {}
        v_smm = {}
        v_dn_done = None
        for tg in range(2):
            t0 = tg * TB
            for e in range(64):
                key = (tg, e)
                nkey = (tg, e + 1) if e + 1 < 64 else ((tg + 1, 0) if tg == 0 else None)
                if nkey is not None:
                    if (tg, e) != (0, 0):
                        p.wait("gpsimd", mm, v_umm[pkey])
                    v_wgl[nkey] = p.dma("gpsimd", sb3(wgbuf[(e + 1) % 2], 128),
                                        cstripe(wg, nkey[1] if nkey[0] == tg else 0), s_wg[(e + 1) % 2])
                    v_wul[nkey] = p.dma("gpsimd", sb3(wubuf[(e + 1) % 2], 128),
                                        cstripe(wu, nkey[1] if nkey[0] == tg else 0), s_wu[(e + 1) % 2])
                p.wait("tensor", s_wg[e % 2], v_wgl[key])
                p.wait("tensor", s_wu[e % 2], v_wul[key])
                if (tg, e) != (0, 0):
                    p.wait("tensor", ac, v_ev_prev)
                if tg == 1 and e == 0:
                    p.wait("tensor", ve, v_dn_done)
                for k in range(16):
                    def mmop(t, e=e, k=k, t0=t0, s=(k == 0), z=(k == 15)):
                        return t.matmul(psD[0][:], wgbuf[e % 2][:, k * 128:(k + 1) * 128],
                                        x1b[:, k * NTOK + t0:k * NTOK + t0 + TB], start=s, stop=z)
                    if k == 15:
                        v_gmm[key] = p.op("tensor", mmop, sem=mm)
                    else:
                        p.op("tensor", mmop)
                for k in range(16):
                    def mmop(t, e=e, k=k, t0=t0, s=(k == 0), z=(k == 15)):
                        return t.matmul(psD[1][:], wubuf[e % 2][:, k * 128:(k + 1) * 128],
                                        x1b[:, k * NTOK + t0:k * NTOK + t0 + TB], start=s, stop=z)
                    if k == 15:
                        v_umm[key] = p.op("tensor", mmop, sem=mm)
                    else:
                        p.op("tensor", mmop)
                pkey = key
                p.wait("scalar", mm, v_gmm[key])
                if (tg, e) != (0, 0):
                    p.wait("scalar", ve, v_fu[pfkey])
                p.op("scalar", lambda sc, e=e: sc.activation(
                    out=gbuf[e % 2][:], in_=psD[0][:], func=AF.Silu, scale=consts[:, e:e + 1]))
                p.wait("scalar", mm, v_umm[key])
                v_ev_prev = p.op("scalar", lambda sc, e=e: sc.activation(
                    out=ubuf[e % 2][:], in_=psD[1][:], func=AF.Copy, scale=consts[:, 64 + e:65 + e]), sem=ac)
                p.wait("vector", ac, v_ev_prev)
                v_fu[key] = p.op("vector", lambda v, e=e: v.tensor_tensor(
                    planar(fuse, e, TB), gbuf[e % 2][:], ubuf[e % 2][:], ALU.mult), sem=ve)
                pfkey = key

            for dmh in range(2):
                if not (tg == 0 and dmh == 0):
                    p.wait("sync", ve, v_res_last)
                v_x = p.dma("sync", sb3(xfb, TB),
                            x_in[dmh * 1024:(dmh + 1) * 1024, t0:t0 + TB].rearrange("(j p) t -> p j t", p=128), s_xf)
                v_wdl = {}
                p.wait("gpsimd", ve, v_fu[(tg, 63)])
                if not (tg == 0 and dmh == 0):
                    p.wait("gpsimd", ve, v_res_last)
                v_wdl[0] = p.dma("gpsimd", wdbuf[0][:], wd[0:128, :], s_wd[0])
                v_dmm = {}
                for e in range(64):
                    if e + 1 < 64:
                        if e >= 1:
                            p.wait("gpsimd", mm, v_dmm[e - 1])
                        v_wdl[e + 1] = p.dma("gpsimd", wdbuf[(e + 1) % 2][:], wd[(e + 1) * 128:(e + 2) * 128, :], s_wd[(e + 1) % 2])
                    p.wait("tensor", s_wd[e % 2], v_wdl[e])
                    if e == 0:
                        p.wait("tensor", ve, v_fu[(tg, 63)])
                        if not (tg == 0 and dmh == 0):
                            p.wait("tensor", ve, v_res_last)
                    for d8 in range(8):
                        def mmop(t, e=e, d8=d8, dmh=dmh, t0=t0, s=(e == 0), z=(e == 63)):
                            return t.matmul(psD[d8][:],
                                            wdbuf[e % 2][:, (dmh * 8 + d8) * 128:(dmh * 8 + d8 + 1) * 128],
                                            fuse[:, e * TB:(e + 1) * TB], start=s, stop=z)
                        if d8 == 7:
                            v_dmm[e] = p.op("tensor", mmop, sem=mm)
                        else:
                            p.op("tensor", mmop)
                p.wait("vector", mm, v_dmm[63])
                p.wait("vector", s_xf, v_x)
                ic = 2 * tg + dmh
                for d8 in range(8):
                    dm = dmh * 8 + d8
                    v_res[(ic, d8)] = p.op("vector", lambda v, dm=dm, d8=d8, t0=t0: v.scalar_tensor_tensor(
                        out=r_sb[:, dm * NTOK + t0:dm * NTOK + t0 + TB],
                        in0=psD[d8][:],
                        scalar=consts[:, 128 + dm:129 + dm],
                        in1=planar(xfb, d8, TB), op0=ALU.mult, op1=ALU.add), sem=ve)
                v_res_last = v_res[(ic, 7)]
            v_dn_done = v_res_last

        stPH.close()
        stT = contextlib.ExitStack()
        ET = stT.enter_context
        sq = ET(nc.sbuf_tensor(f"{p.pre}sq", [128, 512], F32))
        sqb = ET(nc.sbuf_tensor(f"{p.pre}sqb", [128, 512], F32))
        rows = ET(nc.sbuf_tensor(f"{p.pre}rows", [1, 3 * NTOK], F32))
        bca = ET(nc.sbuf_tensor(f"{p.pre}bca", [128, NTOK], F32))
        bcb = ET(nc.sbuf_tensor(f"{p.pre}bcb", [128, NTOK], F32))
        stgx = ET(nc.sbuf_tensor(f"{p.pre}stgx", [128, NTOK], F32))
        R = NTOK
        v_sqp = {}
        v_s1l = None
        for dm in range(16):
            p.wait("scalar", ve, v_res[(3, 7)])
            if dm >= 1:
                p.wait("scalar", mm, v_sqp[dm - 1])
            v_sq2 = p.op("scalar", lambda sc, dm=dm: sc.activation(
                out=sq[:], in_=r_sb[:, dm * NTOK:dm * NTOK + 512], func=AF.Square), sem=ac)
            v_sq3 = p.op("scalar", lambda sc, dm=dm: sc.activation(
                out=sqb[:], in_=r_sb[:, dm * NTOK + 512:dm * NTOK + NTOK], func=AF.Square), sem=ac)
            if dm == 0:
                p.wait("tensor", ve, v_res[(3, 7)])
            for tgb in range(2):
                p.op("tensor", lambda t, dm=dm, tgb=tgb, s=(dm == 0), z=(dm == 15): t.matmul(
                    psD[tgb][0:1, :], onecol[:], r_sb[:, dm * NTOK + tgb * 512:dm * NTOK + (tgb + 1) * 512],
                    start=s, stop=z))
            p.wait("tensor", ac, v_sq3)
            p.op("tensor", lambda t, dm=dm, s=(dm == 0), z=(dm == 15): t.matmul(
                psD[2][0:1, :], onecol[:], sq[:], start=s, stop=z))
            v_sqp[dm] = p.op("tensor", lambda t, dm=dm, s=(dm == 0), z=(dm == 15): t.matmul(
                psD[3][0:1, :], onecol[:], sqb[:], start=s, stop=z), sem=mm)
        v_stats = v_sqp[15]
        p.wait("vector", mm, v_stats)
        p.op("vector", lambda v: v.tensor_copy(rows[:, 0:512], psD[0][0:1, :]))
        p.op("vector", lambda v: v.tensor_copy(rows[:, 512:1024], psD[1][0:1, :]))
        p.op("vector", lambda v: v.tensor_copy(rows[:, R:R + 512], psD[2][0:1, :]))
        p.op("vector", lambda v: v.tensor_copy(rows[:, R + 512:R + 1024], psD[3][0:1, :]))
        p.op("vector", lambda v: v.tensor_scalar_mul(rows[:, 0:R], rows[:, 0:R], 1.0 / DM))
        p.op("vector", lambda v: v.tensor_scalar_mul(rows[:, R:2 * R], rows[:, R:2 * R], 1.0 / DM))
        p.op("vector", lambda v: v.tensor_tensor(rows[:, 2 * R:3 * R], rows[:, 0:R], rows[:, 0:R], ALU.mult))
        v_var = p.op("vector", lambda v: v.tensor_tensor(rows[:, R:2 * R], rows[:, R:2 * R], rows[:, 2 * R:3 * R], ALU.subtract), sem=ve)
        p.wait("scalar", ve, v_var)
        p.op("scalar", lambda sc: sc.activation(out=rows[:, R:2 * R], in_=rows[:, R:2 * R], func=AF.Ln, bias=consts[0:1, 144:145]))
        v_rstd = p.op("scalar", lambda sc: sc.activation(out=rows[:, R:2 * R], in_=rows[:, R:2 * R], func=AF.Exp, scale=consts[0:1, 145:146]), sem=ac)
        p.wait("vector", ac, v_rstd)
        v_mr = p.op("vector", lambda v: v.tensor_tensor(rows[:, 2 * R:3 * R], rows[:, 0:R], rows[:, R:2 * R], ALU.mult), sem=ve)
        p.wait("tensor", ve, v_mr)
        v_bc = {}
        for tgb in range(2):
            if tgb > 0:
                p.wait("tensor", ac, v_bc[tgb - 1])
            v1 = p.op("tensor", lambda t, tgb=tgb: t.matmul(
                psD[0][:], onerow[:], rows[:, R + tgb * 512:R + (tgb + 1) * 512], start=True, stop=True), sem=mm)
            p.wait("scalar", mm, v1)
            vba = p.op("scalar", lambda sc, tgb=tgb: sc.activation(out=bca[:, tgb * 512:(tgb + 1) * 512], in_=psD[0][:], func=AF.Copy), sem=ac)
            p.wait("tensor", ac, vba)
            v2 = p.op("tensor", lambda t, tgb=tgb: t.matmul(
                psD[0][:], onerow[:], rows[:, 2 * R + tgb * 512:2 * R + (tgb + 1) * 512], start=True, stop=True), sem=mm)
            p.wait("scalar", mm, v2)
            v_bc[tgb] = p.op("scalar", lambda sc, tgb=tgb: sc.activation(out=bcb[:, tgb * 512:(tgb + 1) * 512], in_=psD[0][:], func=AF.Copy), sem=ac)
        p.wait("vector", ac, v_bc[1])
        for do in range(16):
            if do - 1 >= 0:
                p.wait("vector", s_o, 16 * do)
            p.op("vector", lambda v, do=do: v.tensor_tensor(stgx[:], r_sb[:, do * NTOK:(do + 1) * NTOK], bca[:], ALU.mult))
            vx1 = p.op("vector", lambda v, do=do: v.tensor_tensor(stgx[:], stgx[:], bcb[:], ALU.subtract), sem=ve)
            p.wait("sync", ve, vx1)
            p.dma("sync", x2_out[do * 128:(do + 1) * 128, :], stgx[:], s_o)
        p.wait("sync", s_o, s_o.v)
        stT.close()
        pass
    p.stack = _oldstack
    return


def build_l3_ffn0():
    nc = bass.Bass("TRN2", target_bir_lowering=False, debug=False)
    stack0 = contextlib.ExitStack()
    with stack0:
        p = Prog(nc, stack0)
        D = declare_l3(nc)
        emit_l3(nc, p, D)
        p.finalize()
    return nc

def host_inputs_l3(core, I, x1_by_core):
    return {
        **({"x_in": x1_by_core[core]} if x1_by_core is not None else {}),
        "wg": I["L0_ffn_w_gate"],
        "wu": I["L0_ffn_w_up"],
        "wd": I["L0_ffn_w_down"],
        "sg": plan128(I["L0_ffn_s_gate"]),
        "su": plan128(I["L0_ffn_s_up"]),
        "sd": plan128(I["L0_ffn_s_down"]),
    }


def declare_l4(nc, internal=(), alias=None):
    KVL, QL = 512, 1536
    D = {}
    alias = alias or {}
    if "x_in" in alias:
        D["x_in"] = alias["x_in"]
    else:
        k = "Internal" if "x_in" in internal else "ExternalInput"
        D["x_in"] = nc.dram_tensor("l4_x_in", [DM, NTOK], F32, kind=k).ap()
    if "w_kvd" in alias:
        D["w_kvd"] = alias["w_kvd"]
    else:
        k = "Internal" if "w_kvd" in internal else "ExternalInput"
        D["w_kvd"] = nc.dram_tensor("l4_w_kvd", [DM, KVL], F32, kind=k).ap()
    if "w_qd" in alias:
        D["w_qd"] = alias["w_qd"]
    else:
        k = "Internal" if "w_qd" in internal else "ExternalInput"
        D["w_qd"] = nc.dram_tensor("l4_w_qd", [DM, QL], F32, kind=k).ap()
    if "w_kr" in alias:
        D["w_kr"] = alias["w_kr"]
    else:
        k = "Internal" if "w_kr" in internal else "ExternalInput"
        D["w_kr"] = nc.dram_tensor("l4_w_kr", [DM, 64], F32, kind=k).ap()
    if "w_kc" in alias:
        D["w_kc"] = alias["w_kc"]
    else:
        k = "Internal" if "w_kc" in internal else "ExternalInput"
        D["w_kc"] = nc.dram_tensor("l4_w_kc", [KVL, 1024], F32, kind=k).ap()
    if "w_qc" in alias:
        D["w_qc"] = alias["w_qc"]
    else:
        k = "Internal" if "w_qc" in internal else "ExternalInput"
        D["w_qc"] = nc.dram_tensor("l4_w_qc", [QL, 1024], F32, kind=k).ap()
    if "w_qr" in alias:
        D["w_qr"] = alias["w_qr"]
    else:
        k = "Internal" if "w_qr" in internal else "ExternalInput"
        D["w_qr"] = nc.dram_tensor("l4_w_qr", [QL, 1024], F32, kind=k).ap()
    if "w_v" in alias:
        D["w_v"] = alias["w_v"]
    else:
        k = "Internal" if "w_v" in internal else "ExternalInput"
        D["w_v"] = nc.dram_tensor("l4_w_v", [KVL, DM], F32, kind=k).ap()
    if "rotm" in alias:
        D["rotm"] = alias["rotm"]
    else:
        k = "Internal" if "rotm" in internal else "ExternalInput"
        D["rotm"] = nc.dram_tensor("l4_rotm", [128, 128], BF16, kind=k).ap()
    if "cosp" in alias:
        D["cosp"] = alias["cosp"]
    else:
        k = "Internal" if "cosp" in internal else "ExternalInput"
        D["cosp"] = nc.dram_tensor("l4_cosp", [128, NTOK], F32, kind=k).ap()
    if "sinp" in alias:
        D["sinp"] = alias["sinp"]
    else:
        k = "Internal" if "sinp" in internal else "ExternalInput"
        D["sinp"] = nc.dram_tensor("l4_sinp", [128, NTOK], F32, kind=k).ap()
    if "q_out" in alias:
        D["q_out"] = alias["q_out"]
    else:
        k = "Internal" if "q_out" in internal else "ExternalOutput"
        D["q_out"] = nc.dram_tensor("l4_q_out", [DM, NTOK], BF16, kind=k).ap()
    if "k_out" in alias:
        D["k_out"] = alias["k_out"]
    else:
        k = "Internal" if "k_out" in internal else "ExternalOutput"
        D["k_out"] = nc.dram_tensor("l4_k_out", [DM, NTOK], BF16, kind=k).ap()
    if "vtok_out" in alias:
        D["vtok_out"] = alias["vtok_out"]
    else:
        k = "Internal" if "vtok_out" in internal else "ExternalOutput"
        D["vtok_out"] = nc.dram_tensor("l4_vtok_out", [NTOK, DM], BF16, kind=k).ap()
    return D


def emit_l4(nc, p, D):
    stack = contextlib.ExitStack()
    _oldstack = p.stack
    p.stack = stack
    KVL, QL = 512, 1536

    x_in = D["x_in"]
    w_kvd = D["w_kvd"]
    w_qd = D["w_qd"]
    w_kr = D["w_kr"]
    w_kc = D["w_kc"]
    w_qc = D["w_qc"]
    w_qr = D["w_qr"]
    w_v = D["w_v"]
    rotm = D["rotm"]
    cosp = D["cosp"]
    sinp = D["sinp"]

    q_out = D["q_out"]
    k_out = D["k_out"]
    vtok_out = D["vtok_out"]

    with stack:
        pass
        mm = p.sem("mm")
        ve = p.sem("ve")
        ac = p.sem("ac")
        s_c = p.sem("s_c")
        s_w = [p.sem(f"s_w{i}") for i in range(3)]
        s_xb = p.sem("s_xb")
        s_o = p.sem("s_o")
        s_o2 = p.sem("s_o2")
        s_o3 = p.sem("s_o3")

        x2b = p.sbuf("x2b", [128, 16 * NTOK], BF16)
        ckv = p.sbuf("ckv", [128, 4 * NTOK], BF16)
        cq = p.sbuf("cq", [128, 12 * NTOK], BF16)
        rot_sb = p.sbuf("rot_sb", [128, 128], BF16)
        cos_sb = p.sbuf("cos_sb", [128, NTOK], F32)
        sin_sb = p.sbuf("sin_sb", [128, NTOK], F32)
        scl = p.sbuf("scl", [128, 1], F32)
        qr_sb = p.sbuf("qr_sb", [128, 8 * NTOK], BF16)
        qc2 = p.sbuf("qc2", [128, 8 * NTOK], BF16)
        kc3 = p.sbuf("kc3", [128, 8 * NTOK], BF16)
        rtmp = p.sbuf("rtmp", [128, NTOK], BF16)
        f1 = p.sbuf("f1", [128, NTOK], F32)
        f2 = p.sbuf("f2", [128, NTOK], F32)
        kr_sb = p.sbuf("kr_sb", [64, NTOK], BF16)
        asm = p.sbuf("asm", [128, NTOK], BF16)
        asm2 = p.sbuf("asm2", [128, NTOK], BF16)
        vst = p.sbuf("vst", [128, DM], BF16)

        ps = [p.psum(f"ps{i}", [128, 512], F32) for i in range(4)]

        p.dma("sync", rot_sb[:], rotm, s_c)
        p.dma("sync", cos_sb[:], cosp, s_c)
        v_cc = p.dma("sync", sin_sb[:], sinp, s_c)
        v_scl = p.op("vector", lambda v: v.memset(scl[:], 0.08838834764831845), sem=ve)
        v_xb = p.dma("gpsimd", sb3(x2b, NTOK), x_in.rearrange("(j p) t -> p j t", p=128), s_xb)

        p.wait("tensor", s_xb, v_xb)
        p.wait("tensor", s_c, v_cc)
        p.wait("tensor", ve, v_scl)

        v_ev = {}
        git = [0]

        def proj(dst, wsb, wstride, oc, src_chunks, rhs, scale=None, out_rows=128,
                 dst_off=None, wcoloff=0, wait_first=None):
            last = None
            for tg in range(2):
                pb = ps[git[0] % 4]
                if git[0] - 4 >= 0:
                    p.wait("tensor", ac, v_ev[git[0] - 4])
                if wait_first is not None and tg == 0:
                    p.wait("tensor", wait_first[0], wait_first[1])
                for k in range(src_chunks):
                    def mmop(t, pb=pb, k=k, oc=oc, tg=tg, wsb=wsb, wstride=wstride, rhs=rhs,
                             out_rows=out_rows, wcoloff=wcoloff,
                             s=(k == 0), z=(k == src_chunks - 1)):
                        return t.matmul(pb[0:out_rows, :],
                                        wsb[:, k * wstride + wcoloff + oc * 128:k * wstride + wcoloff + oc * 128 + out_rows],
                                        rhs[:, k * NTOK + tg * TB:k * NTOK + (tg + 1) * TB], start=s, stop=z)
                    if k == src_chunks - 1:
                        vmm = p.op("tensor", mmop, sem=mm)
                    else:
                        p.op("tensor", mmop)
                p.wait("scalar", mm, vmm)
                off = (dst_off if dst_off is not None else oc * NTOK) + tg * TB
                v_ev[git[0]] = p.op("scalar", lambda sc, pb=pb, dst=dst, off=off, scale=scale, out_rows=out_rows:
                                    sc.activation(out=dst[0:out_rows, off:off + TB],
                                                  in_=pb[0:out_rows, :], func=AF.Copy,
                                                  **({"scale": scale} if scale is not None else {})), sem=ac)
                last = v_ev[git[0]]
                git[0] += 1
            return last

        stW1 = contextlib.ExitStack()
        wkvd_sb = stW1.enter_context(nc.sbuf_tensor(f"{p.pre}wkvd_sb", [128, 16 * KVL], BF16))
        wqd_sb = stW1.enter_context(nc.sbuf_tensor(f"{p.pre}wqd_sb", [128, 16 * QL], BF16))
        wkr_sb = stW1.enter_context(nc.sbuf_tensor(f"{p.pre}wkr_sb", [128, 16 * 64], BF16))
        p.dma("gpsimd", wkvd_sb[:].rearrange("p (j e) -> p j e", e=KVL), w_kvd.rearrange("(j p) e -> p j e", p=128), s_w[0])
        p.dma("gpsimd", wqd_sb[:].rearrange("p (j e) -> p j e", e=QL), w_qd.rearrange("(j p) e -> p j e", p=128), s_w[0])
        v_w1 = p.dma("gpsimd", wkr_sb[:].rearrange("p (j e) -> p j e", e=64), w_kr.rearrange("(j p) e -> p j e", p=128), s_w[0])
        p.wait("tensor", s_w[0], v_w1)
        for oc in range(4):
            proj(ckv, wkvd_sb, KVL, oc, 16, x2b)
        for oc in range(12):
            proj(cq, wqd_sb, QL, oc, 16, x2b)
        v_kr = proj(kr_sb, wkr_sb, 64, 0, 16, x2b, out_rows=64, dst_off=0)
        if git[0] - 4 >= 0:
            p.wait("tensor", ac, v_ev[git[0] - 4])
        v_w1_mm_done = p.op("tensor", lambda t, pb=ps[git[0] % 4]: t.matmul(pb[0:1, 0:1], rot_sb[0:1, 0:1], rot_sb[0:1, 0:1], start=True, stop=True), sem=mm)
        stW1.close()

        stW2 = contextlib.ExitStack()
        wqr_sb = stW2.enter_context(nc.sbuf_tensor(f"{p.pre}wqr_sb", [128, 12 * 1024], BF16))
        wqc_sb = stW2.enter_context(nc.sbuf_tensor(f"{p.pre}wqc_sb", [128, 12 * 1024], BF16))
        p.wait("gpsimd", mm, v_w1_mm_done)
        p.dma("gpsimd", wqr_sb[:].rearrange("p (j e) -> p j e", e=1024), w_qr.rearrange("(j p) e -> p j e", p=128), s_w[1])
        v_w2 = p.dma("gpsimd", wqc_sb[:].rearrange("p (j e) -> p j e", e=1024), w_qc.rearrange("(j p) e -> p j e", p=128), s_w[1])
        v_qr = None
        for oc in range(8):
            v_qr = proj(qr_sb, wqr_sb, 1024, oc, 12, cq, scale=scl[:, 0:1],
                        wait_first=((s_w[1], v_w2) if oc == 0 else None))
        v_qc = None
        for oc in range(8):
            v_qc = proj(qc2, wqc_sb, 1024, oc, 12, cq, scale=scl[:, 0:1])

        v_rope = {}
        v_rope_prev = None
        for oc in range(8):
            for tg in range(2):
                pb = ps[git[0] % 4]
                if git[0] - 4 >= 0:
                    p.wait("tensor", ac, v_ev[git[0] - 4])
                p.wait("tensor", ac, v_qr)
                vmm = p.op("tensor", lambda t, pb=pb, oc=oc, tg=tg: t.matmul(
                    pb[:], rot_sb[:], qr_sb[:, oc * NTOK + tg * TB:oc * NTOK + (tg + 1) * TB],
                    start=True, stop=True), sem=mm)
                p.wait("scalar", mm, vmm)
                if v_rope_prev is not None:
                    p.wait("scalar", ve, v_rope_prev)
                v_ev[git[0]] = p.op("scalar", lambda sc, pb=pb: sc.activation(
                    out=rtmp[:, 0:TB], in_=pb[:], func=AF.Copy), sem=ac)
                p.wait("vector", ac, v_ev[git[0]])
                git[0] += 1
                p.op("vector", lambda v, oc=oc, tg=tg: v.tensor_tensor(
                    f1[:, 0:TB], qr_sb[:, oc * NTOK + tg * TB:oc * NTOK + (tg + 1) * TB],
                    cos_sb[:, tg * TB:(tg + 1) * TB], ALU.mult))
                p.op("vector", lambda v, tg=tg: v.tensor_tensor(
                    f2[:, 0:TB], rtmp[:, 0:TB], sin_sb[:, tg * TB:(tg + 1) * TB], ALU.mult))
                v_rope[(oc, tg)] = p.op("vector", lambda v, oc=oc, tg=tg: v.tensor_tensor(
                    qr_sb[:, oc * NTOK + tg * TB:oc * NTOK + (tg + 1) * TB], f1[:, 0:TB], f2[:, 0:TB], ALU.add), sem=ve)
                v_rope_prev = v_rope[(oc, tg)]

        p.wait("vector", ac, v_kr)
        p.op("vector", lambda v: v.tensor_tensor(f1[0:64, 0:NTOK], kr_sb[:], cos_sb[0:64, :], ALU.mult))
        for tg in range(2):
            pb = ps[git[0] % 4]
            if git[0] - 4 >= 0:
                p.wait("tensor", ac, v_ev[git[0] - 4])
            p.wait("tensor", ac, v_kr)
            vmm = p.op("tensor", lambda t, pb=pb, tg=tg: t.matmul(
                pb[0:64, :], rot_sb[0:64, 0:64], kr_sb[:, tg * TB:(tg + 1) * TB], start=True, stop=True), sem=mm)
            p.wait("scalar", mm, vmm)
            p.wait("scalar", ve, v_rope_prev)
            v_ev[git[0]] = p.op("scalar", lambda sc, pb=pb, tg=tg: sc.activation(
                out=rtmp[0:64, tg * TB:(tg + 1) * TB], in_=pb[0:64, :], func=AF.Copy), sem=ac)
            git[0] += 1
        p.wait("vector", ac, v_ev[git[0] - 1])
        p.op("vector", lambda v: v.tensor_tensor(f2[0:64, 0:NTOK], rtmp[0:64, 0:NTOK], sin_sb[0:64, :], ALU.mult))
        p.op("vector", lambda v: v.tensor_tensor(f1[0:64, 0:NTOK], f1[0:64, 0:NTOK], f2[0:64, 0:NTOK], ALU.add))
        v_krb = p.op("vector", lambda v: v.tensor_copy(kr_sb[:], f1[0:64, 0:NTOK]), sem=ve)

        for h in range(16):
            til, half = h // 2, (h % 2) * 64
            if h == 0:
                p.wait("vector", ac, v_qc)
                p.wait("vector", ve, v_rope[(7, 1)])
            if h > 0:
                p.wait("vector", s_o, 16 * h)
            p.op("vector", lambda v, til=til, half=half: v.tensor_copy(
                asm[0:64, :], qc2[half:half + 64, til * NTOK:(til + 1) * NTOK]))
            vq = p.op("vector", lambda v, til=til, half=half: v.tensor_copy(
                asm[64:128, :], qr_sb[half:half + 64, til * NTOK:(til + 1) * NTOK]), sem=ve)
            p.wait("sync", ve, vq)
            p.dma("sync", q_out[h * 128:(h + 1) * 128, :], asm[:], s_o)
        v_w2_mm_done = v_ev[git[0] - 1]
        stW2.close()

        stW3 = contextlib.ExitStack()
        wkc_sb = stW3.enter_context(nc.sbuf_tensor(f"{p.pre}wkc_sb", [128, 4 * 1024], BF16))
        wv_sb = stW3.enter_context(nc.sbuf_tensor(f"{p.pre}wv_sb", [128, 4 * DM], BF16))
        p.wait("gpsimd", ac, v_w2_mm_done)
        p.dma("gpsimd", wkc_sb[:].rearrange("p (j e) -> p j e", e=1024), w_kc.rearrange("(j p) e -> p j e", p=128), s_w[2])
        v_w3 = p.dma("gpsimd", wv_sb[:].rearrange("p (j e) -> p j e", e=DM), w_v.rearrange("(j p) e -> p j e", p=128), s_w[2])
        v_kc = None
        for oc in range(8):
            v_kc = proj(kc3, wkc_sb, 1024, oc, 4, ckv, wait_first=((s_w[2], v_w3) if oc == 0 else None))
        for h in range(16):
            til, half = h // 2, (h % 2) * 64
            if h == 0:
                p.wait("vector", ac, v_kc)
                p.wait("vector", ve, v_krb)
            if h > 0:
                p.wait("vector", s_o2, 16 * h)
            p.op("vector", lambda v, til=til, half=half: v.tensor_copy(
                asm2[0:64, :], kc3[half:half + 64, til * NTOK:(til + 1) * NTOK]))
            vk = p.op("vector", lambda v: v.tensor_copy(asm2[64:128, :], kr_sb[:]), sem=ve)
            p.wait("sync", ve, vk)
            p.dma("sync", k_out[h * 128:(h + 1) * 128, :], asm2[:], s_o2)

        vidx = 0
        v_vev = {}
        for tc in range(8):
            for ng in range(4):
                pb = ps[git[0] % 4]
                if git[0] - 4 >= 0:
                    p.wait("tensor", ac, v_ev[git[0] - 4])
                for k in range(4):
                    def mmop(t, pb=pb, tc=tc, ng=ng, k=k, s=(k == 0), z=(k == 3)):
                        return t.matmul(pb[:], ckv[:, k * NTOK + tc * 128:k * NTOK + (tc + 1) * 128],
                                        wv_sb[:, k * DM + ng * 512:k * DM + (ng + 1) * 512], start=s, stop=z)
                    if k == 3:
                        vmm = p.op("tensor", mmop, sem=mm)
                    else:
                        p.op("tensor", mmop)
                p.wait("scalar", mm, vmm)
                if vidx >= 4:
                    p.wait("scalar", s_o3, 16 * (vidx - 3))
                v_ev[git[0]] = p.op("scalar", lambda sc, pb=pb, ng=ng: sc.activation(
                    out=vst[:, ng * 512:(ng + 1) * 512], in_=pb[:], func=AF.Copy), sem=ac)
                p.wait("sync", ac, v_ev[git[0]])
                git[0] += 1
                p.dma("sync", vtok_out[tc * 128:(tc + 1) * 128, ng * 512:(ng + 1) * 512], vst[:, ng * 512:(ng + 1) * 512], s_o3)
                vidx += 1

        p.wait("sync", s_o, s_o.v)
        p.wait("sync", s_o2, s_o2.v)
        p.wait("sync", s_o3, s_o3.v)
        stW3.close()
        pass
    p.stack = _oldstack
    return


def build_l4_qkv():
    nc = bass.Bass("TRN2", target_bir_lowering=False, debug=False)
    stack0 = contextlib.ExitStack()
    with stack0:
        p = Prog(nc, stack0)
        D = declare_l4(nc)
        emit_l4(nc, p, D)
        p.finalize()
    return nc

def host_rope_tables(core):
    b0, b1 = blocks_of(core)
    toks = np.r_[b0 * TB:(b0 + 1) * TB, b1 * TB:(b1 + 1) * TB].astype(np.float32)
    RD = 64
    inv = 1.0 / 10000.0 ** (np.arange(0, RD, 2, dtype=np.float32) / RD)
    fr = np.outer(toks, inv)
    c = np.cos(fr).T
    s = np.sin(fr).T
    cos64 = np.concatenate([c, c], 0)
    sin64 = np.concatenate([s, s], 0)
    cosp = np.concatenate([cos64, cos64], 0).astype(np.float32)
    sinp = np.concatenate([sin64, sin64], 0).astype(np.float32)
    rot = np.zeros((128, 128), np.float32)
    for r in range(128):
        base = (r // 64) * 64
        i = r % 64
        if i < 32:
            rot[base + i + 32, r] = -1.0
        else:
            rot[base + i - 32, r] = 1.0
    return cosp, sinp, rot.astype(BF)


def host_inputs_l4(core, I, x2_by_core):
    cosp, sinp, rot = host_rope_tables(core)
    return {
        **({"x_in": x2_by_core[core]} if x2_by_core is not None else {}),
        "w_kvd": I["L1_attn_W_kv_down"],
        "w_qd": I["L1_attn_W_q_down"],
        "w_kr": I["L1_attn_W_kr"],
        "w_kc": I["L1_attn_W_kc"],
        "w_qc": I["L1_attn_W_qc"],
        "w_qr": I["L1_attn_W_qr"],
        "w_v": I["L1_attn_W_v"],
        "rotm": rot, "cosp": cosp, "sinp": sinp,
    }

NKS = 64


def declare_l5(nc, internal=(), alias=None):
    D = {}
    alias = alias or {}
    if "q_in" in alias:
        D["q_in"] = alias["q_in"]
    else:
        k = "Internal" if "q_in" in internal else "ExternalInput"
        D["q_in"] = nc.dram_tensor("l5_q_in", [DM, NTOK], BF16, kind=k).ap()
    if "k_in" in alias:
        D["k_in"] = alias["k_in"]
    else:
        k = "Internal" if "k_in" in internal else "ExternalInput"
        D["k_in"] = nc.dram_tensor("l5_k_in", [DM, NKS * 128], BF16, kind=k).ap()
    if "v_in" in alias:
        D["v_in"] = alias["v_in"]
    else:
        k = "Internal" if "v_in" in internal else "ExternalInput"
        D["v_in"] = nc.dram_tensor("l5_v_in", [NKS * 128, DM], BF16, kind=k).ap()
    if "m_in" in alias:
        D["m_in"] = alias["m_in"]
    else:
        k = "Internal" if "m_in" in internal else "ExternalInput"
        D["m_in"] = nc.dram_tensor("l5_m_in", [NKS * 128, NTOK], BF16, kind=k).ap()
    if "x_res" in alias:
        D["x_res"] = alias["x_res"]
    else:
        k = "Internal" if "x_res" in internal else "ExternalInput"
        D["x_res"] = nc.dram_tensor("l5_x_res", [DM, NTOK], F32, kind=k).ap()
    if "w_o" in alias:
        D["w_o"] = alias["w_o"]
    else:
        k = "Internal" if "w_o" in internal else "ExternalInput"
        D["w_o"] = nc.dram_tensor("l5_w_o", [DM, DM], F32, kind=k).ap()
    if "b_o" in alias:
        D["b_o"] = alias["b_o"]
    else:
        k = "Internal" if "b_o" in internal else "ExternalInput"
        D["b_o"] = nc.dram_tensor("l5_b_o", [128, 16], F32, kind=k).ap()
    if "x3_out" in alias:
        D["x3_out"] = alias["x3_out"]
    else:
        k = "Internal" if "x3_out" in internal else "ExternalOutput"
        D["x3_out"] = nc.dram_tensor("l5_x3_out", [DM, NTOK], F32, kind=k).ap()
    return D


def emit_l5(nc, p, D, dbg=False):
    stack = contextlib.ExitStack()
    _oldstack = p.stack
    p.stack = stack

    q_in = D["q_in"]
    k_in = D["k_in"]
    v_in = D["v_in"]
    m_in = D["m_in"]
    x_res = D["x_res"]
    w_o = D["w_o"]
    b_o = D["b_o"]
    x3_out = D["x3_out"]

    with stack:
        pass
        mm = p.sem("mm")
        ve = p.sem("ve")
        ac = p.sem("ac")
        s_c = p.sem("s_c")
        s_q = p.sem("s_q")
        s_k = [p.sem("s_k0"), p.sem("s_k1")]
        s_v = [p.sem("s_v0"), p.sem("s_v1")]
        s_m = [p.sem("s_m0"), p.sem("s_m1")]
        s_wo = [p.sem("s_wo0"), p.sem("s_wo1")]
        s_xr = [p.sem("s_xr0"), p.sem("s_xr1")]
        s_o = p.sem("s_o")

        consts = p.sbuf("consts", [128, 18], F32)
        onecol = p.sbuf("onecol", [128, 1], BF16)
        onecol_f = p.sbuf("onecol_f", [128, 1], F32)
        onerow = p.sbuf("onerow", [1, 128], F32)
        q_sb = p.sbuf("q_sb", [128, 16 * NTOK], BF16)
        ctx = p.sbuf("ctx", [128, 16 * NTOK], BF16)
        kbuf = [p.sbuf(f"kbuf{i}", [128, 128], BF16) for i in range(2)]
        vbuf = [p.sbuf(f"vbuf{i}", [128, DM], BF16) for i in range(2)]
        mbuf = [p.sbuf(f"mbuf{i}", [128, NTOK], BF16) for i in range(2)]
        pt_sb = p.sbuf("pt_sb", [128, NTOK], BF16)
        linv = p.sbuf("linv", [1, NTOK], F32)
        lbc = p.sbuf("lbc", [128, NTOK], F32)
        r_sb = p.sbuf("r_sb", [128, 16 * NTOK], F32)
        wbuf = [p.sbuf(f"wbuf{i}", [128, 16 * 128], BF16) for i in range(2)]
        xbuf = [p.sbuf(f"xbuf{i}", [128, NTOK], F32) for i in range(2)]
        sq = p.sbuf("sq", [128, 512], F32)
        rows = p.sbuf("rows", [1, 3 * NTOK], F32)
        bca = p.sbuf("bca", [128, NTOK], F32)
        bcb = p.sbuf("bcb", [128, NTOK], F32)
        stgx = p.sbuf("stgx", [128, NTOK], F32)

        ps_st = [p.psum(f"ps_st{i}", [128, NTOK], F32) for i in range(2)]
        ps_o = p.psum("ps_o", [128, NTOK], F32)
        ps_l = p.psum("ps_l", [1, NTOK], F32)

        v_c1 = p.dma("sync", consts[:, 0:16], b_o, s_c)
        p.op("vector", lambda v: v.memset(onecol[:], 1.0))
        p.op("vector", lambda v: v.memset(onecol_f[:], 1.0))
        p.op("vector", lambda v: v.memset(onerow[:], 1.0))
        p.op("vector", lambda v: v.memset(consts[:, 16:17], 1e-5))
        v_ms = p.op("vector", lambda v: v.memset(consts[:, 17:18], -0.5), sem=ve)
        p.wait("tensor", ve, v_ms)
        v_q = p.dma("sync", sb3(q_sb, NTOK), q_in.rearrange("(j p) t -> p j t", p=128), s_q)

        v_kl = {}
        v_vl = {}
        v_ml = {}
        v_smm = {}
        v_pvmm = {}
        v_msk = {}

        def kidx(h, kk):
            return h * NKS + kk

        v_kl[0] = p.dma("sync", kbuf[0][:], k_in[0:128, 0:128], s_k[0])
        v_vl[0] = p.dma("gpsimd", vbuf[0][:], v_in[0:128, :], s_v[0])
        v_ml[0] = p.dma("sync", mbuf[0][:], m_in[0:128, :], s_m[0])

        p.wait("tensor", s_q, v_q)
        v_ctx = {}
        for h in range(16):
            for kk in range(NKS):
                it = kidx(h, kk)
                nit = it + 1
                if nit < 16 * NKS:
                    nh, nkk = divmod(nit, NKS)
                    if it >= 1:
                        p.wait("sync", mm, v_smm[it - 1])
                        p.wait("gpsimd", mm, v_pvmm[it - 1])
                        p.wait("sync", ve, v_msk[it - 1])
                    v_kl[nit] = p.dma("sync", kbuf[nit % 2][:],
                                      k_in[nh * 128:(nh + 1) * 128, nkk * 128:(nkk + 1) * 128], s_k[nit % 2])
                    v_vl[nit] = p.dma("gpsimd", vbuf[nit % 2][:],
                                      v_in[nkk * 128:(nkk + 1) * 128, nh * 128:(nh + 1) * 128].rearrange("a b -> a b") if False else v_in[nkk * 128:(nkk + 1) * 128, :], s_v[nit % 2])
                    v_ml[nit] = p.dma("sync", mbuf[nit % 2][:],
                                      m_in[nkk * 128:(nkk + 1) * 128, :], s_m[nit % 2])
                pb = ps_st[it % 2]
                p.wait("tensor", s_k[it % 2], v_kl[it])
                if it - 2 >= 0:
                    p.wait("tensor", ac, v_exp[it - 2])
                for tg in range(2):
                    def mmop(t, pb=pb, h=h, kk=kk, tg=tg):
                        return t.matmul(pb[:, tg * TB:(tg + 1) * TB], kbuf[kk if False else (h * NKS + kk) % 2][:],
                                        q_sb[:, h * NTOK + tg * TB:h * NTOK + (tg + 1) * TB], start=True, stop=True)
                    if tg == 1:
                        v_smm[it] = p.op("tensor", mmop, sem=mm)
                    else:
                        p.op("tensor", mmop)
                p.wait("scalar", mm, v_smm[it])
                if it - 1 >= 0:
                    p.wait("scalar", ve, v_msk[it - 1])
                v_exp_cur = p.op("scalar", lambda sc, pb=pb: sc.activation(
                    out=pt_sb[:], in_=pb[:], func=AF.Exp), sem=ac)
                if it == 0:
                    v_exp = {}
                v_exp[it] = v_exp_cur
                p.wait("vector", ac, v_exp[it])
                p.wait("vector", s_m[it % 2], v_ml[it])
                v_msk[it] = p.op("vector", lambda v, it=it: v.tensor_tensor(pt_sb[:], pt_sb[:], mbuf[it % 2][:], ALU.mult), sem=ve)
                if dbg and it == 0:
                    p.wait("sync", ve, v_msk[it])
                    p.dma("sync", pt_dbg, pt_sb[:], s_o)
                    p.wait("tensor", s_o, 16)
                p.wait("tensor", ve, v_msk[it])
                p.wait("tensor", s_v[it % 2], v_vl[it])
                for tg in range(2):
                    p.op("tensor", lambda t, tg=tg, kk=kk, h=h: t.matmul(
                        ps_l[:, tg * TB:(tg + 1) * TB], onecol[:], pt_sb[:, tg * TB:(tg + 1) * TB],
                        start=(kk == 0), stop=(kk == NKS - 1)))
                for tg in range(2):
                    def mmop(t, tg=tg, kk=kk, h=h, it=it):
                        return t.matmul(ps_o[:, tg * TB:(tg + 1) * TB],
                                        vbuf[it % 2][:, h * 128:(h + 1) * 128],
                                        pt_sb[:, tg * TB:(tg + 1) * TB],
                                        start=(kk == 0), stop=(kk == NKS - 1))
                    if tg == 1:
                        v_pvmm[it] = p.op("tensor", mmop, sem=mm)
                    else:
                        p.op("tensor", mmop)
            p.wait("vector", mm, v_pvmm[kidx(h, NKS - 1)])
            if dbg:
                vdbg = p.op("vector", lambda v, h=h: v.tensor_copy(lbc[0:1, :], ps_l[:]), sem=ve)
                p.wait("sync", ve, vdbg)
                p.dma("sync", l_dbg[h:h + 1, :], lbc[0:1, :], s_o)
            v_li = p.op("vector", lambda v: v.reciprocal(linv[:], ps_l[:]), sem=ve)
            p.wait("tensor", ve, v_li)
            vb1 = p.op("tensor", lambda t: t.matmul(ps_st[0][:, 0:512], onerow[:], linv[:, 0:512], start=True, stop=True))
            vb2 = p.op("tensor", lambda t: t.matmul(ps_st[0][:, 512:1024], onerow[:], linv[:, 512:1024], start=True, stop=True), sem=mm)
            p.wait("scalar", mm, vb2)
            vlb = p.op("scalar", lambda sc: sc.activation(out=lbc[:], in_=ps_st[0][:], func=AF.Copy), sem=ac)
            p.wait("vector", ac, vlb)
            v_ctx[h] = p.op("vector", lambda v, h=h: v.tensor_tensor(
                ctx[:, h * NTOK:(h + 1) * NTOK], ps_o[:], lbc[:], ALU.mult), sem=ve)
            p.wait("tensor", ac, vlb)
            p.wait("tensor", ve, v_ctx[h])

        def wstripe_ap(do):
            return w_o[:, do * 128:(do + 1) * 128].rearrange("(k p) e -> p k e", p=128)

        ps_s = [ps_st[0], ps_l]
        v_wl = {0: None}
        v_xl = {0: None}
        p.wait("gpsimd", ve, v_ctx[15])
        p.wait("sync", ve, v_ctx[15])
        v_wl[0] = p.dma("gpsimd", sb3(wbuf[0], 128), wstripe_ap(0), s_wo[0])
        v_xl[0] = p.dma("sync", xbuf[0][:], x_res[0:128, :], s_xr[0])
        v_wmm = {}
        v_res = {}
        v_sq = {}
        v_smm2 = {}
        for do in range(16):
            if do + 1 < 16:
                if do >= 1:
                    p.wait("gpsimd", mm, v_wmm[(do - 1, 1)])
                    p.wait("sync", ve, v_res[(do - 1, 1)])
                v_wl[do + 1] = p.dma("gpsimd", sb3(wbuf[(do + 1) % 2], 128), wstripe_ap(do + 1), s_wo[(do + 1) % 2])
                v_xl[do + 1] = p.dma("sync", xbuf[(do + 1) % 2][:], x_res[(do + 1) * 128:(do + 2) * 128, :], s_xr[(do + 1) % 2])
            p.wait("tensor", s_wo[do % 2], v_wl[do])
            for tg in range(2):
                pb = ps_o[:, tg * 512:(tg + 1) * 512]
                if do >= 1:
                    p.wait("tensor", ve, v_res[(do - 1, tg)])
                for k in range(16):
                    def mmop(t, pb=pb, do=do, k=k, tg=tg, s=(k == 0), z=(k == 15)):
                        return t.matmul(pb, wbuf[do % 2][:, k * 128:(k + 1) * 128],
                                        ctx[:, k * NTOK + tg * 512:k * NTOK + tg * 512 + 512], start=s, stop=z)
                    if k == 15:
                        v_wmm[(do, tg)] = p.op("tensor", mmop, sem=mm)
                    else:
                        p.op("tensor", mmop)
                p.wait("vector", mm, v_wmm[(do, tg)])
                p.wait("vector", s_xr[do % 2], v_xl[do])
                v_res[(do, tg)] = p.op("vector", lambda v, pb=pb, do=do, tg=tg: v.scalar_tensor_tensor(
                    out=r_sb[:, do * NTOK + tg * 512:do * NTOK + (tg + 1) * 512], in0=pb,
                    scalar=consts[:, do:do + 1], in1=xbuf[do % 2][:, tg * 512:(tg + 1) * 512],
                    op0=ALU.add, op1=ALU.add), sem=ve)
                p.wait("scalar", ve, v_res[(do, tg)])
                if (do, tg) != (0, 0):
                    p.wait("scalar", mm, v_smm2.get((do * 2 + tg) - 1, 0))
                v_sq[(do, tg)] = p.op("scalar", lambda sc, do=do, tg=tg: sc.activation(
                    out=sq[:], in_=r_sb[:, do * NTOK + tg * 512:do * NTOK + (tg + 1) * 512], func=AF.Square), sem=ac)
                p.wait("tensor", ve, v_res[(do, tg)])
                p.op("tensor", lambda t, do=do, tg=tg, s=(do == 0), z=(do == 15): t.matmul(
                    ps_l[:, tg * 512:(tg + 1) * 512], onecol_f[:],
                    r_sb[:, do * NTOK + tg * 512:do * NTOK + (tg + 1) * 512], start=s, stop=z))
                p.wait("tensor", ac, v_sq[(do, tg)])
                v_smm2[do * 2 + tg] = p.op("tensor", lambda t, do=do, tg=tg, s=(do == 0), z=(do == 15): t.matmul(
                    ps_st[1][0:1, tg * 512:(tg + 1) * 512], onecol_f[:], sq[:], start=s, stop=z), sem=mm)

        R = NTOK
        p.wait("vector", mm, v_smm2[31])
        p.op("vector", lambda v: v.tensor_scalar_mul(rows[:, 0:R], ps_l[:], 1.0 / DM))
        p.op("vector", lambda v: v.tensor_scalar_mul(rows[:, R:2 * R], ps_st[1][0:1, 0:NTOK], 1.0 / DM))
        p.op("vector", lambda v: v.tensor_tensor(rows[:, 2 * R:3 * R], rows[:, 0:R], rows[:, 0:R], ALU.mult))
        v_var = p.op("vector", lambda v: v.tensor_tensor(rows[:, R:2 * R], rows[:, R:2 * R], rows[:, 2 * R:3 * R], ALU.subtract), sem=ve)
        p.wait("scalar", ve, v_var)
        p.op("scalar", lambda sc: sc.activation(out=rows[:, R:2 * R], in_=rows[:, R:2 * R], func=AF.Ln, bias=consts[0:1, 16:17]))
        v_rstd = p.op("scalar", lambda sc: sc.activation(out=rows[:, R:2 * R], in_=rows[:, R:2 * R], func=AF.Exp, scale=consts[0:1, 17:18]), sem=ac)
        p.wait("vector", ac, v_rstd)
        v_mr = p.op("vector", lambda v: v.tensor_tensor(rows[:, 2 * R:3 * R], rows[:, 0:R], rows[:, R:2 * R], ALU.mult), sem=ve)
        p.wait("tensor", ve, v_mr)
        v_bc = {}
        for tgb in range(2):
            if tgb > 0:
                p.wait("tensor", ac, v_bc[tgb - 1])
            v1 = p.op("tensor", lambda t, tgb=tgb: t.matmul(
                ps_o[:, 0:512], onerow[:], rows[:, R + tgb * 512:R + (tgb + 1) * 512], start=True, stop=True), sem=mm)
            p.wait("scalar", mm, v1)
            vba = p.op("scalar", lambda sc, tgb=tgb: sc.activation(out=bca[:, tgb * 512:(tgb + 1) * 512], in_=ps_o[:, 0:512], func=AF.Copy), sem=ac)
            p.wait("tensor", ac, vba)
            v2 = p.op("tensor", lambda t, tgb=tgb: t.matmul(
                ps_o[:, 0:512], onerow[:], rows[:, 2 * R + tgb * 512:2 * R + (tgb + 1) * 512], start=True, stop=True), sem=mm)
            p.wait("scalar", mm, v2)
            v_bc[tgb] = p.op("scalar", lambda sc, tgb=tgb: sc.activation(out=bcb[:, tgb * 512:(tgb + 1) * 512], in_=ps_o[:, 0:512], func=AF.Copy), sem=ac)
        p.wait("vector", ac, v_bc[1])
        for do in range(16):
            if do - 1 >= 0:
                p.wait("vector", s_o, 16 * do)
            p.op("vector", lambda v, do=do: v.tensor_tensor(stgx[:], r_sb[:, do * NTOK:(do + 1) * NTOK], bca[:], ALU.mult))
            vx1 = p.op("vector", lambda v, do=do: v.tensor_tensor(stgx[:], stgx[:], bcb[:], ALU.subtract), sem=ve)
            p.wait("sync", ve, vx1)
            p.dma("sync", x3_out[do * 128:(do + 1) * 128, :], stgx[:], s_o)
        p.wait("sync", s_o, s_o.v)
        pass
    p.stack = _oldstack
    return


def build_l5_attn(dbg=False):
    nc = bass.Bass("TRN2", target_bir_lowering=False, debug=False)
    stack0 = contextlib.ExitStack()
    with stack0:
        p = Prog(nc, stack0)
        D = declare_l5(nc)
        emit_l5(nc, p, D)
        p.finalize()
    return nc

def host_inputs_l5(core, I, q_by_core, k_full, v_full, x3res_by_core):
    b0, b1 = blocks_of(core)
    nsub = (max(b0, b1) + 1) * 4
    idx = list(range(nsub)) + [0] * (NKS - nsub)
    assert len(idx) == NKS and nsub <= NKS, (nsub, NKS)
    toks = np.r_[b0 * TB:(b0 + 1) * TB, b1 * TB:(b1 + 1) * TB]
    kperm = np.empty((DM, NKS * 128), dtype=BF)
    vperm = np.empty((NKS * 128, DM), dtype=BF)
    masks = np.zeros((NKS * 128, NTOK), dtype=BF)
    for i, sid in enumerate(idx):
        kperm[:, i * 128:(i + 1) * 128] = k_full[:, sid * 128:(sid + 1) * 128]
        vperm[i * 128:(i + 1) * 128, :] = v_full[sid * 128:(sid + 1) * 128, :]
        if i < nsub:
            kpos = np.arange(sid * 128, (sid + 1) * 128)
            masks[i * 128:(i + 1) * 128, :] = (kpos[:, None] <= toks[None, :]).astype(BF)
    return {
        "q_in": q_by_core[core],
        "k_in": kperm, "v_in": vperm, "m_in": masks,
        "x_res": x3res_by_core[core],
        "w_o": I["L1_attn_W_o"],
        "b_o": plan128(I["L1_attn_b_o"]),
    }


def declare_l6a(nc, internal=(), alias=None):
    D = {}
    alias = alias or {}
    if "wgu" in alias:
        D["wgu"] = alias["wgu"]
    else:
        k = "Internal" if "wgu" in internal else "ExternalInput"
        D["wgu"] = nc.dram_tensor("l6a_wgu", [2048, 2048], F32, kind=k).ap()
    if "wp" in alias:
        D["wp"] = alias["wp"]
    else:
        k = "Internal" if "wp" in internal else "ExternalInput"
        D["wp"] = nc.dram_tensor("l6a_wp", [256, 8192], F32, kind=k).ap()
    if "qgu" in alias:
        D["qgu"] = alias["qgu"]
    else:
        k = "Internal" if "qgu" in internal else "ExternalOutput"
        D["qgu"] = nc.dram_tensor("l6a_qgu", [2048, 2048], BF16, kind=k).ap()
    if "qp" in alias:
        D["qp"] = alias["qp"]
    else:
        k = "Internal" if "qp" in internal else "ExternalOutput"
        D["qp"] = nc.dram_tensor("l6a_qp", [256, 8192], BF16, kind=k).ap()
    return D


def emit_l6a(nc, p, D):
    stack = contextlib.ExitStack()
    _oldstack = p.stack
    p.stack = stack
    wgu = D["wgu"]
    wp = D["wp"]
    qgu = D["qgu"]
    qp = D["qp"]

    with stack:
        pass
        ve = p.sem("ve")
        ac = p.sem("ac")
        s_i = p.sem("s_i")
        s_o = p.sem("s_o")

        w_sb = [p.sbuf(f"w_sb{i}", [128, 2048], F32) for i in range(2)]
        wb = p.sbuf("wb", [128, 2048], BF16)
        ab = p.sbuf("ab", [128, 2048], F32)
        gsum = p.sbuf("gsum", [128, 32], F32)
        gsb = p.sbuf("gsb", [128, 32], BF16)
        grc = p.sbuf("grc", [128, 32], F32)
        tb = p.sbuf("tb", [128, 2048], BF16)
        ta = p.sbuf("ta", [128, 2048], F32)
        msk = p.sbuf("msk", [128, 2048], F32)
        sgn = p.sbuf("sgn", [128, 2048], F32)
        qv = p.sbuf("qv", [128, 2048], F32)
        outb = p.sbuf("outb", [128, 2048], BF16)

        v_ld = {}
        v_st = {}
        v_done = {}
        tiles = [("wgu", wgu, qgu, 16, 2048), ("wp", wp, qp, 2, 8192)]
        it = 0
        for name, src, dst, nt, W in tiles:
            for ti in range(nt):
                pass
        jobs = []
        for ti in range(16):
            jobs.append((wgu, qgu, ti * 128, 0, 2048))
        for ti in range(2):
            for cc in range(4):
                jobs.append((wp, qp, ti * 128, cc * 2048, 2048))
        for jx, (src, dst, r0, c0, W) in enumerate(jobs):
            buf = w_sb[jx % 2]
            if jx >= 2:
                p.wait("sync", ve, v_done[jx - 2])
            v_ld[jx] = p.dma("sync", buf[:], src[r0:r0 + 128, c0:c0 + W], s_i)
            p.wait("vector", s_i, v_ld[jx])
            if jx >= 1:
                p.wait("vector", ve, v_done.get(jx - 1, 0))
            p.op("vector", lambda v, buf=buf: v.tensor_copy(wb[:], buf[:]))
            v_wb = p.op("vector", lambda v: v.tensor_copy(wb[:, 0:1], wb[:, 0:1]), sem=ve)
            p.wait("scalar", ve, v_wb)
            v_ab = p.op("scalar", lambda sc: sc.activation(out=ab[:], in_=wb[:], func=AF.Abs), sem=ac)
            p.wait("vector", ac, v_ab)
            ag = ab[:].rearrange("p (g s) -> p g s", s=64)
            vr1 = p.op("vector", lambda v, ag=ag: v.tensor_reduce(gsum[:], ab[:].rearrange("p (g s) -> p g s", s=64) if False else ag, axis=AX.X, op=ALU.add), sem=ve)
            p.wait("vector", ve, vr1)
            vr2 = p.op("vector", lambda v: v.tensor_scalar_mul(gsum[:], gsum[:], 1.0 / 64.0), sem=ve)
            p.wait("vector", ve, vr2)
            vr3 = p.op("vector", lambda v: v.tensor_copy(gsb[:], gsum[:]), sem=ve)
            p.wait("vector", ve, vr3)
            vr4 = p.op("vector", lambda v: v.tensor_scalar(out=gsum[:], in0=gsb[:], scalar1=1e-8, scalar2=None, op0=ALU.max), sem=ve)
            p.wait("vector", ve, vr4)
            v_rc = p.op("vector", lambda v: v.reciprocal(grc[:], gsum[:]), sem=ve)
            p.wait("vector", ve, v_rc)
            grx = grc[:].rearrange("p g -> p g 1" if False else "p (g o) -> p g o", o=1)
            for g in range(32):
                p.op("vector", lambda v, g=g: v.tensor_scalar(
                    out=tb[:, g * 64:(g + 1) * 64], in0=wb[:, g * 64:(g + 1) * 64],
                    scalar1=grc[:, g:g + 1], scalar2=None, op0=ALU.mult))
            v_tb = p.op("vector", lambda v: v.tensor_copy(tb[:, 0:1], tb[:, 0:1]), sem=ve)
            p.wait("scalar", ve, v_tb)
            p.op("scalar", lambda sc: sc.activation(out=ta[:], in_=tb[:], func=AF.Abs))
            v_sg = p.op("scalar", lambda sc: sc.activation(out=sgn[:], in_=tb[:], func=AF.Sign), sem=ac)
            p.wait("vector", ac, v_sg)
            p.op("vector", lambda v: v.tensor_single_scalar(out=msk[:], in_=ta[:], scalar=0.5, op=ALU.is_gt))
            p.op("vector", lambda v: v.tensor_tensor(qv[:], msk[:], sgn[:], ALU.mult))
            for g in range(32):
                p.op("vector", lambda v, g=g: v.tensor_scalar(
                    out=outb[:, g * 64:(g + 1) * 64], in0=qv[:, g * 64:(g + 1) * 64],
                    scalar1=gsum[:, g:g + 1], scalar2=None, op0=ALU.mult))
            v_done[jx] = p.op("vector", lambda v: v.tensor_copy(outb[:, 0:1], outb[:, 0:1]), sem=ve)
            p.wait("sync", ve, v_done[jx])
            v_st[jx] = p.dma("sync", dst[r0:r0 + 128, c0:c0 + W], outb[:], s_o)
        p.wait("sync", s_o, s_o.v)
        pass
    p.stack = _oldstack
    return


def build_l6a_quant():
    nc = bass.Bass("TRN2", target_bir_lowering=False, debug=False)
    stack0 = contextlib.ExitStack()
    with stack0:
        p = Prog(nc, stack0)
        D = declare_l6a(nc)
        emit_l6a(nc, p, D)
        p.finalize()
    return nc

def host_inputs_l6a(core, I):
    return {
        "wgu": np.ascontiguousarray(I["L1_ffn_w_gate_up"][core * 2048:(core + 1) * 2048]),
        "wp": np.ascontiguousarray(I["L1_ffn_w_proj"][core * 256:(core + 1) * 256]),
    }


def declare_l6b(nc, internal=(), alias=None):
    D = {}
    alias = alias or {}
    if "x_in" in alias:
        D["x_in"] = alias["x_in"]
    else:
        k = "Internal" if "x_in" in internal else "ExternalInput"
        D["x_in"] = nc.dram_tensor("l6b_x_in", [DM, NTOK], F32, kind=k).ap()
    if "wguT" in alias:
        D["wguT"] = alias["wguT"]
    else:
        k = "Internal" if "wguT" in internal else "ExternalInput"
        D["wguT"] = nc.dram_tensor("l6b_wguT", [DM, 2 * IN8], BF16, kind=k).ap()
    if "wpT" in alias:
        D["wpT"] = alias["wpT"]
    else:
        k = "Internal" if "wpT" in internal else "ExternalInput"
        D["wpT"] = nc.dram_tensor("l6b_wpT", [IN8, DM], BF16, kind=k).ap()
    if "out_o" in alias:
        D["out_o"] = alias["out_o"]
    else:
        k = "Internal" if "out_o" in internal else "ExternalOutput"
        D["out_o"] = nc.dram_tensor("l6b_out_o", [DM, NTOK], F32, kind=k).ap()
    return D


def emit_l6b(nc, p, D):
    stack = contextlib.ExitStack()
    _oldstack = p.stack
    p.stack = stack

    x_in = D["x_in"]
    wguT = D["wguT"]
    wpT = D["wpT"]
    out_o = D["out_o"]

    with stack:
        pass
        mm = p.sem("mm")
        ve = p.sem("ve")
        ac = p.sem("ac")
        s_xb = p.sem("s_xb")
        s_xf = p.sem("s_xf")
        s_wg = [p.sem("s_wg0"), p.sem("s_wg1")]
        s_wu = [p.sem("s_wu0"), p.sem("s_wu1")]
        s_wd = [p.sem("s_wd0"), p.sem("s_wd1")]
        s_o = p.sem("s_o")

        consts = p.sbuf("consts", [128, 3], F32)
        onecol = p.sbuf("onecol", [128, 1], F32)
        onerow = p.sbuf("onerow", [1, 128], F32)
        x1b = p.sbuf("x1b", [128, 16 * TB], BF16)
        fuse = p.sbuf("fuse", [128, 64 * TB], BF16)
        r_sb = p.sbuf("r_sb", [128, 16 * NTOK], F32)
        rmsrow = p.sbuf("rmsrow", [1, NTOK], F32)
        rmsbc = p.sbuf("rmsbc", [128, TB], F32)
        sq = p.sbuf("sq", [128, 512], F32)
        stPH = contextlib.ExitStack()
        EP = stPH.enter_context
        wgbuf = [EP(nc.sbuf_tensor(f"{p.pre}wgbuf{i}", [128, 16 * 128], BF16)) for i in range(2)]
        wubuf = [EP(nc.sbuf_tensor(f"{p.pre}wubuf{i}", [128, 16 * 128], BF16)) for i in range(2)]
        wdbuf = [EP(nc.sbuf_tensor(f"{p.pre}wdbuf{i}", [128, DM], BF16)) for i in range(2)]
        gbuf = [EP(nc.sbuf_tensor(f"{p.pre}gbuf{i}", [128, TB], BF16)) for i in range(2)]
        ubuf = [EP(nc.sbuf_tensor(f"{p.pre}ubuf{i}", [128, TB], BF16)) for i in range(2)]
        xfb = EP(nc.sbuf_tensor(f"{p.pre}xfb", [128, 16 * 256], F32))

        psD = [p.psum(f"psD{i}", [128, 512], F32) for i in range(8)]

        p.op("vector", lambda v: v.memset(onecol[:], 1.0))
        p.op("vector", lambda v: v.memset(onerow[:], 1.0))
        p.op("vector", lambda v: v.memset(consts[:, 0:1], 1.1920929e-7))
        p.op("vector", lambda v: v.memset(consts[:, 2:3], 1e-5))
        v_ms = p.op("vector", lambda v: v.memset(consts[:, 1:2], -0.5), sem=ve)

        def cstripe(e):
            return wguT[:, e * 128:(e + 1) * 128].rearrange("(k p) e -> p k e", p=128)

        def ustripe(e):
            return wguT[:, IN8 + e * 128:IN8 + (e + 1) * 128].rearrange("(k p) e -> p k e", p=128)

        v_wgl = {}
        v_wul = {}
        v_gmm = {}
        v_umm = {}
        v_fu = {}
        v_wgl[(0, 0)] = p.dma("sync", sb3(wgbuf[0], 128), cstripe(0), s_wg[0])
        v_wul[(0, 0)] = p.dma("sync", sb3(wubuf[0], 128), ustripe(0), s_wu[0])
        p.wait("tensor", ve, v_ms)

        v_res = {}
        v_dn_done = None
        v_sqs = {}
        for tg in range(2):
            t0 = tg * TB
            if tg == 1:
                p.wait("gpsimd", mm, v_umm[(0, 63)])
            v_xb = p.dma("gpsimd", sb3(x1b, TB), x_in[:, t0:t0 + TB].rearrange("(j p) t -> p j t", p=128), s_xb)
            p.wait("tensor", s_xb, v_xb)
            for e in range(64):
                key = (tg, e)
                nkey = (tg, e + 1) if e + 1 < 64 else ((tg + 1, 0) if tg == 0 else None)
                if nkey is not None:
                    if (tg, e) != (0, 0):
                        p.wait("sync", mm, v_umm[pkey])
                    v_wgl[nkey] = p.dma("sync", sb3(wgbuf[(e + 1) % 2], 128),
                                        cstripe(nkey[1]) if nkey[0] == tg else cstripe(0), s_wg[(e + 1) % 2])
                    v_wul[nkey] = p.dma("sync", sb3(wubuf[(e + 1) % 2], 128),
                                        ustripe(nkey[1]) if nkey[0] == tg else ustripe(0), s_wu[(e + 1) % 2])
                p.wait("tensor", s_wg[e % 2], v_wgl[key])
                p.wait("tensor", s_wu[e % 2], v_wul[key])
                if (tg, e) != (0, 0):
                    p.wait("tensor", ac, v_ev_prev)
                if tg == 1 and e == 0:
                    p.wait("tensor", ve, v_dn_done)
                for k in range(16):
                    def mmop(t, e=e, k=k, s=(k == 0), z=(k == 15)):
                        return t.matmul(psD[0][:], wgbuf[e % 2][:, k * 128:(k + 1) * 128],
                                        x1b[:, k * TB:(k + 1) * TB], start=s, stop=z)
                    if k == 15:
                        v_gmm[key] = p.op("tensor", mmop, sem=mm)
                    else:
                        p.op("tensor", mmop)
                for k in range(16):
                    def mmop(t, e=e, k=k, s=(k == 0), z=(k == 15)):
                        return t.matmul(psD[1][:], wubuf[e % 2][:, k * 128:(k + 1) * 128],
                                        x1b[:, k * TB:(k + 1) * TB], start=s, stop=z)
                    if k == 15:
                        v_umm[key] = p.op("tensor", mmop, sem=mm)
                    else:
                        p.op("tensor", mmop)
                pkey = key
                p.wait("scalar", mm, v_gmm[key])
                if (tg, e) != (0, 0):
                    p.wait("scalar", ve, v_fu[pfkey])
                p.op("scalar", lambda sc, e=e: sc.activation(out=gbuf[e % 2][:], in_=psD[0][:], func=AF.Silu))
                p.wait("scalar", mm, v_umm[key])
                v_ev_prev = p.op("scalar", lambda sc, e=e: sc.activation(out=ubuf[e % 2][:], in_=psD[1][:], func=AF.Copy), sem=ac)
                p.wait("vector", ac, v_ev_prev)
                v_fu[key] = p.op("vector", lambda v, e=e: v.tensor_tensor(
                    planar(fuse, e, TB), gbuf[e % 2][:], ubuf[e % 2][:], ALU.mult), sem=ve)
                pfkey = key
                p.wait("scalar", ve, v_fu[key])
                if (tg, e) != (0, 0):
                    p.wait("scalar", mm, v_sqs[pskey])
                v_sq = p.op("scalar", lambda sc, e=e: sc.activation(out=sq[:], in_=planar(fuse, e, TB), func=AF.Square), sem=ac)
                p.wait("tensor", ac, v_sq)
                v_sqs[key] = p.op("tensor", lambda t, e=e, s=(e == 0), z=(e == 63): t.matmul(
                    psD[7][0:1, :], onecol[:], sq[:], start=s, stop=z), sem=mm)
                pskey = key
            p.wait("vector", mm, v_sqs[(tg, 63)])
            p.op("vector", lambda v, t0=t0: v.tensor_scalar_mul(rmsrow[:, t0:t0 + TB], psD[7][0:1, :], 1.0 / IN8))
            v_rr = p.op("vector", lambda v, t0=t0: v.tensor_copy(rmsrow[:, t0:t0 + 1], rmsrow[:, t0:t0 + 1]), sem=ve)
            p.wait("scalar", ve, v_rr)
            p.op("scalar", lambda sc, t0=t0: sc.activation(out=rmsrow[:, t0:t0 + TB], in_=rmsrow[:, t0:t0 + TB], func=AF.Ln, bias=consts[0:1, 0:1]))
            v_rstd = p.op("scalar", lambda sc, t0=t0: sc.activation(out=rmsrow[:, t0:t0 + TB], in_=rmsrow[:, t0:t0 + TB], func=AF.Exp, scale=consts[0:1, 1:2]), sem=ac)
            p.wait("tensor", ac, v_rstd)
            vb1 = p.op("tensor", lambda t, t0=t0: t.matmul(psD[2][:], onerow[:], rmsrow[:, t0:t0 + TB], start=True, stop=True), sem=mm)
            p.wait("scalar", mm, vb1)
            v_rbc = p.op("scalar", lambda sc: sc.activation(out=rmsbc[:], in_=psD[2][:], func=AF.Copy), sem=ac)
            p.wait("vector", ac, v_rbc)
            v_hn = {}
            for e in range(64):
                v_hn[e] = p.op("vector", lambda v, e=e: v.tensor_tensor(
                    planar(fuse, e, TB), planar(fuse, e, TB), rmsbc[:], ALU.mult), sem=ve)

            for dmh in range(2):
                if not (tg == 0 and dmh == 0):
                    p.wait("sync", ve, v_res_last)
                v_x = p.dma("sync", sb3(xfb, TB),
                            x_in[dmh * 1024:(dmh + 1) * 1024, t0:t0 + TB].rearrange("(j p) t -> p j t", p=128), s_xf)
                v_wdl = {}
                p.wait("gpsimd", ve, v_hn[63])
                if not (tg == 0 and dmh == 0):
                    p.wait("gpsimd", ve, v_res_last)
                v_wdl[0] = p.dma("gpsimd", wdbuf[0][:], wpT[0:128, :], s_wd[0])
                v_dmm = {}
                for e in range(64):
                    if e + 1 < 64:
                        if e >= 1:
                            p.wait("gpsimd", mm, v_dmm[e - 1])
                        v_wdl[e + 1] = p.dma("gpsimd", wdbuf[(e + 1) % 2][:], wpT[(e + 1) * 128:(e + 2) * 128, :], s_wd[(e + 1) % 2])
                    p.wait("tensor", s_wd[e % 2], v_wdl[e])
                    if e == 0:
                        p.wait("tensor", ve, v_hn[63])
                        if not (tg == 0 and dmh == 0):
                            p.wait("tensor", ve, v_res_last)
                    for d8 in range(8):
                        def mmop(t, e=e, d8=d8, dmh=dmh, s=(e == 0), z=(e == 63)):
                            return t.matmul(psD[d8][:],
                                            wdbuf[e % 2][:, (dmh * 8 + d8) * 128:(dmh * 8 + d8 + 1) * 128],
                                            fuse[:, e * TB:(e + 1) * TB], start=s, stop=z)
                        if d8 == 7:
                            v_dmm[e] = p.op("tensor", mmop, sem=mm)
                        else:
                            p.op("tensor", mmop)
                p.wait("vector", mm, v_dmm[63])
                p.wait("vector", s_xf, v_x)
                ic = 2 * tg + dmh
                for d8 in range(8):
                    dm = dmh * 8 + d8
                    v_res[(ic, d8)] = p.op("vector", lambda v, dm=dm, d8=d8, t0=t0: v.tensor_tensor(
                        r_sb[:, dm * NTOK + t0:dm * NTOK + t0 + TB],
                        psD[d8][:],
                        planar(xfb, d8, TB), ALU.add), sem=ve)
                v_res_last = v_res[(ic, 7)]
            v_dn_done = v_res_last

        stPH.close()
        stT = contextlib.ExitStack()
        ET = stT.enter_context
        sqb = ET(nc.sbuf_tensor(f"{p.pre}sqb", [128, 512], F32))
        rows = ET(nc.sbuf_tensor(f"{p.pre}rows", [1, 3 * NTOK], F32))
        bca = ET(nc.sbuf_tensor(f"{p.pre}bca", [128, NTOK], F32))
        bcb = ET(nc.sbuf_tensor(f"{p.pre}bcb", [128, NTOK], F32))
        stgx = ET(nc.sbuf_tensor(f"{p.pre}stgx", [128, NTOK], F32))
        R = NTOK
        v_sqp = {}
        for dm in range(16):
            p.wait("scalar", ve, v_res[(3, 7)])
            if dm >= 1:
                p.wait("scalar", mm, v_sqp[dm - 1])
            v_sq2 = p.op("scalar", lambda sc, dm=dm: sc.activation(
                out=sq[:], in_=r_sb[:, dm * NTOK:dm * NTOK + 512], func=AF.Square), sem=ac)
            v_sq3 = p.op("scalar", lambda sc, dm=dm: sc.activation(
                out=sqb[:], in_=r_sb[:, dm * NTOK + 512:dm * NTOK + NTOK], func=AF.Square), sem=ac)
            if dm == 0:
                p.wait("tensor", ve, v_res[(3, 7)])
            for tgb in range(2):
                p.op("tensor", lambda t, dm=dm, tgb=tgb, s=(dm == 0), z=(dm == 15): t.matmul(
                    psD[tgb][0:1, :], onecol[:], r_sb[:, dm * NTOK + tgb * 512:dm * NTOK + (tgb + 1) * 512],
                    start=s, stop=z))
            p.wait("tensor", ac, v_sq3)
            p.op("tensor", lambda t, dm=dm, s=(dm == 0), z=(dm == 15): t.matmul(
                psD[2][0:1, :], onecol[:], sq[:], start=s, stop=z))
            v_sqp[dm] = p.op("tensor", lambda t, dm=dm, s=(dm == 0), z=(dm == 15): t.matmul(
                psD[3][0:1, :], onecol[:], sqb[:], start=s, stop=z), sem=mm)
        v_stats = v_sqp[15]
        p.wait("vector", mm, v_stats)
        p.op("vector", lambda v: v.tensor_copy(rows[:, 0:512], psD[0][0:1, :]))
        p.op("vector", lambda v: v.tensor_copy(rows[:, 512:1024], psD[1][0:1, :]))
        p.op("vector", lambda v: v.tensor_copy(rows[:, R:R + 512], psD[2][0:1, :]))
        p.op("vector", lambda v: v.tensor_copy(rows[:, R + 512:R + 1024], psD[3][0:1, :]))
        p.op("vector", lambda v: v.tensor_scalar_mul(rows[:, 0:R], rows[:, 0:R], 1.0 / DM))
        p.op("vector", lambda v: v.tensor_scalar_mul(rows[:, R:2 * R], rows[:, R:2 * R], 1.0 / DM))
        p.op("vector", lambda v: v.tensor_tensor(rows[:, 2 * R:3 * R], rows[:, 0:R], rows[:, 0:R], ALU.mult))
        v_var = p.op("vector", lambda v: v.tensor_tensor(rows[:, R:2 * R], rows[:, R:2 * R], rows[:, 2 * R:3 * R], ALU.subtract), sem=ve)
        p.wait("scalar", ve, v_var)
        p.op("scalar", lambda sc: sc.activation(out=rows[:, R:2 * R], in_=rows[:, R:2 * R], func=AF.Ln, bias=consts[0:1, 2:3]))
        v_rstd2 = p.op("scalar", lambda sc: sc.activation(out=rows[:, R:2 * R], in_=rows[:, R:2 * R], func=AF.Exp, scale=consts[0:1, 1:2]), sem=ac)
        p.wait("vector", ac, v_rstd2)
        v_mr = p.op("vector", lambda v: v.tensor_tensor(rows[:, 2 * R:3 * R], rows[:, 0:R], rows[:, R:2 * R], ALU.mult), sem=ve)
        p.wait("tensor", ve, v_mr)
        v_bc = {}
        for tgb in range(2):
            if tgb > 0:
                p.wait("tensor", ac, v_bc[tgb - 1])
            v1 = p.op("tensor", lambda t, tgb=tgb: t.matmul(
                psD[4][:], onerow[:], rows[:, R + tgb * 512:R + (tgb + 1) * 512], start=True, stop=True), sem=mm)
            p.wait("scalar", mm, v1)
            vba = p.op("scalar", lambda sc, tgb=tgb: sc.activation(out=bca[:, tgb * 512:(tgb + 1) * 512], in_=psD[4][:], func=AF.Copy), sem=ac)
            p.wait("tensor", ac, vba)
            v2 = p.op("tensor", lambda t, tgb=tgb: t.matmul(
                psD[4][:], onerow[:], rows[:, 2 * R + tgb * 512:2 * R + (tgb + 1) * 512], start=True, stop=True), sem=mm)
            p.wait("scalar", mm, v2)
            v_bc[tgb] = p.op("scalar", lambda sc, tgb=tgb: sc.activation(out=bcb[:, tgb * 512:(tgb + 1) * 512], in_=psD[4][:], func=AF.Copy), sem=ac)
        p.wait("vector", ac, v_bc[1])
        for do in range(16):
            if do - 1 >= 0:
                p.wait("vector", s_o, 16 * do)
            p.op("vector", lambda v, do=do: v.tensor_tensor(stgx[:], r_sb[:, do * NTOK:(do + 1) * NTOK], bca[:], ALU.mult))
            vx1 = p.op("vector", lambda v, do=do: v.tensor_tensor(stgx[:], stgx[:], bcb[:], ALU.subtract), sem=ve)
            p.wait("sync", ve, vx1)
            p.dma("sync", out_o[do * 128:(do + 1) * 128, :], stgx[:], s_o)
        p.wait("sync", s_o, s_o.v)
        stT.close()
        pass
    p.stack = _oldstack
    return


def build_l6b_ffn1():
    nc = bass.Bass("TRN2", target_bir_lowering=False, debug=False)
    stack0 = contextlib.ExitStack()
    with stack0:
        p = Prog(nc, stack0)
        D = declare_l6b(nc)
        emit_l6b(nc, p, D)
        p.finalize()
    return nc

def host_inputs_l6b(core, I, x3_by_core, wq_gu, wq_p):
    return {
        **({"x_in": x3_by_core[core]} if x3_by_core is not None else {}),
        "wguT": np.ascontiguousarray(wq_gu.T),
        "wpT": np.ascontiguousarray(wq_p.T),
    }


def _gather_tokens(per_core, width_axis):
    rows = per_core[0].shape[0]
    out = np.empty((rows, L), dtype=per_core[0].dtype)
    for c in range(NCORES):
        b0, b1 = blocks_of(c)
        out[:, b0 * TB:(b0 + 1) * TB] = per_core[c][:, 0:TB]
        out[:, b1 * TB:(b1 + 1) * TB] = per_core[c][:, TB:2 * TB]
    return out


def _pref(tag, d):
    return {f"{tag}_{k}": v for k, v in d.items()}


def build_groupA():
    nc = bass.Bass("TRN2", target_bir_lowering=False, debug=False)
    st = contextlib.ExitStack()
    with st:
        p = Prog(nc, st)
        p.pre = "A_"
        D1 = declare_l1(nc)
        emit_l1(nc, p, D1)
        p.fence()
        p.pre = "Q_"
        D6 = declare_l6a(nc)
        emit_l6a(nc, p, D6)
        p.finalize()
    return nc


def build_groupB():
    nc = bass.Bass("TRN2", target_bir_lowering=False, debug=False)
    st = contextlib.ExitStack()
    with st:
        p = Prog(nc, st)
        p.pre = "B_"
        D2 = declare_l2(nc, internal=("x1_out",))
        emit_l2(nc, p, D2)
        p.fence()
        p.pre = "C_"
        D3 = declare_l3(nc, alias={"x_in": D2["x1_out"]})
        emit_l3(nc, p, D3)
        p.fence()
        p.pre = "D_"
        D4 = declare_l4(nc, alias={"x_in": D3["x2_out"]})
        emit_l4(nc, p, D4)
        p.finalize()
    return nc


def build_groupC():
    nc = bass.Bass("TRN2", target_bir_lowering=False, debug=False)
    st = contextlib.ExitStack()
    with st:
        p = Prog(nc, st)
        p.pre = "E_"
        D5 = declare_l5(nc, internal=("x3_out",))
        emit_l5(nc, p, D5)
        p.fence()
        p.pre = "G_"
        D6 = declare_l6b(nc, alias={"x_in": D5["x3_out"]})
        emit_l6b(nc, p, D6)
        p.finalize()
    return nc


def kernel(**I):
    cores = list(range(NCORES))

    nc = build_groupA()
    inA = [dict(**_pref("l1", host_inputs_l1(c, I)), **_pref("l6a", host_inputs_l6a(c, I))) for c in cores]
    rA = run_bass_kernel_spmd(nc, inA, core_ids=cores)
    l1o = [{k[3:]: v for k, v in r.items() if k.startswith("l1_")} for r in rA.results]
    wq_gu = np.concatenate([np.asarray(rA.results[c]["l6a_qgu"]) for c in cores], axis=0)
    wq_p = np.concatenate([np.asarray(rA.results[c]["l6a_qp"]) for c in cores], axis=0)

    sums_all = host_sums_all(l1o)
    nc = build_groupB()
    inB = []
    for c in cores:
        d2 = host_inputs_l2(c, I, l1o, sums_all)
        d3 = host_inputs_l3(c, I, None)
        d3.pop("x_in", None)
        d4 = host_inputs_l4(c, I, None)
        d4.pop("x_in", None)
        inB.append(dict(**_pref("l2", d2), **_pref("l3", d3), **_pref("l4", d4)))
    rB = run_bass_kernel_spmd(nc, inB, core_ids=cores)
    x2_by_core = {c: rB.results[c]["l3_x2_out"] for c in cores}
    q_by_core = {c: rB.results[c]["l4_q_out"] for c in cores}
    k_full = _gather_tokens([rB.results[c]["l4_k_out"] for c in cores], 1)
    v_full = np.ascontiguousarray(_gather_tokens([np.ascontiguousarray(rB.results[c]["l4_vtok_out"].T) for c in cores], 1).T)

    nc = build_groupC()
    inC = []
    for c in cores:
        d5 = host_inputs_l5(c, I, q_by_core, k_full, v_full, x2_by_core)
        d6 = host_inputs_l6b(c, I, None, wq_gu, wq_p)
        d6.pop("x_in", None)
        inC.append(dict(**_pref("l5", d5), **_pref("l6b", d6)))
    rC = run_bass_kernel_spmd(nc, inC, core_ids=cores)
    out = _gather_tokens([rC.results[c]["l6b_out_o"] for c in cores], 1)
    return np.ascontiguousarray(out.T).reshape(1, L, DM).astype(np.float32)
